# version 04
# speedup vs baseline: 1.1689x; 1.0410x over previous
import contextlib
import numpy as np
import ml_dtypes
import concourse.bass as bass
import concourse.mybir as mybir
from concourse.bass_utils import run_bass_kernel_spmd

F32 = mybir.dt.float32
BF16 = mybir.dt.bfloat16
I32 = mybir.dt.int32
AF = mybir.ActivationFunctionType
ALU = mybir.AluOpType
BF = ml_dtypes.bfloat16

NCORE = 8
D = 1024
DCH = D // 128
TT = 512
LN_EPS = 1e-5

ENGSEM = {"q": "ld", "g": "lg", "v": "sv", "s": "ss", "t": "sm"}
ALLSEMS = ["ld", "lg", "sv", "ss", "sm"]



def host_fm(x_td):
    return np.ascontiguousarray(np.asarray(x_td).T)


def host_col(vec, p=128):
    v = np.asarray(vec)
    n = v.shape[0]
    return np.ascontiguousarray(v.reshape(n // p, p).T)


def host_row(vec):
    return np.ascontiguousarray(np.asarray(vec).reshape(1, -1))


def make_consts():
    eye = np.eye(128, dtype=BF)
    ones_bf = np.ones((128, TT), dtype=BF)
    ones_f32 = np.ones((128, 128), dtype=np.float32)
    cvec = np.zeros((128, 8), dtype=np.float32)
    cvec[:, 0] = LN_EPS
    cvec[:, 1] = 1e-6
    cvec[:, 2] = 1.0
    cvec[:, 3] = 0.0
    cvec[:, 4] = -0.5
    cvec[:, 5] = 1.0 / 8.0
    cvec[:, 6] = 1.0 / 127.0
    hot32 = np.zeros((32, 32 * 128), dtype=np.float32)
    for k in range(32):
        hot32[k, k * 128:(k + 1) * 128] = 1.0
    hot16 = np.zeros((16, 16 * 128), dtype=BF)
    for k in range(16):
        hot16[k, k * 128:(k + 1) * 128] = 1.0
    return {"c_eye": eye, "c_ones_bf": ones_bf, "c_ones_f32": ones_f32, "c_cvec": cvec,
            "c_eyef": np.eye(128, dtype=np.float32), "c_hot32": hot32, "c_hot16": hot16}


CONST_SPECS = [("c_eye", [128, 128], BF16), ("c_ones_bf", [128, TT], BF16),
               ("c_ones_f32", [128, 128], F32), ("c_cvec", [128, 8], F32),
               ("c_eyef", [128, 128], F32), ("c_hot32", [32, 32 * 128], F32),
               ("c_hot16", [16, 16 * 128], BF16)]


def fm_ap(hbm_ap, t0, t1):
    return hbm_ap[:, t0:t1].rearrange("(c p) t -> p c t", p=128)


def sb3(sb_ap, t, c=DCH):
    return sb_ap.rearrange("p (c t) -> p c t", c=c)


def fm_ap_w(hbm_ap, t0, t1):
    return hbm_ap[:, t0:t1].rearrange("(c p) t -> p (c t)", p=128)



class Sched:

    def __init__(self, who, E, semh):
        self.who = who
        self.E = E
        self.semh = semh
        self.cnt = {k: 0 for k in semh}

    def op(self, e, fn, sem=None, k=None):
        s = sem or ENGSEM[e]
        if k is None:
            k = 16 if e in ("q", "g") else 1
        if self.who == e:
            if self.cnt[s] > 0:
                self.E.wait_ge(self.semh[s], self.cnt[s])
            instr = fn(self.E)
            instr.then_inc(self.semh[s], k)
        self.cnt[s] += k

    def silent(self, e, fn):
        if self.who == e:
            fn(self.E)

    def wait(self, e, sems=None):
        sems = sems or [s for s in ALLSEMS if s != ENGSEM[e]]
        if self.who == e:
            for s in sems:
                if self.cnt[s] > 0:
                    self.E.wait_ge(self.semh[s], self.cnt[s])

    def wait_at(self, e, sem, n):
        if self.who == e:
            self.E.wait_ge(self.semh[sem], n)


def run_sched(nc, ctx, emit_body):
    semh = {k: ctx.enter_context(nc.semaphore(k)) for k in ALLSEMS}
    block = ctx.enter_context(nc.Block())
    finals = {}

    def replay(who):
        def f(E):
            s = Sched(who, E, semh)
            emit_body(s)
            finals.update(s.cnt)
        return f

    block.sync(replay("q"))
    block.vector(replay("v"))
    block.scalar(replay("s"))
    block.tensor(replay("t"))
    block.gpsimd(replay("g"))
    return finals



BASE_CONSTS = ["c_eye", "c_ones_bf", "c_ones_f32", "c_cvec"]


def emit_load_consts(s, hb, sb):
    for name, _, _ in CONST_SPECS:
        if name in hb:
            s.op("q", lambda E, n=name: E.dma_start(out=sb[n][:], in_=hb[n]))


def consts_subset(consts, names):
    return {k: consts[k] for k in names}


def emit_cast_weight(s, hbm_chunks, stg, dst_slices):
    s.wait("q", ["sv"])
    sv0 = s.cnt["sv"]
    for k, (src, dst) in enumerate(zip(hbm_chunks, dst_slices)):
        ncols = src.shape[1]
        if k >= 2:
            s.wait_at("q", "sv", sv0 + k - 1)
        s.op("q", lambda E, sr=src, kk=k, nc_=ncols: E.dma_start(out=stg[kk % 2][:, :nc_], in_=sr))
        s.wait_at("v", "ld", s.cnt["ld"])
        s.op("v", lambda E, d=dst, kk=k, nc_=ncols: E.tensor_copy(out=d, in_=stg[kk % 2][:, :nc_]))


def emit_ln(s, ps, sb, x_sl, xhat_sl, sq_sl, t, name="ln", dch=DCH):
    for c in range(dch):
        s.op("s", lambda E, c=c: E.activation(out=sq_sl[:, c * t:(c + 1) * t],
                                              in_=x_sl[:, c * t:(c + 1) * t], func=AF.Square))
    s.wait("t", ["ss", "sv", "ld"])
    for c in range(dch):
        s.op("t", lambda E, c=c: E.matmul(out=ps["stat"][0:1, :t], lhsT=sb["c_ones_f32"][:, 0:1],
                                          rhs=x_sl[:, c * t:(c + 1) * t],
                                          start=(c == 0), stop=(c == dch - 1)))
    for c in range(dch):
        s.op("t", lambda E, c=c: E.matmul(out=ps["stat"][64:65, :t], lhsT=sb["c_ones_bf"][:, 0:1],
                                          rhs=sq_sl[:, c * t:(c + 1) * t],
                                          start=(c == 0), stop=(c == dch - 1)))
    s.wait("v", ["sm"])
    s.op("v", lambda E: E.tensor_scalar(out=sb["row0"][0:1, :t], in0=ps["stat"][0:1, :t],
                                        scalar1=1.0 / D, scalar2=None, op0=ALU.mult))
    s.op("v", lambda E: E.tensor_tensor(out=sb["row1"][0:1, :t], in0=sb["row0"][0:1, :t],
                                        in1=sb["row0"][0:1, :t], op=ALU.mult))
    s.op("v", lambda E: E.scalar_tensor_tensor(out=sb["row1"][0:1, :t], in0=ps["stat"][64:65, :t],
                                               scalar=1.0 / D, in1=sb["row1"][0:1, :t],
                                               op0=ALU.mult, op1=ALU.subtract))
    s.wait("s", ["sv"])
    s.op("s", lambda E: E.activation(out=sb["row1"][0:1, :t], in_=sb["row1"][0:1, :t],
                                     func=AF.Ln, bias=sb["c_cvec"][0:1, 0:1], scale=sb["c_cvec"][0:1, 2:3]))
    s.op("s", lambda E: E.activation(out=sb["row1"][0:1, :t], in_=sb["row1"][0:1, :t],
                                     func=AF.Exp, bias=sb["c_cvec"][0:1, 3:4], scale=sb["c_cvec"][0:1, 4:5]))
    s.wait("t", ["ss", "sv"])
    s.op("t", lambda E: E.matmul(out=ps["rep0"][:, :t], lhsT=sb["c_ones_f32"][0:1, :],
                                 rhs=sb["row0"][0:1, :t], start=True, stop=True))
    s.op("t", lambda E: E.matmul(out=ps["rep1"][:, :t], lhsT=sb["c_ones_f32"][0:1, :],
                                 rhs=sb["row1"][0:1, :t], start=True, stop=True))
    s.wait("v", ["sm"])
    for c in range(dch):
        s.op("v", lambda E, c=c: E.tensor_tensor(out=sq_sl[:, c * t:(c + 1) * t],
                                                 in0=x_sl[:, c * t:(c + 1) * t],
                                                 in1=ps["rep0"][:, :t], op=ALU.subtract))
    for c in range(dch):
        s.op("v", lambda E, c=c: E.tensor_tensor(out=xhat_sl[:, c * t:(c + 1) * t],
                                                 in0=sq_sl[:, c * t:(c + 1) * t],
                                                 in1=ps["rep1"][:, :t], op=ALU.mult))




def emit_mm_pipe(s, groups, pbanks, ev_engine="v"):
    ev_sem = ENGSEM[ev_engine]
    ev_marks = {}
    mm_marks = {}
    for g, (mms, evac) in enumerate(groups):
        pb = pbanks[g % 2]
        if g >= 2:
            s.wait_at("t", ev_sem, ev_marks[g - 2])
        for f in mms:
            s.op("t", lambda E, f=f, pb=pb: f(E, pb))
        mm_marks[g] = s.cnt["sm"]
        s.wait_at(ev_engine, "sm", mm_marks[g])
        s.op(ev_engine, lambda E, evac=evac, pb=pb: evac(E, pb))
        ev_marks[g] = s.cnt[ev_sem]




def emit_mm_pipe2(s, groups, pbanks):
    nb = len(pbanks)
    ev_marks = {}
    for g, item in enumerate(groups):
        mms, evac, ev_e = item[0], item[1], item[2]
        extra = item[3] if len(item) > 3 else []
        pb = pbanks[g % nb]
        if g >= nb:
            e2, m2 = ev_marks[g - nb]
            s.wait_at("t", ENGSEM[e2], m2)
        for f in mms:
            s.op("t", lambda E, f=f, pb=pb: f(E, pb))
        mm_mark = s.cnt["sm"]
        s.wait_at(ev_e, "sm", mm_mark)
        for sem in extra:
            s.wait(ev_e, [sem])
        s.op(ev_e, lambda E, evac=evac, pb=pb: evac(E, pb))
        ev_marks[g] = (ev_e, s.cnt[ENGSEM[ev_e]])



L0A_SCALE = 1.0 / np.sqrt(64.0)


def setup_l0a(nc, hb, prefix="", Tc=4096, debug=False):
    NT = Tc // TT
    ctx = contextlib.ExitStack()
    with ctx:
        sb = {}

        def S(name, shape, dt=F32):
            sb[name] = ctx.enter_context(nc.sbuf_tensor("sb_" + prefix + name, shape, dt))
            return sb[name]

        for name, shp, dt in CONST_SPECS:
            if name in hb:
                S(name, shp, dt)
        S("wstg0", [128, 1024], F32); S("wstg1", [128, 1024], F32)
        for w in ("wq", "wk", "wv", "wc"):
            S(w + "_bf", [128, 8 * 1024], BF16)
        S("bq_r", [1, D]); S("bk_r", [1, D])
        S("bv_r", [1, D]); S("bc_r", [1, D])
        S("bq_bf", [1, D], BF16); S("bk_bf", [1, D], BF16)
        S("bv_bf", [1, D], BF16); S("bc_bf", [1, D], BF16)
        S("x0", [128, DCH * TT]); S("x1", [128, DCH * TT])
        S("xhat", [128, DCH * TT], BF16)
        S("sq", [128, DCH * TT], BF16)
        sb["fused"] = sb["sq"]
        S("qT", [128, DCH * TT], BF16); S("kT", [128, DCH * TT], BF16)
        S("vtok", [128, 4 * 1024], BF16)
        S("probs", [128, 512], BF16)
        S("attnT", [128, 512], BF16)
        S("row0", [1, TT]); S("row1", [1, TT])
        S("nm", [128, 4]); S("rs", [128, 4])
        S("summ", [128, DCH * 32]); S("shat", [128, DCH * 32], BF16)
        S("ssq", [128, DCH * 32], BF16)
        S("qgT", [128, DCH * 32], BF16); S("kgT", [128, DCH * 32], BF16)
        S("vgtok", [32, 1024], BF16)
        S("pg", [32, 32], BF16); S("agT", [32, 32], BF16)
        S("gout", [128, DCH * 32])
        S("nmg", [32, 2]); S("rsg", [32, 2])

        ps = {}
        for name, shape, dt in [("stat", [128, TT], F32), ("rep0", [128, TT], F32),
                                ("rep1", [128, TT], F32), ("pq", [128, TT], F32),
                                ("pvt", [128, 1024], F32), ("psc", [128, TT], F32),
                                ("pT", [128, 256], BF16)]:
            ps[name] = ctx.enter_context(nc.psum_tensor(prefix + name, shape, dt))

        def body(s):
            emit_load_consts(s, hb, sb)
            for small in ("bq_r", "bk_r", "bv_r", "bc_r", "summ"):
                src = hb[small] if small != "summ" else fm_ap(hb["summ"], 0, 32)
                s.op("q", lambda E, d=sb[small], sr=src, sm=small: E.dma_start(
                    out=(sb3(d[:], 32) if sm == "summ" else d[:]), in_=sr))
            chunks, dsts = [], []
            for w in ("wq", "wk", "wv", "wc"):
                for ci in range(DCH):
                    chunks.append(hb[w][ci * 128:(ci + 1) * 128, :])
                    dsts.append(sb[w + "_bf"][:, ci * 1024:(ci + 1) * 1024])
            emit_cast_weight(s, chunks, [sb["wstg0"], sb["wstg1"]], dsts)
            s.wait("v", ["ld"])
            for bn in ("bq", "bk", "bv", "bc"):
                s.op("v", lambda E, bn=bn: E.tensor_copy(out=sb[bn + "_bf"][:], in_=sb[bn + "_r"][:]))

            cv = sb["c_cvec"]
            one_ap = cv[:, 2:3]

            s.wait("s", ["ld"])
            emit_ln(s, ps, sb, sb["summ"][:], sb["shat"][:], sb["ssq"][:], 32)
            for (wname, bname, dst) in (("wq_bf", "bq_bf", "qgT"), ("wk_bf", "bk_bf", "kgT")):
                for co in range(DCH):
                    s.wait("t", ["sv", "ss"])
                    for ci in range(DCH):
                        s.op("t", lambda E, ci=ci, co=co, w=wname: E.matmul(
                            out=ps["pq"][:, 0:32],
                            lhsT=sb[w][:, ci * 1024 + co * 128: ci * 1024 + (co + 1) * 128],
                            rhs=sb["shat"][:, ci * 32:(ci + 1) * 32],
                            start=(ci == 0), stop=False))
                    s.op("t", lambda E, co=co, b=bname: E.matmul(
                        out=ps["pq"][:, 0:32], lhsT=sb[b][0:1, co * 128:(co + 1) * 128],
                        rhs=sb["c_ones_bf"][0:1, 0:32], start=False, stop=True))
                    s.wait("s", ["sm"])
                    s.op("s", lambda E, co=co, d=dst: E.copy(
                        out=sb[d][:, co * 32:(co + 1) * 32], in_=ps["pq"][:, 0:32]))
            s.wait("t", ["ss", "sv"])
            for h in range(2):
                for ci in range(DCH):
                    s.op("t", lambda E, ci=ci, h=h: E.matmul(
                        out=ps["pvt"][0:32, h * 512:(h + 1) * 512],
                        lhsT=sb["shat"][:, ci * 32:(ci + 1) * 32],
                        rhs=sb["wv_bf"][:, ci * 1024 + h * 512: ci * 1024 + (h + 1) * 512],
                        start=(ci == 0), stop=False))
                s.op("t", lambda E, h=h: E.matmul(
                    out=ps["pvt"][0:32, h * 512:(h + 1) * 512], lhsT=sb["c_ones_bf"][0:1, 0:32],
                    rhs=sb["bv_bf"][0:1, h * 512:(h + 1) * 512], start=False, stop=True))
            s.wait("v", ["sm"])
            s.op("v", lambda E: E.tensor_copy(out=sb["vgtok"][:], in_=ps["pvt"][0:32, :]))
            s.wait("t", ["sv"])
            for ci in range(DCH):
                s.op("t", lambda E, ci=ci: E.matmul(
                    out=ps["psc"][0:32, 0:32], lhsT=sb["qgT"][:, ci * 32:(ci + 1) * 32],
                    rhs=sb["kgT"][:, ci * 32:(ci + 1) * 32],
                    start=(ci == 0), stop=(ci == DCH - 1)))
            s.wait("v", ["sm"])
            s.op("v", lambda E: E.reduce_max(out=sb["nmg"][0:32, 0:1], in_=ps["psc"][0:32, 0:32],
                                             axis=mybir.AxisListType.X, negate=True))
            s.op("v", lambda E: E.tensor_scalar(out=sb["nmg"][0:32, 1:2], in0=sb["nmg"][0:32, 0:1],
                                                scalar1=L0A_SCALE, scalar2=None, op0=ALU.mult))
            s.wait("s", ["sv"])
            s.op("s", lambda E: E.activation(out=sb["pg"][:], in_=ps["psc"][0:32, 0:32],
                                             func=AF.Exp, bias=sb["nmg"][0:32, 1:2],
                                             scale=cv[0:32, 5:6], accum_out=sb["rsg"][0:32, 0:1]))
            s.wait("v", ["ss"])
            s.op("v", lambda E: E.reciprocal(out=sb["rsg"][0:32, 1:2], in_=sb["rsg"][0:32, 0:1]))
            s.op("v", lambda E: E.tensor_scalar(out=sb["pg"][:], in0=sb["pg"][:],
                                                scalar1=sb["rsg"][0:32, 1:2], scalar2=None,
                                                op0=ALU.mult))
            s.wait("t", ["sv"])
            s.op("t", lambda E: E.transpose(out=ps["pT"][0:32, 0:32], in_=sb["pg"][:],
                                            identity=sb["c_eye"][0:32, 0:32]))
            s.wait("v", ["sm"])
            s.op("v", lambda E: E.tensor_copy(out=sb["agT"][:], in_=ps["pT"][0:32, 0:32]))
            s.wait("t", ["sv"])
            for co in range(DCH):
                s.op("t", lambda E, co=co: E.matmul(
                    out=ps["rep0"][:, 0:32], lhsT=sb["vgtok"][:, co * 128:(co + 1) * 128],
                    rhs=sb["agT"][:], start=True, stop=True))
                s.wait("v", ["sm"])
                s.op("v", lambda E, co=co: E.tensor_copy(out=sb["gout"][:, co * 32:(co + 1) * 32],
                                                         in_=ps["rep0"][:, 0:32]))
                s.wait("t", ["sv"])

            if debug:
                s.wait("g", ["sv", "ss", "sm"])
                s.op("g", lambda E: E.dma_start(out=hb["dbg_shat"], in_=sb["shat"][:]))
                s.op("g", lambda E: E.dma_start(out=hb["dbg_rows"][0:1, 0:32], in_=sb["row0"][0:1, 0:32]))
                s.op("g", lambda E: E.dma_start(out=hb["dbg_rows"][1:2, 0:32], in_=sb["row1"][0:1, 0:32]))
                s.op("g", lambda E: E.dma_start(out=hb["dbg_gout"], in_=sb["gout"][:]))
                for e_ in ("v", "s", "t"):
                    s.wait(e_, ["lg"])
            marks = []
            marks_lg = []
            for i in range(NT):
                t0 = i * TT
                xsb = sb["x0"] if i % 2 == 0 else sb["x1"]
                osb = xsb
                if i >= 2:
                    s.wait_at("q", "sv", marks[i - 2])
                    s.wait_at("q", "lg", marks_lg[i - 2])
                s.op("q", lambda E, t0=t0, xsb=xsb: E.dma_start(out=sb3(xsb[:], TT), in_=fm_ap(hb["x"], t0, t0 + TT)))
                s.wait("s", ["ld", "sm", "sv"])
                s.wait("v", ["ld"])
                emit_ln(s, ps, sb, xsb[:], sb["xhat"][:], sb["sq"][:], TT)
                if debug and i == 0:
                    s.wait("g", ["sv", "ss", "sm"])
                    s.op("g", lambda E: E.dma_start(out=hb["dbg_xhat"], in_=sb["xhat"][:]))
                    for e_ in ("v", "s", "t"):
                        s.wait(e_, ["lg"])
                s.wait("t", ["sv", "ss"])
                groups = []
                for (wname, bname, dst) in (("wq_bf", "bq_bf", "qT"), ("wk_bf", "bk_bf", "kT")):
                    for co in range(DCH):
                        def mkq(co, w, b, d):
                            def mm(E, pb, ci):
                                return E.matmul(
                                    out=pb[:],
                                    lhsT=sb[w][:, ci * 1024 + co * 128: ci * 1024 + (co + 1) * 128],
                                    rhs=sb["xhat"][:, ci * TT:(ci + 1) * TT],
                                    start=(ci == 0), stop=False)
                            mms = [lambda E, pb, ci=ci: mm(E, pb, ci) for ci in range(DCH)]
                            mms.append(lambda E, pb: E.matmul(
                                out=pb[:], lhsT=sb[b][0:1, co * 128:(co + 1) * 128],
                                rhs=sb["c_ones_bf"][0:1, :], start=False, stop=True))
                            ev = lambda E, pb: E.copy(out=sb[d][:, co * TT:(co + 1) * TT], in_=pb[:])
                            return (mms, ev, "s")
                        groups.append(mkq(co, wname, bname, dst))
                emit_mm_pipe2(s, groups, [ps["pq"], ps["psc"]])
                if debug and i == 0:
                    s.wait("g", ["sv", "ss", "sm"])
                    s.op("g", lambda E: E.dma_start(out=hb["dbg_qT"], in_=sb["qT"][:]))
                    for e_ in ("v", "s", "t"):
                        s.wait(e_, ["lg"])
                for j in range(4):
                    s.wait("t", ["sv", "ss"])
                    for h in range(2):
                        for ci in range(DCH):
                            s.op("t", lambda E, ci=ci, j=j, h=h: E.matmul(
                                out=ps["pvt"][:, h * 512:(h + 1) * 512],
                                lhsT=sb["xhat"][:, ci * TT + j * 128: ci * TT + (j + 1) * 128],
                                rhs=sb["wv_bf"][:, ci * 1024 + h * 512: ci * 1024 + (h + 1) * 512],
                                start=(ci == 0), stop=False))
                        s.op("t", lambda E, h=h: E.matmul(
                            out=ps["pvt"][:, h * 512:(h + 1) * 512], lhsT=sb["c_ones_bf"][0:1, 0:128],
                            rhs=sb["bv_bf"][0:1, h * 512:(h + 1) * 512], start=False, stop=True))
                    s.wait("v", ["sm"])
                    s.op("v", lambda E, j=j: E.tensor_copy(out=sb["vtok"][:, j * 1024:(j + 1) * 1024],
                                                           in_=ps["pvt"][:]))
                for seg in range(2):
                    sg_idx = i * 2 + seg
                    s.wait("t", ["sv"])
                    for t1b in range(2):
                        for ci in range(DCH):
                            s.op("t", lambda E, ci=ci, t1b=t1b, seg=seg: E.matmul(
                                out=ps["psc"][:, t1b * 256:(t1b + 1) * 256],
                                lhsT=sb["qT"][:, ci * TT + seg * 256 + t1b * 128: ci * TT + seg * 256 + (t1b + 1) * 128],
                                rhs=sb["kT"][:, ci * TT + seg * 256: ci * TT + seg * 256 + 256],
                                start=(ci == 0), stop=(ci == DCH - 1)))
                    s.wait("v", ["sm"])
                    for t1b in range(2):
                        s.op("v", lambda E, t1b=t1b: E.reduce_max(
                            out=sb["nm"][:, t1b:t1b + 1], in_=ps["psc"][:, t1b * 256:(t1b + 1) * 256],
                            axis=mybir.AxisListType.X, negate=True))
                        s.op("v", lambda E, t1b=t1b: E.tensor_scalar(
                            out=sb["nm"][:, 2 + t1b:3 + t1b], in0=sb["nm"][:, t1b:t1b + 1],
                            scalar1=L0A_SCALE, scalar2=None, op0=ALU.mult))
                    s.wait("s", ["sv"])
                    for t1b in range(2):
                        s.op("s", lambda E, t1b=t1b: E.activation(
                            out=sb["probs"][:, t1b * 256:(t1b + 1) * 256],
                            in_=ps["psc"][:, t1b * 256:(t1b + 1) * 256],
                            func=AF.Exp, bias=sb["nm"][:, 2 + t1b:3 + t1b], scale=cv[:, 5:6],
                            accum_out=sb["rs"][:, t1b:t1b + 1]))
                    s.wait("v", ["ss"])
                    for t1b in range(2):
                        s.op("v", lambda E, t1b=t1b: E.reciprocal(
                            out=sb["rs"][:, 2 + t1b:3 + t1b], in_=sb["rs"][:, t1b:t1b + 1]))
                        s.op("v", lambda E, t1b=t1b: E.tensor_scalar(
                            out=sb["probs"][:, t1b * 256:(t1b + 1) * 256],
                            in0=sb["probs"][:, t1b * 256:(t1b + 1) * 256],
                            scalar1=sb["rs"][:, 2 + t1b:3 + t1b], scalar2=None, op0=ALU.mult))
                    for t1b in range(2):
                        for t2b in range(2):
                            s.wait("t", ["sv"])
                            s.op("t", lambda E, t1b=t1b, t2b=t2b: E.transpose(
                                out=ps["pT"][:, t1b * 128:(t1b + 1) * 128],
                                in_=sb["probs"][:, t1b * 256 + t2b * 128: t1b * 256 + (t2b + 1) * 128],
                                identity=sb["c_eye"][:]))
                            s.wait("v", ["sm"])
                            s.op("v", lambda E, t1b=t1b, t2b=t2b: E.tensor_copy(
                                out=sb["attnT"][:, t2b * 256 + t1b * 128: t2b * 256 + (t1b + 1) * 128],
                                in_=ps["pT"][:, t1b * 128:(t1b + 1) * 128]))
                    for co in range(DCH):
                        s.wait("t", ["sv"])
                        for t2b in range(2):
                            s.op("t", lambda E, co=co, t2b=t2b, seg=seg: E.matmul(
                                out=ps["rep1"][:, 0:256],
                                lhsT=sb["vtok"][:, (seg * 2 + t2b) * 1024 + co * 128: (seg * 2 + t2b) * 1024 + (co + 1) * 128],
                                rhs=sb["attnT"][:, t2b * 256:(t2b + 1) * 256],
                                start=(t2b == 0), stop=(t2b == 1)))
                        s.wait("v", ["sm"])
                        s.op("v", lambda E, co=co, seg=seg, sg_idx=sg_idx: E.tensor_tensor(
                            out=sb["fused"][:, co * TT + seg * 256: co * TT + seg * 256 + 256],
                            in0=ps["rep1"][:, 0:256],
                            in1=sb["gout"][:, co * 32 + sg_idx: co * 32 + sg_idx + 1].broadcast_to((128, 256)),
                            op=ALU.add))
                if debug and i == 0:
                    s.wait("g", ["sv", "ss", "sm"])
                    s.op("g", lambda E: E.dma_start(out=hb["dbg_fused"], in_=sb["fused"][:]))
                    s.op("g", lambda E: E.dma_start(out=hb["dbg_vtok"], in_=sb["vtok"][:]))
                    s.op("g", lambda E: E.dma_start(out=hb["dbg_probs"], in_=sb["probs"][:]))
                    for e_ in ("v", "s", "t"):
                        s.wait(e_, ["lg"])
                if i >= 2:
                    s.wait_at("v", "lg", marks_lg[i - 2])
                s.wait("t", ["sv", "ss"])
                groups = []
                for co in range(DCH):
                    def mkoo(co):
                        def mm(E, pb, ci):
                            return E.matmul(
                                out=pb[:],
                                lhsT=sb["wc_bf"][:, ci * 1024 + co * 128: ci * 1024 + (co + 1) * 128],
                                rhs=sb["fused"][:, ci * TT:(ci + 1) * TT],
                                start=(ci == 0), stop=False)
                        mms = [lambda E, pb, ci=ci: mm(E, pb, ci) for ci in range(DCH)]
                        mms.append(lambda E, pb: E.matmul(
                            out=pb[:], lhsT=sb["bc_bf"][0:1, co * 128:(co + 1) * 128],
                            rhs=sb["c_ones_bf"][0:1, :], start=False, stop=True))
                        ev = lambda E, pb: E.tensor_tensor(
                            out=osb[:, co * TT:(co + 1) * TT], in0=pb[:],
                            in1=xsb[:, co * TT:(co + 1) * TT], op=ALU.add)
                        return (mms, ev, "v")
                    groups.append(mkoo(co))
                emit_mm_pipe2(s, groups, [ps["pq"], ps["psc"]])
                s.wait("g", ["sv"])
                s.op("g", lambda E, t0=t0, osb=osb: E.dma_start(out=fm_ap(hb["xo"], t0, t0 + TT), in_=sb3(osb[:], TT)))
                marks.append(s.cnt["sv"])
                marks_lg.append(s.cnt["lg"])
            s.wait_at("g", "lg", s.cnt["lg"])

    return body



def build_l0a(Tc=4096, debug=False):
    use_consts = BASE_CONSTS
    nc = bass.Bass("TRN2", target_bir_lowering=False)
    NT = Tc // TT
    hb = {}
    hb["x"] = nc.dram_tensor("x", [D, Tc], F32, kind="ExternalInput").ap()
    hb["summ"] = nc.dram_tensor("summ", [D, 32], F32, kind="ExternalInput").ap()
    for w in ("wq", "wk", "wv", "wc"):
        hb[w] = nc.dram_tensor(w, [D, D], F32, kind="ExternalInput").ap()
    hb["bq_r"] = nc.dram_tensor("bq_r", [1, D], F32, kind="ExternalInput").ap()
    hb["bk_r"] = nc.dram_tensor("bk_r", [1, D], F32, kind="ExternalInput").ap()
    hb["bv_r"] = nc.dram_tensor("bv_r", [1, D], F32, kind="ExternalInput").ap()
    hb["bc_r"] = nc.dram_tensor("bc_r", [1, D], F32, kind="ExternalInput").ap()
    for name, shp, dt in CONST_SPECS:
        if name in use_consts:
            hb[name] = nc.dram_tensor(name, shp, dt, kind="ExternalInput").ap()
    hb["xo"] = nc.dram_tensor("xo", [D, Tc], F32, kind="ExternalOutput").ap()
    if debug:
        hb["dbg_shat"] = nc.dram_tensor("dbg_shat", [128, DCH * 32], F32, kind="ExternalOutput").ap()
        hb["dbg_rows"] = nc.dram_tensor("dbg_rows", [2, 512], F32, kind="ExternalOutput").ap()
        hb["dbg_xhat"] = nc.dram_tensor("dbg_xhat", [128, DCH * TT], F32, kind="ExternalOutput").ap()
        hb["dbg_qT"] = nc.dram_tensor("dbg_qT", [128, DCH * TT], F32, kind="ExternalOutput").ap()
        hb["dbg_gout"] = nc.dram_tensor("dbg_gout", [128, DCH * 32], F32, kind="ExternalOutput").ap()
        hb["dbg_fused"] = nc.dram_tensor("dbg_fused", [128, DCH * TT], F32, kind="ExternalOutput").ap()
        hb["dbg_vtok"] = nc.dram_tensor("dbg_vtok", [128, 4 * 1024], F32, kind="ExternalOutput").ap()
        hb["dbg_probs"] = nc.dram_tensor("dbg_probs", [128, 512], F32, kind="ExternalOutput").ap()

    body = setup_l0a(nc, hb, "", Tc, debug)
    with contextlib.ExitStack() as c2:
        run_sched(nc, c2, body)
    return nc

def l0a_inmaps(ins, consts):
    consts = consts_subset(consts, BASE_CONSTS)
    x = np.asarray(ins["L0_attn_x"])
    maps = []
    for c in range(NCORE):
        b, h = c // 2, c % 2
        m = dict(consts)
        m["x"] = host_fm(x[b, h * 4096:(h + 1) * 4096])
        summ = x[b, ::256, :]
        m["summ"] = host_fm(np.roll(summ, -16 * h, axis=0))
        m["wq"] = np.ascontiguousarray(np.asarray(ins["L0_attn_wq"]).T)
        m["wk"] = np.ascontiguousarray(np.asarray(ins["L0_attn_wk"]).T)
        m["wv"] = np.ascontiguousarray(np.asarray(ins["L0_attn_wv"]).T)
        m["wc"] = np.ascontiguousarray(np.asarray(ins["L0_attn_wc"]).T)
        m["bq_r"] = host_row(ins["L0_attn_bq"])
        m["bk_r"] = host_row(ins["L0_attn_bk"])
        m["bv_r"] = host_row(ins["L0_attn_bv"])
        m["bc_r"] = host_row(ins["L0_attn_bc"])
        maps.append(m)
    return maps


def gather_fm(results, key="xo"):
    out = np.empty((4, 8192, D), dtype=np.float32)
    for c in range(NCORE):
        b, h = c // 2, c % 2
        out[b, h * 4096:(h + 1) * 4096] = np.asarray(results[c][key]).T
    return out



HCH = 32


def _stage_matrix(w_s, s):
    st = 2 ** s
    S = np.zeros((4096, 4096), dtype=np.float32)
    k = np.arange(2048)
    n = k // st
    j = k % st
    ia = n * 2 * st + j
    ib = ia + st
    S[ia, ia] = w_s[:, 0, 0]
    S[ia, ib] = w_s[:, 0, 1]
    S[ib, ia] = w_s[:, 1, 0]
    S[ib, ib] = w_s[:, 1, 1]
    return S


def _rmaj_perm():
    ht = np.arange(4096)
    return (ht % 64) * 64 + (ht // 64)


def _tiles_of(S):
    out = np.zeros((32, 128, 128), dtype=np.float32)
    for T in range(32):
        out[T] = S[T * 128:(T + 1) * 128, T * 128:(T + 1) * 128]
    return out


def l0f_stage_tiles(up_w, dn_w):
    perm = _rmaj_perm()
    stmU = np.zeros((12, 32, 128, 128), dtype=np.float32)
    stmD = np.zeros((12, 32, 128, 128), dtype=np.float32)
    for s in range(12):
        Su = _stage_matrix(np.asarray(up_w[s]), s)
        Sd = _stage_matrix(np.asarray(dn_w[s]), s)
        if s >= 6:
            Su = Su[np.ix_(perm, perm)]
            Sd = Sd[np.ix_(perm, perm)]
        stmU[s] = _tiles_of(Su.T)
        stmD[s] = _tiles_of(Sd)
    return stmU.reshape(12 * 32 * 128, 128), stmD.reshape(12 * 32 * 128, 128)


def setup_l0f(nc, hb, prefix="", Tc=4096):
    TL = 256
    NT = Tc // TL
    ctx = contextlib.ExitStack()
    with ctx:
        sb = {}

        def S(name, shape, dt=F32):
            sb[name] = ctx.enter_context(nc.sbuf_tensor("sb_" + prefix + name, shape, dt))
            return sb[name]

        for name, shp, dt in CONST_SPECS:
            if name in hb:
                S(name, shp, dt)
        S("wbuf", [128, HCH * 1025], BF16)
        S("wexp", [128, DCH * 4096], BF16)
        S("row0", [1, TL]); S("row1", [1, TL])
        S("bup_bf", [1, 4096], BF16)
        S("ubias_bf", [1, 4096], BF16)
        S("bcon_bf", [1, D], BF16); S("dbias_bf", [1, D], BF16)
        tmp = contextlib.ExitStack()
        for nm, shp, dt in [("wstg0", [128, 1025], F32), ("wstg1", [128, 1025], F32),
                            ("stg0", [128, 32 * 128], BF16), ("stg1", [128, 32 * 128], BF16)]:
            sb[nm] = tmp.enter_context(nc.sbuf_tensor("sb_" + nm, shp, dt))
        tmp.close()
        S("x0", [128, DCH * TL])
        S("xhat", [128, DCH * TL], BF16)
        S("sq", [128, DCH * TT], BF16)
        S("g", [128, HCH * TL], BF16)

        ps = {}
        for name, shape, dt in [("stat", [128, TT], F32), ("rep0", [128, TT], F32),
                                ("rep1", [128, TT], F32), ("pq", [128, TT], F32),
                                ("pw", [128, 1024], F32), ("pw1", [128, 2], F32),
                                ("pT", [128, 128], BF16)]:
            ps[name] = ctx.enter_context(nc.psum_tensor(prefix + name, shape, dt))

        def body(s):
            emit_load_consts(s, hb, sb)
            row_jobs = [(hb["ubias_p"][k:k + 1, :], sb["ubias_bf"][0:1, k * 1024:(k + 1) * 1024])
                        for k in range(4)]
            row_jobs += [(hb["bcon_r"], sb["bcon_bf"][0:1, :]), (hb["dbias_r"], sb["dbias_bf"][0:1, :])]
            for src, dst in row_jobs:
                s.wait("q", ["sv"])
                s.op("q", lambda E, sr=src: E.dma_start(out=sb["wstg0"][0:1, 0:1024], in_=sr))
                s.wait("v", ["ld"])
                s.op("v", lambda E, d=dst: E.tensor_copy(out=d, in_=sb["wstg0"][0:1, 0:1024]))

            def cast_into_wbuf(hbm, ncols):
                chunks = [hbm[T * 128:(T + 1) * 128, :] for T in range(HCH)]
                dsts = [sb["wbuf"][:, T * 1025: T * 1025 + ncols] for T in range(HCH)]
                emit_cast_weight(s, chunks, [sb["wstg0"], sb["wstg1"]], dsts)

            def load_stage_tiles(sidx, down, stgbuf):
                src = hb["stmD"] if down else hb["stmU"]
                for half in range(8):
                    chunk = src[(sidx * 32 + half * 4) * 128: (sidx * 32 + half * 4 + 4) * 128, :]
                    s.wait("q", ["sv"])
                    s.op("q", lambda E, c=chunk, h=half: E.dma_start(
                        out=sb["wstg0" if h % 2 == 0 else "wstg1"][:, 0:512].rearrange("p (T c) -> p T c", T=4),
                        in_=c.rearrange("(T p) c -> p T c", p=128)))
                    s.wait_at("v", "ld", s.cnt["ld"])
                    s.op("v", lambda E, h=half, d=stgbuf: E.tensor_copy(
                        out=d[:, h * 512:(h + 1) * 512],
                        in_=sb["wstg0" if h % 2 == 0 else "wstg1"][:, 0:512]))

            def stage_apply(sidx, down, ncols):
                stgbuf = sb["stg0"] if sidx % 2 == 0 else sb["stg1"]
                load_stage_tiles(sidx, down, stgbuf)
                for T in range(HCH):
                    s.wait("t", ["sv", "ss"])
                    nmm = (ncols + 511) // 512
                    for m in range(nmm):
                        c0, c1 = m * 512, min(ncols, (m + 1) * 512)
                        s.op("t", lambda E, T=T, c0=c0, c1=c1, d=stgbuf: E.matmul(
                            out=ps["pw"][:, c0:c1] if c1 - c0 > 2 else ps["pw1"][:, 0:c1 - c0],
                            lhsT=d[:, T * 128:(T + 1) * 128],
                            rhs=sb["wbuf"][:, T * 1025 + c0: T * 1025 + c1],
                            start=True, stop=True))
                    s.wait("v", ["sm"])
                    for m in range(nmm):
                        c0, c1 = m * 512, min(ncols, (m + 1) * 512)
                        s.op("v", lambda E, T=T, c0=c0, c1=c1: E.tensor_copy(
                            out=sb["wbuf"][:, T * 1025 + c0: T * 1025 + c1],
                            in_=ps["pw"][:, c0:c1] if c1 - c0 > 2 else ps["pw1"][:, 0:c1 - c0]))

            def bounce(ncols, to_rmaj):
                s.wait("q", ["sv", "sm", "ss"])
                for T in range(HCH):
                    s.op("q", lambda E, T=T: E.dma_start(
                        out=hb["scr"][T * 128:(T + 1) * 128, 0:ncols],
                        in_=sb["wbuf"][:, T * 1025: T * 1025 + ncols]))
                src = hb["scr"].rearrange("(X y) c -> y X c", X=64)
                s.wait_at("q", "ld", s.cnt["ld"])
                for T in range(HCH):
                    for rr in range(2):
                        s.op("q", lambda E, T=T, rr=rr: E.dma_start(
                            out=sb["wbuf"][rr * 64:(rr + 1) * 64, T * 1025: T * 1025 + ncols],
                            in_=src[2 * T + rr, :, 0:ncols]))
                s.wait("t", ["ld"]); s.wait("v", ["ld"]); s.wait("s", ["ld"])

            cast_into_wbuf(hb["w1i"], 1025)
            for sidx in range(6):
                stage_apply(sidx, False, 1025)
            bounce(1025, True)
            for sidx in range(6, 12):
                stage_apply(sidx, False, 1025)
            for T in range(HCH):
                for co in range(DCH):
                    s.wait("t", ["sv", "ss"])
                    s.op("t", lambda E, T=T, co=co: E.transpose(
                        out=ps["pT"][:], in_=sb["wbuf"][:, T * 1025 + co * 128: T * 1025 + (co + 1) * 128],
                        identity=sb["c_eye"][:]))
                    s.wait("v", ["sm"])
                    s.op("v", lambda E, T=T, co=co: E.tensor_copy(
                        out=sb["wexp"][:, co * 4096 + T * 128: co * 4096 + (T + 1) * 128],
                        in_=ps["pT"][:]))
                s.wait("t", ["sv", "ss"])
                s.op("t", lambda E, T=T: E.transpose(
                    out=ps["pT"][0:1, :], in_=sb["wbuf"][:, T * 1025 + 1024: T * 1025 + 1025],
                    identity=sb["c_eye"][:]))
                s.wait("v", ["sm"])
                s.op("v", lambda E, T=T: E.tensor_copy(out=sb["bup_bf"][0:1, T * 128:(T + 1) * 128],
                                                       in_=ps["pT"][0:1, :]))
            s.wait("v", ["ld"])
            s.op("v", lambda E: E.tensor_add(out=sb["bup_bf"][:], in0=sb["bup_bf"][:], in1=sb["ubias_bf"][:]))

            s.wait("v", ["sm", "ss"])
            cast_into_wbuf(hb["w2i"], 1024)
            bounce(1024, True)
            for sidx in range(11, 5, -1):
                stage_apply(sidx, True, 1024)
            bounce(1024, False)
            for sidx in range(5, -1, -1):
                stage_apply(sidx, True, 1024)
            bounce(1024, True)

            marks = []
            marks_lg = []
            for i in range(NT):
                t0 = i * TL
                xsb = sb["x0"]
                if i >= 1:
                    s.wait_at("q", "sv", marks[i - 1])
                    s.wait_at("q", "lg", marks_lg[i - 1])
                s.op("q", lambda E, t0=t0, xsb=xsb: E.dma_start(out=sb3(xsb[:], TL), in_=fm_ap(hb["x"], t0, t0 + TL)))
                s.wait("s", ["ld", "sm", "sv"])
                s.wait("v", ["ld"])
                emit_ln(s, ps, sb, xsb[:], sb["xhat"][:], sb["sq"][:], TL)
                s.wait("t", ["sv", "ss"])
                groups = []
                for co in range(HCH):
                    def mku(co):
                        def mm(E, pb, ci):
                            return E.matmul(
                                out=pb[:, 0:TL],
                                lhsT=sb["wexp"][:, ci * 4096 + co * 128: ci * 4096 + (co + 1) * 128],
                                rhs=sb["xhat"][:, ci * TL:(ci + 1) * TL],
                                start=(ci == 0), stop=False)
                        mms = [lambda E, pb, ci=ci: mm(E, pb, ci) for ci in range(DCH)]
                        mms.append(lambda E, pb: E.matmul(
                            out=pb[:, 0:TL], lhsT=sb["bup_bf"][0:1, co * 128:(co + 1) * 128],
                            rhs=sb["c_ones_bf"][0:1, 0:TL], start=False, stop=True))
                        ev = lambda E, pb: E.activation(
                            out=sb["g"][:, co * TL:(co + 1) * TL], in_=pb[:, 0:TL], func=AF.Gelu)
                        return (mms, ev, "s")
                    groups.append(mku(co))
                emit_mm_pipe2(s, groups, [ps["pq"], ps["rep0"], ps["rep1"]])
                s.wait("t", ["ss", "sv", "ld"])
                groups = []
                for co in range(DCH):
                    def mkd(co):
                        def mm(E, pb, ci):
                            return E.matmul(
                                out=pb[:, 0:TL],
                                lhsT=sb["wbuf"][:, ci * 1025 + co * 128: ci * 1025 + (co + 1) * 128],
                                rhs=sb["g"][:, ci * TL:(ci + 1) * TL],
                                start=(ci == 0), stop=False)
                        mms = [lambda E, pb, ci=ci: mm(E, pb, ci) for ci in range(HCH)]
                        mms.append(lambda E, pb: E.matmul(
                            out=pb[:, 0:TL], lhsT=sb["bcon_bf"][0:1, co * 128:(co + 1) * 128],
                            rhs=sb["c_ones_bf"][0:1, 0:TL], start=False, stop=False))
                        mms.append(lambda E, pb: E.matmul(
                            out=pb[:, 0:TL], lhsT=sb["dbias_bf"][0:1, co * 128:(co + 1) * 128],
                            rhs=sb["c_ones_bf"][0:1, 0:TL], start=False, stop=True))
                        ev = lambda E, pb: E.tensor_tensor(
                            out=xsb[:, co * TL:(co + 1) * TL], in0=pb[:, 0:TL],
                            in1=xsb[:, co * TL:(co + 1) * TL], op=ALU.add)
                        return (mms, ev, "v")
                    groups.append(mkd(co))
                emit_mm_pipe2(s, groups, [ps["pq"], ps["rep0"]])
                s.wait("g", ["sv"])
                s.op("g", lambda E, t0=t0, xsb=xsb: E.dma_start(out=fm_ap(hb["xo"], t0, t0 + TL), in_=sb3(xsb[:], TL)))
                marks.append(s.cnt["sv"])
                marks_lg.append(s.cnt["lg"])
            s.wait_at("g", "lg", s.cnt["lg"])

    return body



def build_l0f(Tc=4096):
    use_consts = BASE_CONSTS
    nc = bass.Bass("TRN2", target_bir_lowering=False)
    TL = 256
    NT = Tc // TL
    hb = {}
    hb["x"] = nc.dram_tensor("x", [D, Tc], F32, kind="ExternalInput").ap()
    hb["w1i"] = nc.dram_tensor("w1i", [4096, 1025], F32, kind="ExternalInput").ap()
    hb["w2i"] = nc.dram_tensor("w2i", [4096, 1024], F32, kind="ExternalInput").ap()
    hb["stmU"] = nc.dram_tensor("stmU", [12 * 32 * 128, 128], F32, kind="ExternalInput").ap()
    hb["stmD"] = nc.dram_tensor("stmD", [12 * 32 * 128, 128], F32, kind="ExternalInput").ap()
    hb["ubias_p"] = nc.dram_tensor("ubias_p", [4, 1024], F32, kind="ExternalInput").ap()
    hb["bcon_r"] = nc.dram_tensor("bcon_r", [1, D], F32, kind="ExternalInput").ap()
    hb["dbias_r"] = nc.dram_tensor("dbias_r", [1, D], F32, kind="ExternalInput").ap()
    for name, shp, dt in CONST_SPECS:
        if name in use_consts:
            hb[name] = nc.dram_tensor(name, shp, dt, kind="ExternalInput").ap()
    hb["xo"] = nc.dram_tensor("xo", [D, Tc], F32, kind="ExternalOutput").ap()
    hb["scr"] = nc.dram_tensor("scr", [4096, 1025], BF16, kind="Internal").ap()

    body = setup_l0f(nc, hb, "", Tc)
    with contextlib.ExitStack() as c2:
        run_sched(nc, c2, body)
    return nc

def l0f_inmaps(ins, consts):
    consts = consts_subset(consts, BASE_CONSTS)
    stmU, stmD = l0f_stage_tiles(ins["L0_ffn_up_weights"], ins["L0_ffn_down_weights"])
    perm = _rmaj_perm()
    w1i = np.concatenate([np.asarray(ins["L0_ffn_W_exp"]).T,
                          np.asarray(ins["L0_ffn_b_exp"]).reshape(-1, 1)], axis=1)
    w1i = np.ascontiguousarray(w1i.astype(np.float32))
    w2i = np.ascontiguousarray(np.asarray(ins["L0_ffn_W_con"]).astype(np.float32))
    ub_p = np.ascontiguousarray(np.asarray(ins["L0_ffn_up_bias"])[perm].reshape(1, -1))
    maps = []
    for c in range(NCORE):
        m = dict(consts)
        m["w1i"] = w1i
        m["w2i"] = w2i
        m["stmU"] = stmU
        m["stmD"] = stmD
        m["ubias_p"] = ub_p.reshape(4, 1024)
        m["bcon_r"] = host_row(ins["L0_ffn_b_con"])
        m["dbias_r"] = host_row(ins["L0_ffn_down_bias"])
        maps.append(m)
    return maps



def setup_l1f(nc, hb, prefix="", Tc=4096):
    HC = 24
    NT = Tc // TT
    ctx = contextlib.ExitStack()
    with ctx:
        sb = {}

        def S(name, shape, dt=F32):
            sb[name] = ctx.enter_context(nc.sbuf_tensor("sb_" + prefix + name, shape, dt))
            return sb[name]

        for name, shp, dt in CONST_SPECS:
            if name in hb:
                S(name, shp, dt)
        S("wg", [128, DCH * 3072], BF16)
        S("wu", [128, DCH * 3072], BF16)
        S("wd", [128, HC * 1024], BF16)
        S("wstg0", [128, 1024], I32); S("wstg1", [128, 1024], I32)
        S("x0", [128, DCH * TT])
        S("xhat", [128, DCH * TT], BF16)
        S("hbuf", [128, HC * TT], BF16)
        sb["sq"] = sb["hbuf"]
        S("sg0", [128, TT], BF16)
        S("sg1", [128, TT], BF16)
        S("row0", [1, TT]); S("row1", [1, TT])

        ps = {}
        for name, shape, dt in [("stat", [128, TT], F32), ("rep0", [128, TT], F32),
                                ("rep1", [128, TT], F32), ("pq", [128, TT], F32),
                                ("pu", [128, TT], F32)]:
            ps[name] = ctx.enter_context(nc.psum_tensor(prefix + name, shape, dt))

        def body(s):
            emit_load_consts(s, hb, sb)
            chunks, dsts = [], []
            for w, hbk, nch, ncol in (("wg", "wgT", DCH, 3072), ("wu", "wuT", DCH, 3072), ("wd", "wdT", HC, 1024)):
                for ci in range(nch):
                    for half in range(ncol // 1024):
                        chunks.append(hb[hbk][ci * 128:(ci + 1) * 128, half * 1024:(half + 1) * 1024])
                        dsts.append(sb[w][:, ci * ncol + half * 1024: ci * ncol + (half + 1) * 1024])
            emit_cast_weight(s, chunks, [sb["wstg0"], sb["wstg1"]], dsts)

            cv = sb["c_cvec"]
            marks = []
            marks_lg = []
            for i in range(NT):
                t0 = i * TT
                xsb = sb["x0"]
                if i >= 1:
                    s.wait_at("q", "sv", marks[i - 1])
                    s.wait_at("q", "lg", marks_lg[i - 1])
                s.op("q", lambda E, t0=t0: E.dma_start(out=sb3(sb["x0"][:], TT), in_=fm_ap(hb["x"], t0, t0 + TT)))
                s.wait("s", ["ld", "sm", "sv"])
                s.wait("v", ["ld"])
                emit_ln(s, ps, sb, xsb[:], sb["xhat"][:], sb["sq"][:], TT)
                s.wait("t", ["sv", "ss"])
                groups = []
                for co in range(HC):
                    def mkg(co, w, par):
                        def mm(E, pb, ci):
                            return E.matmul(
                                out=pb[:],
                                lhsT=sb[w][:, ci * 3072 + co * 128: ci * 3072 + (co + 1) * 128],
                                rhs=sb["xhat"][:, ci * TT:(ci + 1) * TT],
                                start=(ci == 0), stop=(ci == DCH - 1))
                        mms = [lambda E, pb, ci=ci: mm(E, pb, ci) for ci in range(DCH)]
                        if w == "wg":
                            ev = lambda E, pb: E.activation(out=sb["sg" + str(par)][:], in_=pb[:],
                                                            func=AF.Silu, bias=cv[:, 3:4], scale=cv[:, 6:7])
                            return (mms, ev, "s")
                        ev = lambda E, pb: E.scalar_tensor_tensor(
                            out=sb["hbuf"][:, co * TT:(co + 1) * TT], in0=pb[:],
                            scalar=1.0 / 127.0, in1=sb["sg" + str(par)][:], op0=ALU.mult, op1=ALU.mult)
                        return (mms, ev, "v", ["ss"])
                    groups.append(mkg(co, "wg", co % 2))
                    groups.append(mkg(co, "wu", co % 2))
                emit_mm_pipe2(s, groups, [ps["pq"], ps["pu"], ps["stat"], ps["rep0"]])
                groups = []
                for co in range(DCH):
                    def mkd(co):
                        def mm(E, pb, ci):
                            return E.matmul(
                                out=pb[:],
                                lhsT=sb["wd"][:, ci * 1024 + co * 128: ci * 1024 + (co + 1) * 128],
                                rhs=sb["hbuf"][:, ci * TT:(ci + 1) * TT],
                                start=(ci == 0), stop=(ci == HC - 1))
                        mms = [lambda E, pb, ci=ci: mm(E, pb, ci) for ci in range(HC)]
                        ev = lambda E, pb: E.scalar_tensor_tensor(
                            out=xsb[:, co * TT:(co + 1) * TT], in0=pb[:],
                            scalar=1.0 / 127.0, in1=xsb[:, co * TT:(co + 1) * TT],
                            op0=ALU.mult, op1=ALU.add)
                        return (mms, ev, "v")
                    groups.append(mkd(co))
                emit_mm_pipe2(s, groups, [ps["pq"], ps["pu"]])
                s.wait("g", ["sv"])
                s.op("g", lambda E, t0=t0, xsb=xsb: E.dma_start(out=fm_ap(hb["xo"], t0, t0 + TT), in_=sb3(xsb[:], TT)))
                marks.append(s.cnt["sv"])
                marks_lg.append(s.cnt["lg"])
            s.wait_at("g", "lg", s.cnt["lg"])

    return body



def build_l1f(Tc=4096):
    use_consts = BASE_CONSTS
    nc = bass.Bass("TRN2", target_bir_lowering=False)
    NT = Tc // TT
    HC = 24
    hb = {}
    hb["x"] = nc.dram_tensor("x", [D, Tc], F32, kind="ExternalInput").ap()
    hb["wgT"] = nc.dram_tensor("wgT", [D, 3072], I32, kind="ExternalInput").ap()
    hb["wuT"] = nc.dram_tensor("wuT", [D, 3072], I32, kind="ExternalInput").ap()
    hb["wdT"] = nc.dram_tensor("wdT", [3072, D], I32, kind="ExternalInput").ap()
    for name, shp, dt in CONST_SPECS:
        if name in use_consts:
            hb[name] = nc.dram_tensor(name, shp, dt, kind="ExternalInput").ap()
    hb["xo"] = nc.dram_tensor("xo", [D, Tc], F32, kind="ExternalOutput").ap()

    body = setup_l1f(nc, hb, "", Tc)
    with contextlib.ExitStack() as c2:
        run_sched(nc, c2, body)
    return nc

def l1f_inmaps(ins, consts):
    consts = consts_subset(consts, BASE_CONSTS)
    wgT = np.ascontiguousarray(np.asarray(ins["L1_ffn_w_gate"]).T)
    wuT = np.ascontiguousarray(np.asarray(ins["L1_ffn_w_up"]).T)
    wdT = np.ascontiguousarray(np.asarray(ins["L1_ffn_w_down"]).T)
    maps = []
    for c in range(NCORE):
        m = dict(consts)
        m["wgT"] = wgT
        m["wuT"] = wuT
        m["wdT"] = wdT
        maps.append(m)
    return maps



def setup_l2f(nc, hb, prefix="", Tc=4096, add_parts=False):
    NT = Tc // TT
    ctx = contextlib.ExitStack()
    with ctx:
        sb = {}

        def S(name, shape, dt=F32):
            sb[name] = ctx.enter_context(nc.sbuf_tensor("sb_" + prefix + name, shape, dt))
            return sb[name]

        for name, shp, dt in CONST_SPECS:
            if name in hb:
                S(name, shp, dt)
        for w in ("wr", "wk", "wv"):
            S(w, [128, DCH * 1024], BF16)
        S("wstg0", [128, 1024], F32); S("wstg1", [128, 1024], F32)
        S("x0", [128, DCH * TT])
        S("xhat", [128, DCH * TT], BF16)
        S("sq", [128, DCH * TT], BF16)
        S("ext", [128, DCH * (TT + 1)], BF16)
        S("xrk", [128, DCH * TT], BF16)
        S("rbuf", [128, DCH * TT], BF16)
        S("kact", [128, DCH * TT], BF16)
        S("mur", [128, 8]); S("muk", [128, 8])
        S("omr", [128, 8]); S("omk", [128, 8])
        S("hx", [128, DCH])
        S("hx2", [128, DCH]); S("hx3", [128, DCH])
        S("px", [128, DCH * TT])
        S("kvb", [128, TT], BF16)
        S("gv", [128, TT], F32)
        S("row0", [1, TT]); S("row1", [1, TT])

        ps = {}
        for name, shape, dt in [("stat", [128, TT], F32), ("rep0", [128, TT], F32),
                                ("rep1", [128, TT], F32), ("pq", [128, TT], F32)]:
            ps[name] = ctx.enter_context(nc.psum_tensor(prefix + name, shape, dt))

        def body(s):
            emit_load_consts(s, hb, sb)
            for small in ("mur", "muk"):
                s.op("q", lambda E, d=sb[small], sr=hb[small]: E.dma_start(out=d[:], in_=sr))
            s.op("q", lambda E: E.dma_start(out=sb["hx"][:], in_=hb["xhalo"]))
            if add_parts == 2:
                s.op("q", lambda E: E.dma_start(out=sb["hx2"][:], in_=hb["hh0"]))
                s.op("q", lambda E: E.dma_start(out=sb["hx3"][:], in_=hb["hh1"]))
                s.wait("v", ["ld"])
                s.op("v", lambda E: E.tensor_tensor(out=sb["hx"][:], in0=sb["hx"][:], in1=sb["hx2"][:], op=ALU.add))
                s.op("v", lambda E: E.tensor_tensor(out=sb["hx"][:], in0=sb["hx"][:], in1=sb["hx3"][:], op=ALU.add))
            elif add_parts == 1:
                s.op("q", lambda E: E.dma_start(out=sb["hx2"][:], in_=hb["hh0"]))
                s.op("q", lambda E: E.dma_start(out=sb["hx3"][:], in_=hb["hflag"]))
                s.wait("v", ["ld"])
                s.op("v", lambda E: E.tensor_tensor(out=sb["hx"][:], in0=sb["hx2"][:], in1=sb["hx3"][:], op=ALU.mult))
            chunks, dsts = [], []
            for w in ("wr", "wk", "wv"):
                for ci in range(DCH):
                    chunks.append(hb[w + "T"][ci * 128:(ci + 1) * 128, :])
                    dsts.append(sb[w][:, ci * 1024:(ci + 1) * 1024])
            emit_cast_weight(s, chunks, [sb["wstg0"], sb["wstg1"]], dsts)
            s.wait("v", ["ld"])
            s.op("v", lambda E: E.scalar_tensor_tensor(out=sb["omr"][:], in0=sb["mur"][:],
                                                       scalar=-1.0, in1=sb["c_cvec"][:, 2:3].broadcast_to((128, 8)),
                                                       op0=ALU.mult, op1=ALU.add))
            s.op("v", lambda E: E.scalar_tensor_tensor(out=sb["omk"][:], in0=sb["muk"][:],
                                                       scalar=-1.0, in1=sb["c_cvec"][:, 2:3].broadcast_to((128, 8)),
                                                       op0=ALU.mult, op1=ALU.add))
            s.wait("s", ["ld"])
            emit_ln(s, ps, sb, sb["hx"][:], sb["sq"][:, 0:DCH], sb["sq"][:, DCH:2 * DCH], 1)
            s.wait("v", ["ld", "ss", "sm"])
            for c in range(DCH):
                s.op("v", lambda E, c=c: E.tensor_copy(
                    out=sb["ext"][:, c * (TT + 1): c * (TT + 1) + 1],
                    in_=sb["sq"][:, c:c + 1]))
            marks = []
            marks_lg = []
            for i in range(NT):
                t0 = i * TT
                xsb = sb["x0"]
                if i >= 1:
                    s.wait_at("q", "sv", marks[i - 1])
                    s.wait_at("q", "lg", marks_lg[i - 1])
                s.op("q", lambda E, t0=t0: E.dma_start(out=sb3(sb["x0"][:], TT), in_=fm_ap(hb["x"], t0, t0 + TT)))
                if add_parts:
                    for pk in (("pp0", "pp1") if add_parts == 2 else ("pp0",)):
                        s.op("q", lambda E, t0=t0, pk=pk: E.dma_start(out=sb3(sb["px"][:], TT), in_=(hb[pk](t0, t0 + TT) if callable(hb[pk]) else fm_ap(hb[pk], t0, t0 + TT))))
                        s.wait("v", ["ld", "sm", "ss"])
                        for c in range(DCH):
                            s.op("v", lambda E, c=c: E.tensor_tensor(
                                out=sb["x0"][:, c * TT:(c + 1) * TT], in0=sb["x0"][:, c * TT:(c + 1) * TT],
                                in1=sb["px"][:, c * TT:(c + 1) * TT], op=ALU.add))
                s.wait("s", ["ld", "sm", "sv"])
                s.wait("v", ["ld"])
                emit_ln(s, ps, sb, xsb[:], sb["xhat"][:], sb["sq"][:], TT)
                s.wait("v", ["sm", "ss"])
                for c in range(DCH):
                    s.op("v", lambda E, c=c: E.tensor_copy(
                        out=sb["ext"][:, c * (TT + 1) + 1: (c + 1) * (TT + 1)],
                        in_=sb["xhat"][:, c * TT:(c + 1) * TT]))
                for which, mu, om, wname, dst in (("r", "mur", "omr", "wr", "rbuf"), ("k", "muk", "omk", "wk", "kact")):
                    s.wait("v", ["sm"])
                    for c in range(DCH):
                        s.op("v", lambda E, c=c, mu=mu: E.tensor_scalar(
                            out=sb["xrk"][:, c * TT:(c + 1) * TT], in0=sb["xhat"][:, c * TT:(c + 1) * TT],
                            scalar1=sb[mu][:, c:c + 1], scalar2=None, op0=ALU.mult))
                    for c in range(DCH):
                        s.op("v", lambda E, c=c, om=om: E.scalar_tensor_tensor(
                            out=sb["xrk"][:, c * TT:(c + 1) * TT],
                            in0=sb["ext"][:, c * (TT + 1): c * (TT + 1) + TT],
                            scalar=sb[om][:, c:c + 1],
                            in1=sb["xrk"][:, c * TT:(c + 1) * TT], op0=ALU.mult, op1=ALU.add))
                    s.wait("t", ["sv", "ss"])
                    groups = []
                    for co in range(DCH):
                        def mkr(co, w, d, fn):
                            def mm(E, pb, ci):
                                return E.matmul(
                                    out=pb[:],
                                    lhsT=sb[w][:, ci * 1024 + co * 128: ci * 1024 + (co + 1) * 128],
                                    rhs=sb["xrk"][:, ci * TT:(ci + 1) * TT],
                                    start=(ci == 0), stop=(ci == DCH - 1))
                            mms = [lambda E, pb, ci=ci: mm(E, pb, ci) for ci in range(DCH)]
                            ev = lambda E, pb: E.activation(
                                out=sb[d][:, co * TT:(co + 1) * TT], in_=pb[:], func=fn)
                            return (mms, ev, "s")
                        groups.append(mkr(co, wname, dst, AF.Sigmoid if which == "r" else AF.Relu))
                    emit_mm_pipe2(s, groups, [ps["pq"], ps["rep0"]])
                s.wait("s", ["sv"])
                for c in range(DCH):
                    s.op("s", lambda E, c=c: E.activation(
                        out=sb["kact"][:, c * TT:(c + 1) * TT], in_=sb["kact"][:, c * TT:(c + 1) * TT],
                        func=AF.Square))
                s.wait("v", ["ss"])
                for c in range(DCH):
                    s.op("v", lambda E, c=c: E.tensor_copy(
                        out=sb["ext"][:, c * (TT + 1): c * (TT + 1) + 1],
                        in_=sb["xhat"][:, (c + 1) * TT - 1: (c + 1) * TT]))
                for co in range(DCH):
                    s.wait("t", ["sv", "ss"])
                    for ci in range(DCH):
                        s.op("t", lambda E, ci=ci, co=co: E.matmul(
                            out=ps["pq"][:],
                            lhsT=sb["wv"][:, ci * 1024 + co * 128: ci * 1024 + (co + 1) * 128],
                            rhs=sb["kact"][:, ci * TT:(ci + 1) * TT],
                            start=(ci == 0), stop=(ci == DCH - 1)))
                    s.wait("s", ["sm"])
                    s.op("s", lambda E: E.copy(out=sb["kvb"][:], in_=ps["pq"][:]))
                    s.wait("v", ["ss"])
                    s.op("v", lambda E, co=co: E.tensor_tensor(
                        out=sb["gv"][:], in0=sb["kvb"][:],
                        in1=sb["rbuf"][:, co * TT:(co + 1) * TT], op=ALU.mult))
                    s.op("v", lambda E, co=co, xsb=xsb: E.tensor_tensor(
                        out=xsb[:, co * TT:(co + 1) * TT], in0=sb["gv"][:],
                        in1=xsb[:, co * TT:(co + 1) * TT], op=ALU.add))
                s.wait("g", ["sv"])
                s.op("g", lambda E, t0=t0, xsb=xsb: E.dma_start(out=fm_ap(hb["xo"], t0, t0 + TT), in_=sb3(xsb[:], TT)))
                marks.append(s.cnt["sv"])
                marks_lg.append(s.cnt["lg"])
            s.wait_at("g", "lg", s.cnt["lg"])

    return body



def build_l2f(Tc=4096, add_parts=False):
    use_consts = BASE_CONSTS
    nc = bass.Bass("TRN2", target_bir_lowering=False)
    NT = Tc // TT
    hb = {}
    hb["x"] = nc.dram_tensor("x", [D, Tc], F32, kind="ExternalInput").ap()
    hb["xhalo"] = nc.dram_tensor("xhalo", [128, 8], F32, kind="ExternalInput").ap()
    if add_parts:
        hb["pp0"] = nc.dram_tensor("pp0", [D, Tc], F32, kind="ExternalInput").ap()
        hb["pp1"] = nc.dram_tensor("pp1", [D, Tc], F32, kind="ExternalInput").ap()
        hb["hh0"] = nc.dram_tensor("hh0", [128, 8], F32, kind="ExternalInput").ap()
        hb["hh1"] = nc.dram_tensor("hh1", [128, 8], F32, kind="ExternalInput").ap()
    for w in ("wrT", "wkT", "wvT"):
        hb[w] = nc.dram_tensor(w, [D, D], F32, kind="ExternalInput").ap()
    hb["mur"] = nc.dram_tensor("mur", [128, 8], F32, kind="ExternalInput").ap()
    hb["muk"] = nc.dram_tensor("muk", [128, 8], F32, kind="ExternalInput").ap()
    for name, shp, dt in CONST_SPECS:
        if name in use_consts:
            hb[name] = nc.dram_tensor(name, shp, dt, kind="ExternalInput").ap()
    hb["xo"] = nc.dram_tensor("xo", [D, Tc], F32, kind="ExternalOutput").ap()

    body = setup_l2f(nc, hb, "", Tc, add_parts)
    with contextlib.ExitStack() as c2:
        run_sched(nc, c2, body)
    return nc

def l2f_inmaps(ins, consts, x_prev_full=None):
    consts = consts_subset(consts, BASE_CONSTS)
    maps = []
    for c in range(NCORE):
        b, h = c // 2, c % 2
        m = dict(consts)
        m["wrT"] = np.ascontiguousarray(np.asarray(ins["L2_ffn_W_r"]).T)
        m["wkT"] = np.ascontiguousarray(np.asarray(ins["L2_ffn_W_k"]).T)
        m["wvT"] = np.ascontiguousarray(np.asarray(ins["L2_ffn_W_v"]).T)
        m["mur"] = host_col(ins["L2_ffn_mu_r"])
        m["muk"] = host_col(ins["L2_ffn_mu_k"])
        if h == 0 or x_prev_full is None:
            m["xhalo"] = np.zeros((128, 8), dtype=np.float32)
        else:
            m["xhalo"] = host_col(x_prev_full[b, 4095])
        maps.append(m)
    return maps



def setup_l3f(nc, hb, prefix="", Tc=4096, add_parts=False):
    NT = Tc // TT
    ctx = contextlib.ExitStack()
    with ctx:
        sb = {}

        def S(name, shape, dt=F32):
            sb[name] = ctx.enter_context(nc.sbuf_tensor("sb_" + prefix + name, shape, dt))
            return sb[name]

        for name, shp, dt in CONST_SPECS:
            if name in hb:
                S(name, shp, dt)
        S("we", [128, 64 * 1024], BF16)
        S("gw", [128, DCH * 8], BF16)
        S("wstg0", [128, 1024], F32); S("wstg1", [128, 1024], F32)
        S("x0", [128, DCH * TT])
        S("xhat", [128, DCH * TT], BF16)
        S("xe", [128, DCH * TT], BF16)
        sb["sq"] = sb["xe"]
        S("wrep", [128, 8 * TT], BF16)
        S("wfull", [128, 32])
        S("msk", [128, 32])
        S("tk", [128, 16])
        S("wt", [32, 128])
        S("row0", [1, TT]); S("row1", [1, TT])

        ps = {}
        pool = []
        for k in range(8):
            pool.append(ctx.enter_context(nc.psum_tensor(f"{prefix}psl{k}", [128, TT], F32)))
        ps["stat"] = pool[0]; ps["rep0"] = pool[1]; ps["rep1"] = pool[2]

        def body(s):
            emit_load_consts(s, hb, sb)
            chunks = [hb["ewT"][k * 128:(k + 1) * 128, :] for k in range(64)]
            dsts = [sb["we"][:, k * 1024:(k + 1) * 1024] for k in range(64)]
            emit_cast_weight(s, chunks, [sb["wstg0"], sb["wstg1"]], dsts)
            s.wait("q", ["sv"])
            s.op("q", lambda E: E.dma_start(out=sb["wstg0"][:, 0:64].rearrange("p (c g) -> p c g", c=8),
                                            in_=hb["gwT"].rearrange("(c p) g -> p c g", p=128)))
            s.wait("v", ["ld"])
            s.op("v", lambda E: E.tensor_copy(out=sb["gw"][:], in_=sb["wstg0"][:, 0:64]))
            cv = sb["c_cvec"]

            marks = []
            marks_lg = []
            for i in range(NT):
                t0 = i * TT
                xsb = sb["x0"]
                if i >= 1:
                    s.wait_at("q", "sv", marks[i - 1])
                    s.wait_at("q", "lg", marks_lg[i - 1])
                s.op("q", lambda E, t0=t0: E.dma_start(out=sb3(sb["x0"][:], TT), in_=fm_ap(hb["x"], t0, t0 + TT)))
                if add_parts:
                    for pk in (("pp0", "pp1") if add_parts == 2 else ("pp0",)):
                        for g in range(4):
                            s.wait("q", ["sv"])
                            s.op("q", lambda E, t0=t0, pk=pk, g=g: E.dma_start(
                                out=sb["wstg0"][:].rearrange("p (c t) -> p c t", c=2),
                                in_=(hb[pk](t0, t0 + TT) if callable(hb[pk]) else fm_ap(hb[pk], t0, t0 + TT))[:, g * 2:(g + 1) * 2, :]))
                            s.wait("v", ["ld", "sm", "ss"])
                            for c2 in range(2):
                                c = g * 2 + c2
                                s.op("v", lambda E, c=c, c2=c2: E.tensor_tensor(
                                    out=sb["x0"][:, c * TT:(c + 1) * TT], in0=sb["x0"][:, c * TT:(c + 1) * TT],
                                    in1=sb["wstg0"][:, c2 * TT:(c2 + 1) * TT], op=ALU.add))
                s.wait("s", ["ld", "sm", "sv"])
                emit_ln(s, ps, sb, xsb[:], sb["xhat"][:], sb["sq"][:], TT)
                s.wait("t", ["sv", "ss"])
                for j in range(4):
                    for ci in range(DCH):
                        s.op("t", lambda E, j=j, ci=ci: E.matmul(
                            out=pool[3][:, j * 8:(j + 1) * 8],
                            lhsT=sb["xhat"][:, ci * TT + j * 128: ci * TT + (j + 1) * 128],
                            rhs=sb["gw"][:, ci * 8:(ci + 1) * 8],
                            start=(ci == 0), stop=(ci == DCH - 1)))
                s.wait("v", ["sm"])
                for j in range(4):
                    pj = pool[3][:, j * 8:(j + 1) * 8]
                    s.op("v", lambda E, pj=pj, j=j: E.reduce_max(
                        out=sb["tk"][:, j:j + 1], in_=pj, axis=mybir.AxisListType.X))
                    s.op("v", lambda E, pj=pj, j=j: E.tensor_scalar(
                        out=sb["msk"][:, j * 8:(j + 1) * 8], in0=pj,
                        scalar1=sb["tk"][:, j:j + 1], scalar2=None, op0=ALU.is_ge))
                    s.op("v", lambda E, pj=pj, j=j: E.scalar_tensor_tensor(
                        out=sb["wfull"][:, j * 8:(j + 1) * 8], in0=sb["msk"][:, j * 8:(j + 1) * 8],
                        scalar=-1e30, in1=pj, op0=ALU.mult, op1=ALU.add))
                    s.op("v", lambda E, j=j: E.reduce_max(
                        out=sb["tk"][:, 4 + j:5 + j], in_=sb["wfull"][:, j * 8:(j + 1) * 8],
                        axis=mybir.AxisListType.X))
                    s.op("v", lambda E, j=j: E.tensor_scalar(
                        out=sb["wfull"][:, j * 8:(j + 1) * 8], in0=sb["wfull"][:, j * 8:(j + 1) * 8],
                        scalar1=sb["tk"][:, 4 + j:5 + j], scalar2=None, op0=ALU.is_ge))
                    s.op("v", lambda E, j=j: E.tensor_tensor(
                        out=sb["tk"][:, 8 + j:9 + j], in0=sb["tk"][:, 4 + j:5 + j],
                        in1=sb["tk"][:, j:j + 1], op=ALU.subtract))
                s.wait("s", ["sv"])
                for j in range(4):
                    s.op("s", lambda E, j=j: E.activation(
                        out=sb["tk"][:, 12 + j:13 + j], in_=sb["tk"][:, 8 + j:9 + j], func=AF.Sigmoid))
                s.wait("v", ["ss"])
                for j in range(4):
                    s.op("v", lambda E, j=j: E.scalar_tensor_tensor(
                        out=sb["tk"][:, 8 + j:9 + j], in0=sb["tk"][:, 12 + j:13 + j],
                        scalar=-1.0, in1=cv[:, 2:3], op0=ALU.mult, op1=ALU.add))
                    s.op("v", lambda E, j=j: E.tensor_scalar(
                        out=sb["wfull"][:, j * 8:(j + 1) * 8], in0=sb["wfull"][:, j * 8:(j + 1) * 8],
                        scalar1=sb["tk"][:, 12 + j:13 + j], scalar2=None, op0=ALU.mult))
                    s.op("v", lambda E, j=j: E.scalar_tensor_tensor(
                        out=sb["wfull"][:, j * 8:(j + 1) * 8], in0=sb["msk"][:, j * 8:(j + 1) * 8],
                        scalar=sb["tk"][:, 8 + j:9 + j], in1=sb["wfull"][:, j * 8:(j + 1) * 8],
                        op0=ALU.mult, op1=ALU.add))
                s.wait("t", ["sv"])
                s.op("t", lambda E: E.transpose(out=pool[4][0:32, 0:128], in_=sb["wfull"][:],
                                                identity=sb["c_eyef"][:]))
                s.wait("v", ["sm"])
                s.op("v", lambda E: E.tensor_copy(out=sb["wt"][:], in_=pool[4][0:32, 0:128]))
                for e in range(8):
                    s.wait("t", ["sv"])
                    for j in range(4):
                        s.op("t", lambda E, e=e, j=j: E.matmul(
                            out=pool[5][:, j * 128:(j + 1) * 128],
                            lhsT=sb["c_hot32"][:, (j * 8 + e) * 128:(j * 8 + e + 1) * 128],
                            rhs=sb["wt"][:], start=True, stop=True))
                    s.wait("v", ["sm"])
                    s.op("v", lambda E, e=e: E.tensor_copy(out=sb["wrep"][:, e * TT:(e + 1) * TT],
                                                           in_=pool[5][:, 0:TT]))
                for e in range(8):
                    s.wait("v", ["sm"])
                    for c in range(DCH):
                        s.op("v", lambda E, e=e, c=c: E.scalar_tensor_tensor(
                            out=sb["xe"][:, c * TT:(c + 1) * TT],
                            in0=sb["wrep"][:, e * TT:(e + 1) * TT], scalar=1.0,
                            in1=sb["xhat"][:, c * TT:(c + 1) * TT], op0=ALU.mult, op1=ALU.mult))
                    s.wait("t", ["sv"])
                    for ci in range(DCH):
                        for co in range(DCH):
                            s.op("t", lambda E, e=e, ci=ci, co=co: E.matmul(
                                out=pool[co][:],
                                lhsT=sb["we"][:, (e * 8 + ci) * 1024 + co * 128: (e * 8 + ci) * 1024 + (co + 1) * 128],
                                rhs=sb["xe"][:, ci * TT:(ci + 1) * TT],
                                start=(e == 0 and ci == 0), stop=(e == 7 and ci == 7)))
                s.wait("v", ["sm"])
                for co in range(DCH):
                    s.op("v", lambda E, co=co, xsb=xsb: E.tensor_tensor(
                        out=xsb[:, co * TT:(co + 1) * TT], in0=pool[co][:],
                        in1=xsb[:, co * TT:(co + 1) * TT], op=ALU.add))
                s.wait("g", ["sv"])
                s.op("g", lambda E, t0=t0, xsb=xsb: E.dma_start(out=fm_ap(hb["xo"], t0, t0 + TT), in_=sb3(xsb[:], TT)))
                marks.append(s.cnt["sv"])
                marks_lg.append(s.cnt["lg"])
            s.wait_at("g", "lg", s.cnt["lg"])

    return body



def build_l3f(Tc=4096, add_parts=False):
    use_consts = BASE_CONSTS + ["c_eyef", "c_hot32"]
    nc = bass.Bass("TRN2", target_bir_lowering=False)
    NT = Tc // TT
    hb = {}
    hb["x"] = nc.dram_tensor("x", [D, Tc], F32, kind="ExternalInput").ap()
    hb["gwT"] = nc.dram_tensor("gwT", [D, 8], F32, kind="ExternalInput").ap()
    if add_parts:
        hb["pp0"] = nc.dram_tensor("pp0", [D, Tc], F32, kind="ExternalInput").ap()
        hb["pp1"] = nc.dram_tensor("pp1", [D, Tc], F32, kind="ExternalInput").ap()
    hb["ewT"] = nc.dram_tensor("ewT", [8 * D, D], F32, kind="ExternalInput").ap()
    for name, shp, dt in CONST_SPECS:
        if name in use_consts:
            hb[name] = nc.dram_tensor(name, shp, dt, kind="ExternalInput").ap()
    hb["xo"] = nc.dram_tensor("xo", [D, Tc], F32, kind="ExternalOutput").ap()

    body = setup_l3f(nc, hb, "", Tc, add_parts)
    with contextlib.ExitStack() as c2:
        run_sched(nc, c2, body)
    return nc

def l3f_inmaps(ins, consts):
    consts = consts_subset(consts, BASE_CONSTS + ['c_eyef', 'c_hot32'])
    ew = np.asarray(ins["L3_ffn_expert_w"])
    ewT = np.ascontiguousarray(np.transpose(ew, (0, 2, 1)).reshape(8 * D, D))
    gwT = np.ascontiguousarray(np.asarray(ins["L3_ffn_gate_w"]).T)
    maps = []
    for c in range(NCORE):
        m = dict(consts)
        m["ewT"] = ewT
        m["gwT"] = gwT
        maps.append(m)
    return maps



def emit_elu1(s, sb, ps_src, dst_sl, tmp_f32, n):
    s.wait("v", ["sm"])
    s.op("v", lambda E: E.tensor_scalar(out=tmp_f32[:, 0:n], in0=ps_src, scalar1=0.0,
                                        scalar2=None, op0=ALU.min))
    s.wait("s", ["sv"])
    s.op("s", lambda E: E.activation(out=tmp_f32[:, 0:n], in_=tmp_f32[:, 0:n], func=AF.Exp))
    s.wait("v", ["ss"])
    s.op("v", lambda E: E.tensor_scalar(out=dst_sl, in0=ps_src, scalar1=0.0,
                                        scalar2=None, op0=ALU.max))
    s.op("v", lambda E: E.tensor_tensor(out=dst_sl, in0=dst_sl, in1=tmp_f32[:, 0:n], op=ALU.add))


def setup_l1a(nc, hb, prefix="", Tc=4096):
    NT = Tc // TT
    ctx = contextlib.ExitStack()
    with ctx:
        sb = {}

        def S(name, shape, dt=F32):
            sb[name] = ctx.enter_context(nc.sbuf_tensor("sb_" + prefix + name, shape, dt))
            return sb[name]

        for name, shp, dt in CONST_SPECS:
            if name in hb:
                S(name, shp, dt)
        S("wqkv", [128, DCH * 3072], BF16)
        S("bq_bf", [1, 3072], BF16)
        S("wstg0", [128, 1024], F32); S("wstg1", [128, 1024], F32)
        S("x0", [128, DCH * TT])
        S("xhat", [128, DCH * TT], BF16)
        S("sq", [128, DCH * TT], BF16)
        S("qp", [128, DCH * TT], BF16)
        S("kp", [128, 4 * 1024], BF16)
        S("vt", [128, 4 * 1024], BF16)
        S("kvacc", [128, DCH * 1024])
        S("ksacc", [128, 8])
        S("tmp", [128, 1024])
        S("row0", [1, TT]); S("row1", [1, TT])

        ps = {}
        for name, shape, dt in [("stat", [128, TT], F32), ("rep0", [128, TT], F32),
                                ("rep1", [128, TT], F32), ("pq", [128, TT], F32),
                                ("pvt", [128, 1024], F32), ("pks", [128, 8], F32)]:
            ps[name] = ctx.enter_context(nc.psum_tensor(prefix + name, shape, dt))

        def body(s):
            emit_load_consts(s, hb, sb)
            chunks, dsts = [], []
            for ci in range(DCH):
                for half in range(3):
                    chunks.append(hb["wqkv"][ci * 128:(ci + 1) * 128, half * 1024:(half + 1) * 1024])
                    dsts.append(sb["wqkv"][:, ci * 3072 + half * 1024: ci * 3072 + (half + 1) * 1024])
            emit_cast_weight(s, chunks, [sb["wstg0"], sb["wstg1"]], dsts)
            s.wait("q", ["sv"])
            s.op("q", lambda E: E.dma_start(out=sb["wstg0"][0:1, :], in_=hb["bqkv"][0:1, 0:1024]))
            s.op("q", lambda E: E.dma_start(out=sb["wstg1"][0:1, :], in_=hb["bqkv"][0:1, 1024:2048]))
            s.wait("v", ["ld"])
            s.op("v", lambda E: E.tensor_copy(out=sb["bq_bf"][0:1, 0:1024], in_=sb["wstg0"][0:1, :]))
            s.op("v", lambda E: E.tensor_copy(out=sb["bq_bf"][0:1, 1024:2048], in_=sb["wstg1"][0:1, :]))
            s.wait("q", ["sv"])
            s.op("q", lambda E: E.dma_start(out=sb["wstg0"][0:1, :], in_=hb["bqkv"][0:1, 2048:3072]))
            s.wait("v", ["ld"])
            s.op("v", lambda E: E.tensor_copy(out=sb["bq_bf"][0:1, 2048:3072], in_=sb["wstg0"][0:1, :]))
            s.op("v", lambda E: E.memset(sb["kvacc"][:], 0.0))
            s.op("v", lambda E: E.memset(sb["ksacc"][:], 0.0))

            marks = []
            marks_lg = []
            for i in range(NT):
                t0 = i * TT
                xsb = sb["x0"]
                if i >= 1:
                    s.wait_at("q", "sv", marks[i - 1])
                    s.wait_at("q", "lg", marks_lg[i - 1])
                s.op("q", lambda E, t0=t0: E.dma_start(out=sb3(sb["x0"][:], TT), in_=fm_ap(hb["x"], t0, t0 + TT)))
                s.wait("s", ["ld", "sm", "sv"])
                s.wait("v", ["ld"])
                emit_ln(s, ps, sb, xsb[:], sb["xhat"][:], sb["sq"][:], TT)
                for co in range(DCH):
                    s.wait("t", ["sv", "ss"])
                    for ci in range(DCH):
                        s.op("t", lambda E, ci=ci, co=co: E.matmul(
                            out=ps["pq"][:],
                            lhsT=sb["wqkv"][:, ci * 3072 + co * 128: ci * 3072 + (co + 1) * 128],
                            rhs=sb["xhat"][:, ci * TT:(ci + 1) * TT],
                            start=(ci == 0), stop=False))
                    s.op("t", lambda E, co=co: E.matmul(
                        out=ps["pq"][:], lhsT=sb["bq_bf"][0:1, co * 128:(co + 1) * 128],
                        rhs=sb["c_ones_bf"][0:1, :], start=False, stop=True))
                    emit_elu1(s, sb, ps["pq"][:], sb["qp"][:, co * TT:(co + 1) * TT], sb["tmp"], TT)
                s.wait("g", ["sv"])
                s.op("g", lambda E, t0=t0: E.dma_start(out=fm_ap(hb["qp_o"], t0, t0 + TT), in_=sb3(sb["qp"][:], TT)))
                for j in range(4):
                    for which, base, dst in (("k", 1024, "kp"), ("v", 2048, "vt")):
                        s.wait("t", ["sv", "ss"])
                        for h in range(2):
                            for ci in range(DCH):
                                s.op("t", lambda E, ci=ci, j=j, h=h, base=base: E.matmul(
                                    out=ps["pvt"][:, h * 512:(h + 1) * 512],
                                    lhsT=sb["xhat"][:, ci * TT + j * 128: ci * TT + (j + 1) * 128],
                                    rhs=sb["wqkv"][:, ci * 3072 + base + h * 512: ci * 3072 + base + (h + 1) * 512],
                                    start=(ci == 0), stop=False))
                            s.op("t", lambda E, j=j, h=h, base=base: E.matmul(
                                out=ps["pvt"][:, h * 512:(h + 1) * 512],
                                lhsT=sb["c_ones_bf"][0:1, 0:128],
                                rhs=sb["bq_bf"][0:1, base + h * 512: base + (h + 1) * 512],
                                start=False, stop=True))
                        if which == "k":
                            emit_elu1(s, sb, ps["pvt"][:], sb["kp"][:, j * 1024:(j + 1) * 1024], sb["tmp"], 1024)
                        else:
                            s.wait("v", ["sm"])
                            s.op("v", lambda E, j=j: E.tensor_copy(out=sb["vt"][:, j * 1024:(j + 1) * 1024],
                                                                   in_=ps["pvt"][:]))
                s.wait("t", ["sv"])
                for co in range(DCH):
                    for j in range(4):
                        s.op("t", lambda E, j=j, co=co: E.matmul(
                            out=ps["pks"][:, co:co + 1],
                            lhsT=sb["kp"][:, j * 1024 + co * 128: j * 1024 + (co + 1) * 128],
                            rhs=sb["c_ones_bf"][:, 0:1],
                            start=(j == 0), stop=(j == 3)))
                s.wait("v", ["sm"])
                s.op("v", lambda E: E.tensor_tensor(out=sb["ksacc"][:], in0=ps["pks"][:],
                                                    in1=sb["ksacc"][:], op=ALU.add))
                for co in range(DCH):
                    s.wait("t", ["sv", "ss"])
                    for h in range(2):
                        for j in range(4):
                            s.op("t", lambda E, co=co, j=j, h=h: E.matmul(
                                out=ps["pvt"][:, h * 512:(h + 1) * 512],
                                lhsT=sb["kp"][:, j * 1024 + co * 128: j * 1024 + (co + 1) * 128],
                                rhs=sb["vt"][:, j * 1024 + h * 512: j * 1024 + (h + 1) * 512],
                                start=(j == 0), stop=(j == 3)))
                    s.wait("v", ["sm"])
                    s.op("v", lambda E, co=co: E.tensor_tensor(
                        out=sb["kvacc"][:, co * 1024:(co + 1) * 1024], in0=ps["pvt"][:],
                        in1=sb["kvacc"][:, co * 1024:(co + 1) * 1024], op=ALU.add))
                marks.append(s.cnt["sv"])
                marks_lg.append(s.cnt["lg"])
            s.wait("g", ["sv"])
            for co in range(DCH):
                s.op("g", lambda E, co=co: E.dma_start(out=hb["kv_o"][co * 128:(co + 1) * 128, :],
                                                       in_=sb["kvacc"][:, co * 1024:(co + 1) * 1024]))
            s.op("g", lambda E: E.dma_start(out=hb["ks_o"], in_=sb["ksacc"][:]))
            s.wait_at("g", "lg", s.cnt["lg"])

    return body



def build_l1a(Tc=4096):
    use_consts = BASE_CONSTS
    nc = bass.Bass("TRN2", target_bir_lowering=False)
    NT = Tc // TT
    hb = {}
    hb["x"] = nc.dram_tensor("x", [D, Tc], F32, kind="ExternalInput").ap()
    hb["wqkv"] = nc.dram_tensor("wqkv", [D, 3072], F32, kind="ExternalInput").ap()
    hb["bqkv"] = nc.dram_tensor("bqkv", [1, 3072], F32, kind="ExternalInput").ap()
    for name, shp, dt in CONST_SPECS:
        if name in use_consts:
            hb[name] = nc.dram_tensor(name, shp, dt, kind="ExternalInput").ap()
    hb["qp_o"] = nc.dram_tensor("qp_o", [D, Tc], BF16, kind="ExternalOutput").ap()
    hb["kv_o"] = nc.dram_tensor("kv_o", [D, 1024], F32, kind="ExternalOutput").ap()
    hb["ks_o"] = nc.dram_tensor("ks_o", [128, 8], F32, kind="ExternalOutput").ap()

    body = setup_l1a(nc, hb, "", Tc)
    with contextlib.ExitStack() as c2:
        run_sched(nc, c2, body)
    return nc

def l1a_inmaps(ins, consts):
    consts = consts_subset(consts, BASE_CONSTS)
    maps = []
    for c in range(NCORE):
        m = dict(consts)
        m["wqkv"] = np.ascontiguousarray(np.asarray(ins["L1_attn_Wqkv"]).astype(np.float32))
        m["bqkv"] = host_row(ins["L1_attn_bqkv"])
        maps.append(m)
    return maps



def emit_ln2(s, ps, sb, x_sl, out_sl, sq_sl, t, gcol=None, bcol=None, out2_sl=None):
    for c in range(DCH):
        s.op("s", lambda E, c=c: E.activation(out=sq_sl[:, c * t:(c + 1) * t],
                                              in_=x_sl[:, c * t:(c + 1) * t], func=AF.Square))
    s.wait("t", ["ss", "sv", "ld"])
    for c in range(DCH):
        s.op("t", lambda E, c=c: E.matmul(out=ps["stat"][0:1, :t], lhsT=sb["c_ones_f32"][:, 0:1],
                                          rhs=x_sl[:, c * t:(c + 1) * t],
                                          start=(c == 0), stop=(c == DCH - 1)))
    for c in range(DCH):
        s.op("t", lambda E, c=c: E.matmul(out=ps["stat"][64:65, :t], lhsT=sb["c_ones_bf"][:, 0:1],
                                          rhs=sq_sl[:, c * t:(c + 1) * t],
                                          start=(c == 0), stop=(c == DCH - 1)))
    s.wait("v", ["sm"])
    s.op("v", lambda E: E.tensor_scalar(out=sb["row0"][0:1, :t], in0=ps["stat"][0:1, :t],
                                        scalar1=1.0 / D, scalar2=None, op0=ALU.mult))
    s.op("v", lambda E: E.tensor_tensor(out=sb["row1"][0:1, :t], in0=sb["row0"][0:1, :t],
                                        in1=sb["row0"][0:1, :t], op=ALU.mult))
    s.op("v", lambda E: E.scalar_tensor_tensor(out=sb["row1"][0:1, :t], in0=ps["stat"][64:65, :t],
                                               scalar=1.0 / D, in1=sb["row1"][0:1, :t],
                                               op0=ALU.mult, op1=ALU.subtract))
    s.wait("s", ["sv"])
    s.op("s", lambda E: E.activation(out=sb["row1"][0:1, :t], in_=sb["row1"][0:1, :t],
                                     func=AF.Ln, bias=sb["c_cvec"][0:1, 0:1], scale=sb["c_cvec"][0:1, 2:3]))
    s.op("s", lambda E: E.activation(out=sb["row1"][0:1, :t], in_=sb["row1"][0:1, :t],
                                     func=AF.Exp, bias=sb["c_cvec"][0:1, 3:4], scale=sb["c_cvec"][0:1, 4:5]))
    s.wait("t", ["ss", "sv"])
    s.op("t", lambda E: E.matmul(out=ps["rep0"][:, :t], lhsT=sb["c_ones_f32"][0:1, :],
                                 rhs=sb["row0"][0:1, :t], start=True, stop=True))
    s.op("t", lambda E: E.matmul(out=ps["rep1"][:, :t], lhsT=sb["c_ones_f32"][0:1, :],
                                 rhs=sb["row1"][0:1, :t], start=True, stop=True))
    s.wait("v", ["sm"])
    for c in range(DCH):
        s.op("v", lambda E, c=c: E.tensor_tensor(out=sq_sl[:, c * t:(c + 1) * t],
                                                 in0=x_sl[:, c * t:(c + 1) * t],
                                                 in1=ps["rep0"][:, :t], op=ALU.subtract))
    for c in range(DCH):
        s.op("v", lambda E, c=c: E.tensor_tensor(out=out_sl[:, c * t:(c + 1) * t],
                                                 in0=sq_sl[:, c * t:(c + 1) * t],
                                                 in1=ps["rep1"][:, :t], op=ALU.mult))
    if gcol is not None:
        for c in range(DCH):
            s.op("v", lambda E, c=c: E.tensor_scalar(
                out=out_sl[:, c * t:(c + 1) * t], in0=out_sl[:, c * t:(c + 1) * t],
                scalar1=gcol[:, c:c + 1], scalar2=bcol[:, c:c + 1], op0=ALU.mult, op1=ALU.add))
    if out2_sl is not None:
        for c in range(DCH):
            s.op("v", lambda E, c=c: E.tensor_copy(out=out2_sl[:, c * t:(c + 1) * t],
                                                   in_=out_sl[:, c * t:(c + 1) * t]))


def setup_l1b(nc, hb, prefix="", Tc=4096, single=False):
    TL = 128
    FH = 22
    NT = Tc // TL
    ctx = contextlib.ExitStack()
    with ctx:
        sb = {}

        def S(name, shape, dt=F32):
            sb[name] = ctx.enter_context(nc.sbuf_tensor("sb_" + prefix + name, shape, dt))
            return sb[name]

        for name, shp, dt in CONST_SPECS:
            if name in hb:
                S(name, shp, dt)
        S("wgb", [128, DCH * 2816], BF16)
        S("wub", [128, DCH * 2816], BF16)
        S("wdb", [128, FH * 1024], BF16)
        S("kvb", [128, DCH * 1024], BF16)
        S("ksb", [128, 8], BF16)
        S("bgr", [1, 2816], BF16); S("bur", [1, 2816], BF16); S("bdr", [1, 1024], BF16)
        S("g1c", [128, 8]); S("b1c", [128, 8]); S("g2c", [128, 8]); S("b2c", [128, 8])
        S("wstg0", [128, 1024], F32); S("wstg1", [128, 1024], F32)
        S("xf", [128, DCH * TL])
        S("bb", [128, DCH * TL])
        S("cc", [128, DCH * TL])
        S("xb", [128, DCH * TL], BF16)
        S("qpt", [128, DCH * TL], BF16)
        S("hbuf", [128, FH * TL], BF16)
        sb["sq"] = sb["hbuf"]
        S("sg0", [128, TL], BF16)
        S("sg1", [128, TL], BF16)
        S("dsb", [128, TL])
        S("row0", [1, TL]); S("row1", [1, TL])

        ps = {}
        for name, shape, dt in [("stat", [128, TL], F32), ("rep0", [128, TL], F32),
                                ("rep1", [128, TL], F32), ("pq", [128, TL], F32),
                                ("pu", [128, TL], F32), ("pd", [128, TL], F32)]:
            ps[name] = ctx.enter_context(nc.psum_tensor(prefix + name, shape, dt))

        def body(s):
            emit_load_consts(s, hb, sb)
            for small in ("g1", "b1", "g2", "b2"):
                s.op("q", lambda E, d=sb[small + "c"], sr=hb[small]: E.dma_start(out=d[:], in_=sr))
            chunks, dsts = [], []
            for w, hbk, nch, ncol in (("wgb", "wg", DCH, 2816), ("wub", "wu", DCH, 2816), ("wdb", "wd", FH, 1024)):
                for ci in range(nch):
                    nh = (ncol + 1023) // 1024
                    for half in range(nh):
                        c0, c1 = half * 1024, min(ncol, (half + 1) * 1024)
                        chunks.append(hb[hbk][ci * 128:(ci + 1) * 128, c0:c1])
                        dsts.append(sb[w][:, ci * ncol + c0: ci * ncol + c1])
            emit_cast_weight(s, chunks, [sb["wstg0"], sb["wstg1"]], dsts)
            for co in range(DCH):
                s.wait("q", ["sv"])
                s.op("q", lambda E, co=co: E.dma_start(out=sb["wstg0"][:], in_=hb["kv0"][co * 128:(co + 1) * 128, :]))
                if not single:
                    s.op("q", lambda E, co=co: E.dma_start(out=sb["wstg1"][:], in_=hb["kv1"][co * 128:(co + 1) * 128, :]))
                s.wait("v", ["ld"])
                if single:
                    s.op("v", lambda E, co=co: E.tensor_copy(out=sb["kvb"][:, co * 1024:(co + 1) * 1024],
                                                             in_=sb["wstg0"][:]))
                else:
                    s.op("v", lambda E, co=co: E.tensor_tensor(out=sb["kvb"][:, co * 1024:(co + 1) * 1024],
                                                               in0=sb["wstg0"][:], in1=sb["wstg1"][:], op=ALU.add))
            s.wait("q", ["sv"])
            s.op("q", lambda E: E.dma_start(out=sb["wstg0"][:, 0:8], in_=hb["ks0"]))
            if not single:
                s.op("q", lambda E: E.dma_start(out=sb["wstg1"][:, 0:8], in_=hb["ks1"]))
            s.wait("v", ["ld"])
            if single:
                s.op("v", lambda E: E.tensor_copy(out=sb["ksb"][:], in_=sb["wstg0"][:, 0:8]))
            else:
                s.op("v", lambda E: E.tensor_tensor(out=sb["ksb"][:], in0=sb["wstg0"][:, 0:8],
                                                   in1=sb["wstg1"][:, 0:8], op=ALU.add))
            for src, dst, n in ((hb["bg"], sb["bgr"], 2816), (hb["bu"], sb["bur"], 2816), (hb["bd"], sb["bdr"], 1024)):
                nh = (n + 1023) // 1024
                for half in range(nh):
                    c0, c1 = half * 1024, min(n, (half + 1) * 1024)
                    s.wait("q", ["sv"])
                    s.op("q", lambda E, src=src, c0=c0, c1=c1: E.dma_start(
                        out=sb["wstg0"][0:1, 0:c1 - c0], in_=src[0:1, c0:c1]))
                    s.wait("v", ["ld"])
                    s.op("v", lambda E, dst=dst, c0=c0, c1=c1: E.tensor_copy(
                        out=dst[0:1, c0:c1], in_=sb["wstg0"][0:1, 0:c1 - c0]))

            cv = sb["c_cvec"]
            marks = []
            marks_lg = []
            for i in range(NT):
                t0 = i * TL
                if i >= 1:
                    s.wait_at("q", "sv", marks[i - 1])
                    s.wait_at("q", "lg", marks_lg[i - 1])
                s.op("q", lambda E, t0=t0: E.dma_start(out=sb3(sb["xf"][:], TL), in_=fm_ap(hb["x"], t0, t0 + TL)))
                s.op("q", lambda E, t0=t0: E.dma_start(out=sb3(sb["qpt"][:], TL), in_=fm_ap(hb["qp"], t0, t0 + TL)))
                s.wait("s", ["ld", "sm", "sv"])
                s.wait("v", ["ld"])
                emit_ln2(s, ps, sb, sb["xf"][:], sb["bb"][:], sb["sq"][:], TL)
                s.wait("t", ["sv", "ss", "ld"])
                for ci in range(DCH):
                    s.op("t", lambda E, ci=ci: E.matmul(
                        out=ps["pd"][0:1, :], lhsT=sb["ksb"][:, ci:ci + 1],
                        rhs=sb["qpt"][:, ci * TL:(ci + 1) * TL],
                        start=(ci == 0), stop=(ci == DCH - 1)))
                s.wait("v", ["sm"])
                s.op("v", lambda E: E.tensor_scalar(out=sb["row0"][0:1, :], in0=ps["pd"][0:1, :],
                                                    scalar1=1e-6, scalar2=None, op0=ALU.add))
                s.op("v", lambda E: E.reciprocal(out=sb["row1"][0:1, :], in_=sb["row0"][0:1, :]))
                s.wait("t", ["sv"])
                s.op("t", lambda E: E.matmul(out=ps["pd"][:], lhsT=sb["c_ones_f32"][0:1, :],
                                             rhs=sb["row1"][0:1, :], start=True, stop=True))
                s.wait("v", ["sm"])
                s.op("v", lambda E: E.tensor_copy(out=sb["dsb"][:], in_=ps["pd"][:]))
                s.wait("t", ["sv", "ss"])
                groups = []
                for co in range(DCH):
                    def mkn(co):
                        def mm(E, pb, ci):
                            return E.matmul(
                                out=pb[:],
                                lhsT=sb["kvb"][:, ci * 1024 + co * 128: ci * 1024 + (co + 1) * 128],
                                rhs=sb["qpt"][:, ci * TL:(ci + 1) * TL],
                                start=(ci == 0), stop=(ci == DCH - 1))
                        mms = [lambda E, pb, ci=ci: mm(E, pb, ci) for ci in range(DCH)]
                        ev = lambda E, pb: E.tensor_tensor(
                            out=sb["cc"][:, co * TL:(co + 1) * TL], in0=pb[:],
                            in1=sb["dsb"][:], op=ALU.mult)
                        return (mms, ev, "v")
                    groups.append(mkn(co))
                emit_mm_pipe2(s, groups, [ps["pq"], ps["pu"]])
                for co in range(DCH):
                    s.op("v", lambda E, co=co: E.tensor_tensor(
                        out=sb["cc"][:, co * TL:(co + 1) * TL], in0=sb["cc"][:, co * TL:(co + 1) * TL],
                        in1=sb["bb"][:, co * TL:(co + 1) * TL], op=ALU.add))
                s.wait("s", ["sv"])
                emit_ln2(s, ps, sb, sb["cc"][:], sb["bb"][:], sb["sq"][:], TL,
                         gcol=sb["g1c"], bcol=sb["b1c"], out2_sl=sb["xb"][:])
                s.wait("t", ["sv", "ss"])
                groups = []
                for co in range(FH):
                    def mks(co, w, b, par):
                        def mm(E, pb, ci):
                            return E.matmul(
                                out=pb[:],
                                lhsT=sb[w][:, ci * 2816 + co * 128: ci * 2816 + (co + 1) * 128],
                                rhs=sb["xb"][:, ci * TL:(ci + 1) * TL],
                                start=(ci == 0), stop=False)
                        mms = [lambda E, pb, ci=ci: mm(E, pb, ci) for ci in range(DCH)]
                        mms.append(lambda E, pb, b=b: E.matmul(
                            out=pb[:], lhsT=sb[b][0:1, co * 128:(co + 1) * 128],
                            rhs=sb["c_ones_bf"][0:1, 0:TL], start=False, stop=True))
                        if w == "wgb":
                            ev = lambda E, pb: E.activation(out=sb["sg" + str(par)][:], in_=pb[:], func=AF.Silu)
                            return (mms, ev, "s")
                        ev = lambda E, pb: E.scalar_tensor_tensor(
                            out=sb["hbuf"][:, co * TL:(co + 1) * TL], in0=pb[:],
                            scalar=1.0, in1=sb["sg" + str(par)][:], op0=ALU.mult, op1=ALU.mult)
                        return (mms, ev, "v", ["ss"])
                    groups.append(mks(co, "wgb", "bgr", co % 2))
                    groups.append(mks(co, "wub", "bur", co % 2))
                emit_mm_pipe2(s, groups, [ps["pq"], ps["pu"], ps["stat"], ps["rep0"]])
                groups = []
                for co in range(DCH):
                    def mkd2(co):
                        def mm(E, pb, ci):
                            return E.matmul(
                                out=pb[:],
                                lhsT=sb["wdb"][:, ci * 1024 + co * 128: ci * 1024 + (co + 1) * 128],
                                rhs=sb["hbuf"][:, ci * TL:(ci + 1) * TL],
                                start=(ci == 0), stop=False)
                        mms = [lambda E, pb, ci=ci: mm(E, pb, ci) for ci in range(FH)]
                        mms.append(lambda E, pb: E.matmul(
                            out=pb[:], lhsT=sb["bdr"][0:1, co * 128:(co + 1) * 128],
                            rhs=sb["c_ones_bf"][0:1, 0:TL], start=False, stop=True))
                        ev = lambda E, pb: E.tensor_tensor(
                            out=sb["cc"][:, co * TL:(co + 1) * TL], in0=pb[:],
                            in1=sb["bb"][:, co * TL:(co + 1) * TL], op=ALU.add)
                        return (mms, ev, "v")
                    groups.append(mkd2(co))
                emit_mm_pipe2(s, groups, [ps["pq"], ps["pu"]])
                s.wait("s", ["sv"])
                emit_ln2(s, ps, sb, sb["cc"][:], sb["bb"][:], sb["sq"][:], TL,
                         gcol=sb["g2c"], bcol=sb["b2c"])
                s.wait("v", ["ss", "sm"])
                for co in range(DCH):
                    s.op("v", lambda E, co=co: E.tensor_tensor(
                        out=sb["xf"][:, co * TL:(co + 1) * TL], in0=sb["bb"][:, co * TL:(co + 1) * TL],
                        in1=sb["xf"][:, co * TL:(co + 1) * TL], op=ALU.add))
                s.wait("g", ["sv"])
                s.op("g", lambda E, t0=t0: E.dma_start(out=fm_ap(hb["xo"], t0, t0 + TL), in_=sb3(sb["xf"][:], TL)))
                marks.append(s.cnt["sv"])
                marks_lg.append(s.cnt["lg"])
            s.wait_at("g", "lg", s.cnt["lg"])

    return body



def build_l1b(Tc=4096):
    use_consts = BASE_CONSTS
    nc = bass.Bass("TRN2", target_bir_lowering=False)
    TL = 128
    NT = Tc // TL
    FH = 22
    hb = {}
    hb["x"] = nc.dram_tensor("x", [D, Tc], F32, kind="ExternalInput").ap()
    hb["qp"] = nc.dram_tensor("qp", [D, Tc], BF16, kind="ExternalInput").ap()
    hb["kv0"] = nc.dram_tensor("kv0", [D, 1024], F32, kind="ExternalInput").ap()
    hb["kv1"] = nc.dram_tensor("kv1", [D, 1024], F32, kind="ExternalInput").ap()
    hb["ks0"] = nc.dram_tensor("ks0", [128, 8], F32, kind="ExternalInput").ap()
    hb["ks1"] = nc.dram_tensor("ks1", [128, 8], F32, kind="ExternalInput").ap()
    hb["wg"] = nc.dram_tensor("wg", [D, 2816], F32, kind="ExternalInput").ap()
    hb["wu"] = nc.dram_tensor("wu", [D, 2816], F32, kind="ExternalInput").ap()
    hb["wd"] = nc.dram_tensor("wd", [2816, D], F32, kind="ExternalInput").ap()
    hb["bg"] = nc.dram_tensor("bg", [1, 2816], F32, kind="ExternalInput").ap()
    hb["bu"] = nc.dram_tensor("bu", [1, 2816], F32, kind="ExternalInput").ap()
    hb["bd"] = nc.dram_tensor("bd", [1, 1024], F32, kind="ExternalInput").ap()
    hb["g1"] = nc.dram_tensor("g1", [128, 8], F32, kind="ExternalInput").ap()
    hb["b1"] = nc.dram_tensor("b1", [128, 8], F32, kind="ExternalInput").ap()
    hb["g2"] = nc.dram_tensor("g2", [128, 8], F32, kind="ExternalInput").ap()
    hb["b2"] = nc.dram_tensor("b2", [128, 8], F32, kind="ExternalInput").ap()
    for name, shp, dt in CONST_SPECS:
        if name in use_consts:
            hb[name] = nc.dram_tensor(name, shp, dt, kind="ExternalInput").ap()
    hb["xo"] = nc.dram_tensor("xo", [D, Tc], F32, kind="ExternalOutput").ap()

    body = setup_l1b(nc, hb, "", Tc)
    with contextlib.ExitStack() as c2:
        run_sched(nc, c2, body)
    return nc

def _pad_cols(a, n):
    out = np.zeros((a.shape[0], n), dtype=np.float32)
    out[:, :a.shape[1]] = a
    return out


def _pad_rows(a, n):
    out = np.zeros((n, a.shape[1]), dtype=np.float32)
    out[:a.shape[0], :] = a
    return out


def l1b_inmaps(ins, consts, qp_full, kv_parts, ks_parts):
    wg = _pad_cols(np.asarray(ins["L1_attn_Wg"]), 2816)
    wu = _pad_cols(np.asarray(ins["L1_attn_Wu"]), 2816)
    wd = _pad_rows(np.asarray(ins["L1_attn_Wd"]), 2816)
    bg = _pad_cols(host_row(ins["L1_attn_bg"]), 2816)
    bu = _pad_cols(host_row(ins["L1_attn_bu"]), 2816)
    consts = consts_subset(consts, BASE_CONSTS)
    maps = []
    for c in range(NCORE):
        b, h = c // 2, c % 2
        m = dict(consts)
        m["qp"] = qp_full[c]
        m["kv0"] = kv_parts[b * 2]
        m["kv1"] = kv_parts[b * 2 + 1]
        m["ks0"] = ks_parts[b * 2]
        m["ks1"] = ks_parts[b * 2 + 1]
        m["wg"] = wg; m["wu"] = wu; m["wd"] = wd
        m["bg"] = bg; m["bu"] = bu
        m["bd"] = host_row(ins["L1_attn_bd"])
        m["g1"] = host_col(ins["L1_attn_g1"]); m["b1"] = host_col(ins["L1_attn_b1"])
        m["g2"] = host_col(ins["L1_attn_g2"]); m["b2"] = host_col(ins["L1_attn_b2"])
        maps.append(m)
    return maps



def setup_mka(nc, hb, prefix="", mode="ln", Tc=8192):
    NT = Tc // TT
    def _apfn(key):
        v = hb.get(key)
        if callable(v):
            return v
        return (lambda t0, t1, v=v: fm_ap(v, t0, t1)) if v is not None else None
    get_x = _apfn("x") if mode == "ln" else None
    get_p0 = _apfn("p0")
    ctx = contextlib.ExitStack()
    with ctx:
        sb = {}

        def S(name, shape, dt=F32):
            sb[name] = ctx.enter_context(nc.sbuf_tensor("sb_" + prefix + name, shape, dt))
            return sb[name]

        for name, shp, dt in CONST_SPECS:
            if name in hb:
                S(name, shp, dt)
        S("wa", [128, DCH * 2048], BF16)
        S("xp", [128, DCH * 96], BF16)
        S("cw", [128, 32]); S("cb", [128, 8])
        S("wstg0", [128, 1024], F32); S("wstg1", [128, 1024], F32)
        S("x0", [128, DCH * TT])
        S("x1", [128, DCH * TT])
        S("xhat", [128, DCH * TT], BF16)
        S("sq", [128, DCH * TT], BF16)
        S("ext", [128, DCH * (TT + 3)], BF16)
        S("xc", [128, DCH * TT], BF16)
        S("sgz", [128, DCH * TT], BF16)
        S("cacc", [128, TT])
        S("row0", [1, TT]); S("row1", [1, TT])

        ps = {}
        for name, shape, dt in [("stat", [128, TT], F32), ("rep0", [128, TT], F32),
                                ("rep1", [128, TT], F32), ("pq", [128, TT], F32),
                                ("pdb", [80, TT], F32), ("pc", [16, TT], F32)]:
            ps[name] = ctx.enter_context(nc.psum_tensor(prefix + name, shape, dt))

        def body(s):
            emit_load_consts(s, hb, sb)
            for small in ("cw", "cb"):
                s.op("q", lambda E, d=sb[small], sr=hb[small]: E.dma_start(out=d[:], in_=sr))
            chunks, dsts = [], []
            for ci in range(DCH):
                for half in range(2):
                    chunks.append(hb["wa"][ci * 128:(ci + 1) * 128, half * 1024:(half + 1) * 1024])
                    dsts.append(sb["wa"][:, ci * 2048 + half * 1024: ci * 2048 + (half + 1) * 1024])
            for ci in range(DCH):
                chunks.append(hb["xpT"][ci * 128:(ci + 1) * 128, :])
                dsts.append(sb["xp"][:, ci * 96: ci * 96 + 80])
            for ci in range(DCH):
                chunks.append(hb["xpTc"][ci * 128:(ci + 1) * 128, :])
                dsts.append(sb["xp"][:, ci * 96 + 80: ci * 96 + 96])
            emit_cast_weight(s, chunks, [sb["wstg0"], sb["wstg1"]], dsts)
            s.wait("v", ["ld"])
            s.op("v", lambda E: E.memset(sb["ext"][:], 0.0))

            marks = []
            marks_lg = []
            for i in range(NT):
                t0 = i * TT
                if i >= 1:
                    s.wait_at("q", "sv", marks[i - 1])
                    s.wait_at("q", "lg", marks_lg[i - 1])
                if mode == "ln":
                    s.op("q", lambda E, t0=t0: E.dma_start(out=sb3(sb["x0"][:], TT), in_=get_x(t0, t0 + TT)))
                    s.wait("s", ["ld", "sm", "sv"])
                    s.wait("v", ["ld"])
                    emit_ln(s, ps, sb, sb["x0"][:], sb["xhat"][:], sb["sq"][:], TT)
                elif mode == "plain":
                    s.op("q", lambda E, t0=t0: E.dma_start(out=sb3(sb["x0"][:], TT), in_=get_p0(t0, t0 + TT)))
                    s.wait("v", ["ld", "sm", "ss"])
                    for c in range(DCH):
                        s.op("v", lambda E, c=c: E.tensor_copy(
                            out=sb["xhat"][:, c * TT:(c + 1) * TT], in_=sb["x0"][:, c * TT:(c + 1) * TT]))
                else:
                    s.op("q", lambda E, t0=t0: E.dma_start(out=sb3(sb["x0"][:], TT), in_=get_p0(t0, t0 + TT)))
                    s.op("q", lambda E, t0=t0: E.dma_start(out=sb3(sb["x1"][:], TT), in_=fm_ap(hb["p1"], t0, t0 + TT)))
                    s.wait("v", ["ld", "sm", "ss"])
                    for c in range(DCH):
                        s.op("v", lambda E, c=c: E.tensor_tensor(
                            out=sb["xhat"][:, c * TT:(c + 1) * TT], in0=sb["x0"][:, c * TT:(c + 1) * TT],
                            in1=sb["x1"][:, c * TT:(c + 1) * TT], op=ALU.add))
                s.wait("t", ["sv", "ss"])
                groups = []
                for co in range(16):
                    def mk(co):
                        def mm(E, pb, ci):
                            return E.matmul(
                                out=pb[:],
                                lhsT=sb["wa"][:, ci * 2048 + co * 128: ci * 2048 + (co + 1) * 128],
                                rhs=sb["xhat"][:, ci * TT:(ci + 1) * TT],
                                start=(ci == 0), stop=(ci == DCH - 1))
                        mms = [lambda E, pb, ci=ci: mm(E, pb, ci) for ci in range(DCH)]
                        if co < 8:
                            ev = lambda E, pb: E.tensor_copy(
                                out=sb["ext"][:, co * (TT + 3) + 3: (co + 1) * (TT + 3)], in_=pb[:])
                            return mms, ev, "v"
                        ev = lambda E, pb: E.activation(
                            out=sb["sgz"][:, (co - 8) * TT:(co - 7) * TT], in_=pb[:], func=AF.Silu)
                        return mms, ev, "s"
                    groups.append(mk(co))
                emit_mm_pipe2(s, groups, [ps["pq"], ps["rep0"]])
                s.wait("v", ["ss", "sm"])
                for c in range(DCH):
                    s.op("v", lambda E, c=c: E.tensor_scalar(
                        out=sb["cacc"][:], in0=sb["ext"][:, c * (TT + 3): c * (TT + 3) + TT],
                        scalar1=sb["cw"][:, c * 4: c * 4 + 1], scalar2=None, op0=ALU.mult))
                    for k in range(1, 4):
                        s.op("v", lambda E, c=c, k=k: E.scalar_tensor_tensor(
                            out=sb["cacc"][:], in0=sb["ext"][:, c * (TT + 3) + k: c * (TT + 3) + k + TT],
                            scalar=sb["cw"][:, c * 4 + k: c * 4 + k + 1], in1=sb["cacc"][:],
                            op0=ALU.mult, op1=ALU.add))
                    s.op("v", lambda E, c=c: E.tensor_scalar(
                        out=sb["cacc"][:], in0=sb["cacc"][:],
                        scalar1=sb["cb"][:, c:c + 1], scalar2=None, op0=ALU.add))
                    s.wait("s", ["sv"])
                    s.op("s", lambda E, c=c: E.activation(
                        out=sb["xc"][:, c * TT:(c + 1) * TT], in_=sb["cacc"][:], func=AF.Silu))
                    s.wait("v", ["ss"])
                for c in range(DCH):
                    s.op("v", lambda E, c=c: E.tensor_copy(
                        out=sb["ext"][:, c * (TT + 3): c * (TT + 3) + 3],
                        in_=sb["ext"][:, c * (TT + 3) + TT: c * (TT + 3) + TT + 3]))
                s.wait("t", ["ss", "sv"])
                for ci in range(DCH):
                    s.op("t", lambda E, ci=ci: E.matmul(
                        out=ps["pdb"][:], lhsT=sb["xp"][:, ci * 96: ci * 96 + 80],
                        rhs=sb["xc"][:, ci * TT:(ci + 1) * TT],
                        start=(ci == 0), stop=(ci == DCH - 1)))
                for ci in range(DCH):
                    s.op("t", lambda E, ci=ci: E.matmul(
                        out=ps["pc"][:], lhsT=sb["xp"][:, ci * 96 + 80: ci * 96 + 96],
                        rhs=sb["xc"][:, ci * TT:(ci + 1) * TT],
                        start=(ci == 0), stop=(ci == DCH - 1)))
                s.wait("v", ["sm"])
                s.op("v", lambda E: E.tensor_copy(out=sb["x0"][0:80, 0:TT], in_=ps["pdb"][:]))
                s.op("v", lambda E: E.tensor_copy(out=sb["x1"][0:16, 0:TT], in_=ps["pc"][:]))
                s.wait("g", ["sv", "ss"])
                s.op("g", lambda E, t0=t0: E.dma_start(out=fm_ap(hb["xc_o"], t0, t0 + TT), in_=sb3(sb["xc"][:], TT)))
                s.op("g", lambda E, t0=t0: E.dma_start(out=fm_ap(hb["sz_o"], t0, t0 + TT), in_=sb3(sb["sgz"][:], TT)))
                s.op("g", lambda E, t0=t0: E.dma_start(out=hb["db_o"][:, t0:t0 + TT], in_=sb["x0"][0:80, 0:TT]))
                s.op("g", lambda E, t0=t0: E.dma_start(out=hb["c_o"][:, t0:t0 + TT], in_=sb["x1"][0:16, 0:TT]))
                marks.append(s.cnt["sv"])
                marks_lg.append(s.cnt["lg"])
            s.wait_at("g", "lg", s.cnt["lg"])

    return body



def build_mka(mode, Tc=8192):
    use_consts = BASE_CONSTS
    nc = bass.Bass("TRN2", target_bir_lowering=False)
    NT = Tc // TT
    hb = {}
    if mode == "ln":
        hb["x"] = nc.dram_tensor("x", [D, Tc], F32, kind="ExternalInput").ap()
    else:
        hb["p0"] = nc.dram_tensor("p0", [D, Tc], F32, kind="ExternalInput").ap()
        hb["p1"] = nc.dram_tensor("p1", [D, Tc], F32, kind="ExternalInput").ap()
    hb["wa"] = nc.dram_tensor("wa", [D, 2048], F32, kind="ExternalInput").ap()
    hb["xpT"] = nc.dram_tensor("xpT", [D, 80], F32, kind="ExternalInput").ap()
    hb["xpTc"] = nc.dram_tensor("xpTc", [D, 16], F32, kind="ExternalInput").ap()
    hb["cw"] = nc.dram_tensor("cw", [128, 32], F32, kind="ExternalInput").ap()
    hb["cb"] = nc.dram_tensor("cb", [128, 8], F32, kind="ExternalInput").ap()
    for name, shp, dt in CONST_SPECS:
        if name in use_consts:
            hb[name] = nc.dram_tensor(name, shp, dt, kind="ExternalInput").ap()
    hb["xc_o"] = nc.dram_tensor("xc_o", [D, Tc], BF16, kind="ExternalOutput").ap()
    hb["sz_o"] = nc.dram_tensor("sz_o", [D, Tc], BF16, kind="ExternalOutput").ap()
    hb["db_o"] = nc.dram_tensor("db_o", [80, Tc], F32, kind="ExternalOutput").ap()
    hb["c_o"] = nc.dram_tensor("c_o", [16, Tc], F32, kind="ExternalOutput").ap()

    body = setup_mka(nc, hb, "", mode, Tc)
    with contextlib.ExitStack() as c2:
        run_sched(nc, c2, body)
    return nc

def mka_inmaps(consts, in_w, xp_w, conv_w, conv_b, x_full=None, parts=None):
    consts = consts_subset(consts, BASE_CONSTS)
    maps = []
    in_w = np.asarray(in_w); xp_w = np.asarray(xp_w)
    conv_w = np.asarray(conv_w); conv_b = np.asarray(conv_b)
    for c in range(NCORE):
        b, eh = c // 2, c % 2
        m = dict(consts)
        rows = np.r_[eh * 1024:(eh + 1) * 1024, 2048 + eh * 1024: 2048 + (eh + 1) * 1024]
        m["wa"] = np.ascontiguousarray(in_w[rows, :].T)
        xs = xp_w[:, eh * 1024:(eh + 1) * 1024]
        m["xpT"] = np.ascontiguousarray(xs[0:80, :].T)
        m["xpTc"] = np.ascontiguousarray(xs[80:96, :].T)
        cw = conv_w[eh * 1024:(eh + 1) * 1024, :]
        m["cw"] = np.ascontiguousarray(cw.reshape(8, 128, 4).transpose(1, 0, 2).reshape(128, 32))
        m["cb"] = host_col(conv_b[eh * 1024:(eh + 1) * 1024])
        if x_full is not None:
            m["x"] = host_fm(x_full[b])
        else:
            m["p0"], m["p1"] = parts[c]
        maps.append(m)
    return maps



def setup_mkb(nc, hb, prefix="", momentum=False, Tc=8192, single=False):
    NT = Tc // TT
    ov = hb["op_o"]
    get_out = ov if callable(ov) else (lambda t0, t1, v=ov: fm_ap(v, t0, t1))
    ctx = contextlib.ExitStack()
    with ctx:
        sb = {}

        def S(name, shape, dt=F32):
            sb[name] = ctx.enter_context(nc.sbuf_tensor("sb_" + prefix + name, shape, dt))
            return sb[name]

        for name, shp, dt in CONST_SPECS:
            if name in hb:
                S(name, shp, dt)
        S("ow", [128, DCH * 1024], BF16)
        S("dtw", [64, 1024], BF16)
        S("dtb_bf", [1, 1024], BF16)
        S("acol", [128, 128])
        S("dcol", [128, 8])
        S("wstg0", [128, 1024], F32); S("wstg1", [128, 1024], F32)
        S("xct", [128, DCH * TT], BF16)
        S("szt", [128, DCH * TT], BF16)
        S("du", [128, DCH * TT], BF16)
        S("delta", [128, DCH * TT])
        S("db_a", [80, TT]); S("db_b", [80, TT])
        S("ct_a", [16, TT]); S("ct_b", [16, TT])
        S("dtc", [64, TT], BF16)
        S("bt", [16, TT], BF16); S("ctb", [16, TT], BF16)
        S("Bb", [128, 16 * TT], BF16)
        S("Cb", [128, 16 * TT], BF16)
        S("dab", [128, TT], BF16)
        S("qb", [128, TT], BF16)
        S("hb", [128, TT], BF16)
        S("vb", [128, TT], BF16)
        S("yacc", [128, DCH * TT], BF16)
        S("ybuf", [128, DCH * TT], BF16)
        S("ot", [128, DCH * TT])
        S("hcar", [128, 128]); S("vcar", [128, 128])
        S("spa", [128, TT]); S("spb", [128, TT])
        S("beta", [128, TT], BF16)

        ps = {}
        for name, shape, dt in [("pq", [128, TT], F32), ("prep", [128, TT], F32)]:
            ps[name] = ctx.enter_context(nc.psum_tensor(prefix + name, shape, dt))

        def body(s):
            emit_load_consts(s, hb, sb)
            s.op("q", lambda E: E.dma_start(out=sb["dcol"][:], in_=hb["dcol"]))
            s.op("q", lambda E: E.dma_start(out=sb["wstg0"][0:64, :], in_=hb["dtwT"]))
            s.op("q", lambda E: E.dma_start(out=sb["wstg1"][0:1, :], in_=hb["dtb"]))
            s.wait("v", ["ld"])
            s.op("v", lambda E: E.tensor_copy(out=sb["dtw"][:], in_=sb["wstg0"][0:64, :]))
            s.op("v", lambda E: E.tensor_copy(out=sb["dtb_bf"][:], in_=sb["wstg1"][0:1, :]))
            s.op("v", lambda E: E.memset(sb["hcar"][:], 0.0))
            s.op("v", lambda E: E.memset(sb["vcar"][:], 0.0))
            s.op("v", lambda E: E.memset(sb["beta"][:], 0.6))
            s.wait("q", ["sv"])
            s.op("q", lambda E: E.dma_start(out=sb["wstg0"][:, 0:128], in_=hb["alog"]))
            s.wait("s", ["ld"])
            s.op("s", lambda E: E.activation(out=sb["acol"][:], in_=sb["wstg0"][:, 0:128], func=AF.Exp))
            s.wait("v", ["ss"])
            s.op("v", lambda E: E.tensor_scalar(out=sb["acol"][:], in0=sb["acol"][:],
                                                scalar1=-1.0, scalar2=None, op0=ALU.mult))
            chunks, dsts = [], []
            for ci in range(DCH):
                chunks.append(hb["owT"][ci * 128:(ci + 1) * 128, :])
                dsts.append(sb["ow"][:, ci * 1024:(ci + 1) * 1024])
            emit_cast_weight(s, chunks, [sb["wstg0"], sb["wstg1"]], dsts)

            cv = sb["c_cvec"]
            marks = []
            marks_lg = []
            for i in range(NT):
                t0 = i * TT
                if i >= 1:
                    s.wait_at("q", "sv", marks[i - 1])
                    s.wait_at("q", "lg", marks_lg[i - 1])
                s.op("q", lambda E, t0=t0: E.dma_start(out=sb3(sb["xct"][:], TT), in_=fm_ap(hb["xc"], t0, t0 + TT)))
                s.op("q", lambda E, t0=t0: E.dma_start(out=sb3(sb["szt"][:], TT), in_=fm_ap(hb["sgz"], t0, t0 + TT)))
                s.op("q", lambda E, t0=t0: E.dma_start(out=sb["db_a"][:], in_=hb["db0"][:, t0:t0 + TT]))
                s.op("q", lambda E, t0=t0: E.dma_start(out=sb["ct_a"][:], in_=hb["c0"][:, t0:t0 + TT]))
                if not single:
                    s.op("q", lambda E, t0=t0: E.dma_start(out=sb["db_b"][:], in_=hb["db1"][:, t0:t0 + TT]))
                    s.op("q", lambda E, t0=t0: E.dma_start(out=sb["ct_b"][:], in_=hb["c1"][:, t0:t0 + TT]))
                s.wait("v", ["ld", "ss", "sm"])
                if not single:
                    s.op("v", lambda E: E.tensor_tensor(out=sb["db_a"][:], in0=sb["db_a"][:], in1=sb["db_b"][:], op=ALU.add))
                    s.op("v", lambda E: E.tensor_tensor(out=sb["ct_a"][:], in0=sb["ct_a"][:], in1=sb["ct_b"][:], op=ALU.add))
                s.op("v", lambda E: E.tensor_copy(out=sb["dtc"][:], in_=sb["db_a"][0:64, :]))
                s.op("v", lambda E: E.tensor_copy(out=sb["bt"][:], in_=sb["db_a"][64:80, :]))
                s.op("v", lambda E: E.tensor_copy(out=sb["ctb"][:], in_=sb["ct_a"][:]))
                for co in range(DCH):
                    s.wait("t", ["sv", "ss"])
                    s.op("t", lambda E, co=co: E.matmul(
                        out=ps["pq"][:], lhsT=sb["dtw"][:, co * 128:(co + 1) * 128],
                        rhs=sb["dtc"][:], start=True, stop=False))
                    s.op("t", lambda E, co=co: E.matmul(
                        out=ps["pq"][:], lhsT=sb["dtb_bf"][0:1, co * 128:(co + 1) * 128],
                        rhs=sb["c_ones_bf"][0:1, :], start=False, stop=True))
                    s.wait("v", ["sm"])
                    s.op("v", lambda E: E.tensor_scalar(out=sb["spa"][:], in0=ps["pq"][:],
                                                        scalar1=0.0, scalar2=None, op0=ALU.min))
                    s.op("v", lambda E: E.tensor_scalar(out=sb["spb"][:], in0=ps["pq"][:],
                                                        scalar1=0.0, scalar2=None, op0=ALU.max))
                    s.op("v", lambda E: E.tensor_tensor(out=sb["spa"][:], in0=sb["spa"][:],
                                                        in1=sb["spb"][:], op=ALU.subtract))
                    s.wait("s", ["sv"])
                    s.op("s", lambda E: E.activation(out=sb["spa"][:], in_=sb["spa"][:], func=AF.Exp))
                    s.op("s", lambda E: E.activation(out=sb["spa"][:], in_=sb["spa"][:],
                                                     func=AF.Ln, bias=cv[0:128, 2:3], scale=cv[0:128, 2:3]))
                    s.wait("v", ["ss"])
                    s.op("v", lambda E, co=co: E.tensor_tensor(out=sb["delta"][:, co * TT:(co + 1) * TT],
                                                               in0=sb["spa"][:], in1=sb["spb"][:], op=ALU.add))
                for c in range(DCH):
                    s.op("v", lambda E, c=c: E.tensor_tensor(
                        out=sb["du"][:, c * TT:(c + 1) * TT], in0=sb["delta"][:, c * TT:(c + 1) * TT],
                        in1=sb["xct"][:, c * TT:(c + 1) * TT], op=ALU.mult))
                for n in range(16):
                    s.wait("t", ["sv"])
                    s.op("t", lambda E, n=n: E.matmul(
                        out=ps["prep"][:], lhsT=sb["c_hot16"][:, n * 128:(n + 1) * 128],
                        rhs=sb["bt"][:], start=True, stop=True))
                    s.wait("v", ["sm"])
                    s.op("v", lambda E, n=n: E.tensor_copy(out=sb["Bb"][:, n * TT:(n + 1) * TT], in_=ps["prep"][:]))
                    s.wait("t", ["sv"])
                    s.op("t", lambda E, n=n: E.matmul(
                        out=ps["prep"][:], lhsT=sb["c_hot16"][:, n * 128:(n + 1) * 128],
                        rhs=sb["ctb"][:], start=True, stop=True))
                    s.wait("v", ["sm"])
                    s.op("v", lambda E, n=n: E.tensor_copy(out=sb["Cb"][:, n * TT:(n + 1) * TT], in_=ps["prep"][:]))
                s.op("v", lambda E: E.memset(sb["yacc"][:], 0.0))
                for n in range(16):
                    for ec in range(DCH):
                        col = ec * 16 + n
                        s.wait("s", ["sv"])
                        s.op("s", lambda E, ec=ec, col=col: E.activation(
                            out=sb["dab"][:], in_=sb["delta"][:, ec * TT:(ec + 1) * TT],
                            func=AF.Exp, bias=cv[:, 3:4], scale=sb["acol"][:, col:col + 1]))
                        s.wait("v", ["ss"])
                        s.op("v", lambda E, ec=ec, n=n: E.tensor_tensor(
                            out=sb["qb"][:], in0=sb["du"][:, ec * TT:(ec + 1) * TT],
                            in1=sb["Bb"][:, n * TT:(n + 1) * TT], op=ALU.mult))
                        if momentum:
                            s.op("v", lambda E, col=col: E.tensor_tensor_scan(
                                out=sb["vb"][:], data0=sb["beta"][:], data1=sb["qb"][:],
                                initial=sb["vcar"][:, col:col + 1], op0=ALU.mult, op1=ALU.add))
                            s.op("v", lambda E, col=col: E.tensor_copy(
                                out=sb["vcar"][:, col:col + 1], in_=sb["vb"][:, TT - 1:TT]))
                            s.op("v", lambda E, col=col: E.tensor_tensor_scan(
                                out=sb["hb"][:], data0=sb["dab"][:], data1=sb["vb"][:],
                                initial=sb["hcar"][:, col:col + 1], op0=ALU.mult, op1=ALU.add))
                        else:
                            s.op("v", lambda E, col=col: E.tensor_tensor_scan(
                                out=sb["hb"][:], data0=sb["dab"][:], data1=sb["qb"][:],
                                initial=sb["hcar"][:, col:col + 1], op0=ALU.mult, op1=ALU.add))
                        s.op("v", lambda E, col=col: E.tensor_copy(
                            out=sb["hcar"][:, col:col + 1], in_=sb["hb"][:, TT - 1:TT]))
                        s.op("v", lambda E, n=n: E.tensor_tensor(
                            out=sb["hb"][:], in0=sb["hb"][:],
                            in1=sb["Cb"][:, n * TT:(n + 1) * TT], op=ALU.mult))
                        s.op("v", lambda E, ec=ec: E.tensor_tensor(
                            out=sb["yacc"][:, ec * TT:(ec + 1) * TT], in0=sb["hb"][:],
                            in1=sb["yacc"][:, ec * TT:(ec + 1) * TT], op=ALU.add))
                for c in range(DCH):
                    s.op("v", lambda E, c=c: E.scalar_tensor_tensor(
                        out=sb["ybuf"][:, c * TT:(c + 1) * TT], in0=sb["xct"][:, c * TT:(c + 1) * TT],
                        scalar=sb["dcol"][:, c:c + 1], in1=sb["yacc"][:, c * TT:(c + 1) * TT],
                        op0=ALU.mult, op1=ALU.add))
                    s.op("v", lambda E, c=c: E.tensor_tensor(
                        out=sb["ybuf"][:, c * TT:(c + 1) * TT], in0=sb["ybuf"][:, c * TT:(c + 1) * TT],
                        in1=sb["szt"][:, c * TT:(c + 1) * TT], op=ALU.mult))
                for co in range(DCH):
                    s.wait("t", ["sv", "ss"])
                    for ci in range(DCH):
                        s.op("t", lambda E, ci=ci, co=co: E.matmul(
                            out=ps["pq"][:],
                            lhsT=sb["ow"][:, ci * 1024 + co * 128: ci * 1024 + (co + 1) * 128],
                            rhs=sb["ybuf"][:, ci * TT:(ci + 1) * TT],
                            start=(ci == 0), stop=(ci == DCH - 1)))
                    s.wait("v", ["sm"])
                    s.op("v", lambda E, co=co: E.tensor_copy(out=sb["ot"][:, co * TT:(co + 1) * TT],
                                                             in_=ps["pq"][:]))
                s.wait("g", ["sv"])
                s.op("g", lambda E, t0=t0: E.dma_start(out=get_out(t0, t0 + TT), in_=sb3(sb["ot"][:], TT)))
                marks.append(s.cnt["sv"])
                marks_lg.append(s.cnt["lg"])
            s.wait_at("g", "lg", s.cnt["lg"])

    return body



def build_mkb(momentum, Tc=8192):
    use_consts = BASE_CONSTS + ["c_hot16"]
    nc = bass.Bass("TRN2", target_bir_lowering=False)
    NT = Tc // TT
    hb = {}
    hb["xc"] = nc.dram_tensor("xc", [D, Tc], BF16, kind="ExternalInput").ap()
    hb["sgz"] = nc.dram_tensor("sgz", [D, Tc], BF16, kind="ExternalInput").ap()
    hb["db0"] = nc.dram_tensor("db0", [80, Tc], F32, kind="ExternalInput").ap()
    hb["db1"] = nc.dram_tensor("db1", [80, Tc], F32, kind="ExternalInput").ap()
    hb["c0"] = nc.dram_tensor("c0", [16, Tc], F32, kind="ExternalInput").ap()
    hb["c1"] = nc.dram_tensor("c1", [16, Tc], F32, kind="ExternalInput").ap()
    hb["dtwT"] = nc.dram_tensor("dtwT", [64, 1024], F32, kind="ExternalInput").ap()
    hb["dtb"] = nc.dram_tensor("dtb", [1, 1024], F32, kind="ExternalInput").ap()
    hb["alog"] = nc.dram_tensor("alog", [128, 128], F32, kind="ExternalInput").ap()
    hb["dcol"] = nc.dram_tensor("dcol", [128, 8], F32, kind="ExternalInput").ap()
    hb["owT"] = nc.dram_tensor("owT", [1024, 1024], F32, kind="ExternalInput").ap()
    for name, shp, dt in CONST_SPECS:
        if name in use_consts:
            hb[name] = nc.dram_tensor(name, shp, dt, kind="ExternalInput").ap()
    hb["op_o"] = nc.dram_tensor("op_o", [D, Tc], F32, kind="ExternalOutput").ap()

    body = setup_mkb(nc, hb, "", momentum, Tc)
    with contextlib.ExitStack() as c2:
        run_sched(nc, c2, body)
    return nc

def mkb_inmaps(consts, dt_proj_w, dt_proj_b, A_log, Dv, out_w, xc_full, sgz_full, db_parts, c_parts):
    consts = consts_subset(consts, BASE_CONSTS + ["c_hot16"])
    dt_proj_w = np.asarray(dt_proj_w); A_log = np.asarray(A_log)
    out_w = np.asarray(out_w)
    maps = []
    for c in range(NCORE):
        b, eh = c // 2, c % 2
        m = dict(consts)
        sl = slice(eh * 1024, (eh + 1) * 1024)
        m["dtwT"] = np.ascontiguousarray(dt_proj_w[sl, :].T)
        m["dtb"] = host_row(np.asarray(dt_proj_b)[sl])
        m["alog"] = np.ascontiguousarray(A_log[sl, :].reshape(8, 128, 16).transpose(1, 0, 2).reshape(128, 128))
        m["dcol"] = host_col(np.asarray(Dv)[sl])
        m["owT"] = np.ascontiguousarray(out_w[:, sl].T)
        m["xc"] = xc_full[c]
        m["sgz"] = sgz_full[c]
        m["db0"], m["db1"] = db_parts[b * 2], db_parts[b * 2 + 1]
        m["c0"], m["c1"] = c_parts[b * 2], c_parts[b * 2 + 1]
        maps.append(m)
    return maps



def setup_mkb2(nc, hb, prefix="", momentum=False, Tc=8192):
    NT = Tc // TT
    ov = hb["op_o"]
    get_out = ov if callable(ov) else (lambda t0, t1, v=ov: fm_ap(v, t0, t1))
    ctx = contextlib.ExitStack()
    with ctx:
        sb = {}

        def S(name, shape, dt=F32):
            sb[name] = ctx.enter_context(nc.sbuf_tensor("sb_" + prefix + name, shape, dt))
            return sb[name]

        for name, shp, dt in CONST_SPECS:
            if name in hb:
                S(name, shp, dt)
        for g in range(10):
            S(f"g{g}", [128, Tc], BF16)
        S("ow", [128, DCH * 1024], BF16)
        S("dtw", [64, 1024], BF16)
        S("dtb_bf", [1, 1024], BF16)
        S("acol", [128, 128])
        S("dcol", [128, 8])
        S("wstg0", [128, 1024], F32)
        S("wstg1", [128, 1024], F32)
        S("db_a", [80, TT])
        S("ct_a", [16, TT])
        S("dtc", [64, TT], BF16)
        S("btbf", [16, TT], BF16); S("ctbf", [16, TT], BF16)
        S("spa", [128, TT]); S("spb", [128, TT])
        S("z0", [128, 1])

        ps = {}
        ps["pq"] = ctx.enter_context(nc.psum_tensor(prefix + "pq", [128, TT], F32))
        ps["pq2"] = ctx.enter_context(nc.psum_tensor(prefix + "pq2", [128, TT], F32))

        def body(s):
            emit_load_consts(s, hb, sb)
            s.op("q", lambda E: E.dma_start(out=sb["dcol"][:], in_=hb["dcol"]))
            s.op("q", lambda E: E.dma_start(out=sb["wstg0"][0:64, :], in_=hb["dtwT"]))
            s.wait("v", ["ld"])
            s.op("v", lambda E: E.tensor_copy(out=sb["dtw"][:], in_=sb["wstg0"][0:64, :]))
            s.op("v", lambda E: E.memset(sb["z0"][:], 0.0))
            s.wait("q", ["sv"])
            s.op("q", lambda E: E.dma_start(out=sb["wstg0"][0:1, :], in_=hb["dtb"]))
            s.wait("v", ["ld"])
            s.op("v", lambda E: E.tensor_copy(out=sb["dtb_bf"][:], in_=sb["wstg0"][0:1, :]))
            s.wait("q", ["sv"])
            s.op("q", lambda E: E.dma_start(out=sb["wstg0"][:, 0:128], in_=hb["alog"]))
            s.wait("s", ["ld"])
            s.op("s", lambda E: E.activation(out=sb["acol"][:], in_=sb["wstg0"][:, 0:128], func=AF.Exp))
            s.wait("v", ["ss"])
            s.op("v", lambda E: E.tensor_scalar(out=sb["acol"][:], in0=sb["acol"][:],
                                                scalar1=-1.0, scalar2=None, op0=ALU.mult))
            chunks = [hb["owT"][ci * 128:(ci + 1) * 128, :] for ci in range(DCH)]
            dsts = [sb["ow"][:, ci * 1024:(ci + 1) * 1024] for ci in range(DCH)]
            emit_cast_weight(s, chunks, [sb["wstg0"], sb["wstg1"]], dsts)
            cv = sb["c_cvec"]

            for i in range(NT):
                t0 = i * TT
                s.wait("q", ["sv", "ss"])
                s.op("q", lambda E, t0=t0: E.dma_start(out=sb["db_a"][:], in_=hb["db0"][:, t0:t0 + TT]))
                s.op("q", lambda E, t0=t0: E.dma_start(out=sb["ct_a"][:], in_=hb["c0"][:, t0:t0 + TT]))
                s.op("q", lambda E, t0=t0: E.dma_start(out=sb3(sb["g0"][:, 0:DCH * TT], TT),
                                                       in_=fm_ap(hb["xc"], t0, t0 + TT)))
                s.wait("v", ["ld", "sm"])
                s.op("v", lambda E: E.tensor_copy(out=sb["dtc"][:], in_=sb["db_a"][0:64, :]))
                s.op("v", lambda E: E.tensor_copy(out=sb["btbf"][:], in_=sb["db_a"][64:80, :]))
                s.op("v", lambda E: E.tensor_copy(out=sb["ctbf"][:], in_=sb["ct_a"][:]))
                s.wait("g", ["sv"])
                s.op("g", lambda E, t0=t0: E.dma_start(out=hb["bts"][:, t0:t0 + TT], in_=sb["btbf"][:]))
                s.op("g", lambda E, t0=t0: E.dma_start(out=hb["cts"][:, t0:t0 + TT], in_=sb["ctbf"][:]))
                for co in range(DCH):
                    s.wait("t", ["sv", "ss"])
                    s.op("t", lambda E, co=co: E.matmul(
                        out=ps["pq"][:], lhsT=sb["dtw"][:, co * 128:(co + 1) * 128],
                        rhs=sb["dtc"][:], start=True, stop=False))
                    s.op("t", lambda E, co=co: E.matmul(
                        out=ps["pq"][:], lhsT=sb["dtb_bf"][0:1, co * 128:(co + 1) * 128],
                        rhs=sb["c_ones_bf"][0:1, :], start=False, stop=True))
                    s.wait("v", ["sm"])
                    s.op("v", lambda E: E.tensor_scalar(out=sb["spa"][:], in0=ps["pq"][:],
                                                        scalar1=0.0, scalar2=None, op0=ALU.min))
                    s.op("v", lambda E: E.tensor_scalar(out=sb["spb"][:], in0=ps["pq"][:],
                                                        scalar1=0.0, scalar2=None, op0=ALU.max))
                    s.op("v", lambda E: E.tensor_tensor(out=sb["spa"][:], in0=sb["spa"][:],
                                                        in1=sb["spb"][:], op=ALU.subtract))
                    s.wait("s", ["sv"])
                    s.op("s", lambda E: E.activation(out=sb["spa"][:], in_=sb["spa"][:], func=AF.Exp))
                    s.op("s", lambda E: E.activation(out=sb["spa"][:], in_=sb["spa"][:],
                                                     func=AF.Ln, bias=cv[0:128, 2:3], scale=cv[0:128, 2:3]))
                    s.wait("v", ["ss"])
                    s.op("v", lambda E: E.tensor_tensor(out=sb["spa"][:], in0=sb["spa"][:],
                                                        in1=sb["spb"][:], op=ALU.add))
                    s.op("v", lambda E, co=co: E.tensor_copy(out=sb["g1"][:, 0:TT], in_=sb["spa"][:]))
                    s.op("v", lambda E, co=co: E.tensor_tensor(
                        out=sb["g2"][:, 0:TT], in0=sb["g1"][:, 0:TT],
                        in1=sb["g0"][:, co * TT:(co + 1) * TT], op=ALU.mult))
                    s.wait("g", ["sv"])
                    s.op("g", lambda E, co=co, t0=t0: E.dma_start(
                        out=hb["dls"][co * 128:(co + 1) * 128, t0:t0 + TT], in_=sb["g1"][:, 0:TT]))
                    s.op("g", lambda E, co=co, t0=t0: E.dma_start(
                        out=hb["dus"][co * 128:(co + 1) * 128, t0:t0 + TT], in_=sb["g2"][:, 0:TT]))

            if momentum:
                s.wait("v", ["lg", "ld"])
                s.op("v", lambda E: E.memset(sb["g9"][:], 0.6))
            for ec in range(DCH):
                s.wait("q", ["sv", "ss", "lg"])
                s.op("q", lambda E, ec=ec: E.dma_start(out=sb["g0"][:], in_=hb["dls"][ec * 128:(ec + 1) * 128, :]))
                s.op("q", lambda E, ec=ec: E.dma_start(out=sb["g1"][:], in_=hb["dus"][ec * 128:(ec + 1) * 128, :]))
                s.wait("v", ["ld"])
                s.op("v", lambda E: E.memset(sb["g8"][:], 0.0))
                for n in range(16):
                    col = ec * 16 + n
                    s.wait("q", ["sv", "ss"])
                    s.op("q", lambda E, n=n: E.dma_start(out=sb["g6"][:], in_=hb["bts"][n:n + 1, :].broadcast_to((128, Tc))))
                    s.op("q", lambda E, n=n: E.dma_start(out=sb["g7"][:], in_=hb["cts"][n:n + 1, :].broadcast_to((128, Tc))))
                    s.wait("s", ["ld", "sv"])
                    s.op("s", lambda E, col=col: E.activation(
                        out=sb["g2"][:], in_=sb["g0"][:], func=AF.Exp,
                        bias=cv[:, 3:4], scale=sb["acol"][:, col:col + 1]))
                    s.wait("v", ["ss", "ld"])
                    s.op("v", lambda E: E.tensor_tensor(out=sb["g3"][:], in0=sb["g1"][:], in1=sb["g6"][:], op=ALU.mult))
                    if momentum:
                        s.op("v", lambda E: E.tensor_tensor_scan(
                            out=sb["g5"][:], data0=sb["g9"][:], data1=sb["g3"][:],
                            initial=sb["z0"][:], op0=ALU.mult, op1=ALU.add))
                        s.op("v", lambda E: E.tensor_tensor_scan(
                            out=sb["g4"][:], data0=sb["g2"][:], data1=sb["g5"][:],
                            initial=sb["z0"][:], op0=ALU.mult, op1=ALU.add))
                    else:
                        s.op("v", lambda E: E.tensor_tensor_scan(
                            out=sb["g4"][:], data0=sb["g2"][:], data1=sb["g3"][:],
                            initial=sb["z0"][:], op0=ALU.mult, op1=ALU.add))
                    s.op("v", lambda E: E.tensor_tensor(out=sb["g4"][:], in0=sb["g4"][:], in1=sb["g7"][:], op=ALU.mult))
                    s.op("v", lambda E: E.tensor_tensor(out=sb["g8"][:], in0=sb["g8"][:], in1=sb["g4"][:], op=ALU.add))
                s.wait("q", ["sv"])
                s.op("q", lambda E, ec=ec: E.dma_start(out=sb["g3"][:], in_=fm_ec(hb["xc"], ec, Tc)))
                s.op("q", lambda E, ec=ec: E.dma_start(out=sb["g4"][:], in_=fm_ec(hb["sgz"], ec, Tc)))
                s.wait("v", ["ld"])
                s.op("v", lambda E, ec=ec: E.scalar_tensor_tensor(
                    out=sb["g5"][:], in0=sb["g3"][:], scalar=sb["dcol"][:, ec:ec + 1],
                    in1=sb["g8"][:], op0=ALU.mult, op1=ALU.add))
                s.op("v", lambda E: E.tensor_tensor(out=sb["g5"][:], in0=sb["g5"][:], in1=sb["g4"][:], op=ALU.mult))
                s.wait("g", ["sv"])
                s.op("g", lambda E, ec=ec: E.dma_start(out=hb["ybs"][ec * 128:(ec + 1) * 128, :], in_=sb["g5"][:]))

            for i in range(NT):
                t0 = i * TT
                s.wait("q", ["sm", "lg", "sv", "ss"])
                s.op("q", lambda E, t0=t0: E.dma_start(out=sb3(sb["g0"][:, 0:DCH * TT], TT),
                                                       in_=fm_ap(hb["ybs"], t0, t0 + TT)))
                otv = sb["g1"][:].bitcast(F32)
                s.wait("t", ["sv", "ss", "ld"])
                groups = []
                for co in range(DCH):
                    def mko(co):
                        def mm(E, pb, ci):
                            return E.matmul(
                                out=pb[:],
                                lhsT=sb["ow"][:, ci * 1024 + co * 128: ci * 1024 + (co + 1) * 128],
                                rhs=sb["g0"][:, ci * TT:(ci + 1) * TT],
                                start=(ci == 0), stop=(ci == DCH - 1))
                        mms = [lambda E, pb, ci=ci: mm(E, pb, ci) for ci in range(DCH)]
                        ev = lambda E, pb: E.tensor_copy(out=otv[:, co * TT:(co + 1) * TT], in_=pb[:])
                        return (mms, ev, "v")
                    groups.append(mko(co))
                emit_mm_pipe2(s, groups, [ps["pq"], ps["pq2"]])
                s.wait("g", ["sv"])
                s.op("g", lambda E, t0=t0, otv=otv: E.dma_start(out=get_out(t0, t0 + TT), in_=sb3(otv, TT)))
            s.wait_at("g", "lg", s.cnt["lg"])
    return body


def fm_ec(hbm_ap, ec, Tc):
    return hbm_ap[ec * 128:(ec + 1) * 128, :]




RG_PAIRS = [[0, 1], [2, 3], [4, 5], [6, 7]]


def fused_build():
    nc = bass.Bass("TRN2", target_bir_lowering=False)
    Tc = 4096
    Tf = 8192
    hb = {}

    def inp(name, shape, dt=F32):
        hb[name] = nc.dram_tensor(name, shape, dt, kind="ExternalInput").ap()
        return hb[name]

    def scratch(name, shape, dt=F32):
        hb[name] = nc.dram_tensor(name, shape, dt).ap()
        return hb[name]

    for name, shp, dt in CONST_SPECS:
        inp(name, shp, dt)
    inp("x_in", [D, Tc], F32)
    inp("summ_in", [D, 32], F32)
    for w in ("l0_wq", "l0_wk", "l0_wv", "l0_wc"):
        inp(w, [D, D], F32)
    for b in ("l0_bq", "l0_bk", "l0_bv", "l0_bc"):
        inp(b, [1, D], F32)
    inp("f0_w1i", [4096, 1025]); inp("f0_w2i", [4096, 1024])
    inp("f0_stmU", [12 * 32 * 128, 128]); inp("f0_stmD", [12 * 32 * 128, 128])
    inp("f0_ubias", [4, 1024]); inp("f0_bcon", [1, D]); inp("f0_dbias", [1, D])
    inp("l1_wqkv", [D, 3072]); inp("l1_bqkv", [1, 3072])
    inp("l1_wg", [D, 2816]); inp("l1_wu", [D, 2816]); inp("l1_wd", [2816, D])
    inp("l1_bg", [1, 2816]); inp("l1_bu", [1, 2816]); inp("l1_bd", [1, 1024])
    inp("l1_g1", [128, 8]); inp("l1_b1", [128, 8]); inp("l1_g2", [128, 8]); inp("l1_b2", [128, 8])
    inp("f1_wgT", [D, 3072], I32); inp("f1_wuT", [D, 3072], I32); inp("f1_wdT", [3072, D], I32)
    for m in range(5):
        inp(f"m{m}_wa", [D, 2048]); inp(f"m{m}_xpT", [D, 80]); inp(f"m{m}_xpTc", [D, 16])
        inp(f"m{m}_cw", [128, 32]); inp(f"m{m}_cb", [128, 8])
        inp(f"m{m}_dtwT", [64, 1024]); inp(f"m{m}_dtb", [1, 1024])
        inp(f"m{m}_alog", [128, 128]); inp(f"m{m}_dcol", [128, 8]); inp(f"m{m}_owT", [1024, 1024])
    inp("f2_wrT", [D, D]); inp("f2_wkT", [D, D]); inp("f2_wvT", [D, D])
    inp("f2_mur", [128, 8]); inp("f2_muk", [128, 8]); inp("f2_hflag", [128, 8])
    inp("f3_gwT", [D, 8]); inp("f3_ewT", [8 * D, D])
    out_ap = nc.dram_tensor("out", [D, Tc], F32, kind="ExternalOutput").ap()

    scratch("x_a", [D, Tc]); scratch("x_b", [D, Tc])
    scratch("qp_s", [D, Tc], BF16)
    scratch("kv_loc", [D, 1024]); scratch("kv_sum", [D, 1024])
    scratch("ks_loc", [128, 8]); scratch("ks_sum", [128, 8])
    scratch("xg_a", [2 * D, Tc]); scratch("xg_b", [2 * D, Tc])
    scratch("xc_s", [D, Tf], BF16); scratch("sgz_s", [D, Tf], BF16)
    scratch("dbc_loc", [96, Tf]); scratch("dbc_sum", [96, Tf])
    scratch("op_loc", [2 * D, Tc]); scratch("op_sum", [2 * D, Tc])
    scratch("dls", [D, Tf], BF16); scratch("dus", [D, Tf], BF16)
    scratch("bts", [16, Tf], BF16); scratch("cts", [16, Tf], BF16)
    scratch("ybs", [D, Tf], BF16)
    scratch("op_rs", [D, Tc]); scratch("op_rs3", [D, Tc])
    scratch("halo_loc", [128, 8]); scratch("halo_ag", [256, 8])
    scratch("f0scr", [4096, 1025], BF16)

    def split_view(ap):
        def f(t0, t1):
            if t1 <= Tc:
                return fm_ap(ap[0:D, :], t0, t1)
            return fm_ap(ap[D:2 * D, :], t0 - Tc, t1 - Tc)
        return f

    emitters = []

    def add(emit, tag):
        emitters.append((emit, tag))

    base = {k: hb[k] for k in BASE_CONSTS}
    base_h16 = dict(base, c_hot16=hb["c_hot16"])
    base_l3f = dict(base, c_eyef=hb["c_eyef"], c_hot32=hb["c_hot32"])

    h = dict(base); h.update(x=hb["x_in"], summ=hb["summ_in"], wq=hb["l0_wq"], wk=hb["l0_wk"],
                             wv=hb["l0_wv"], wc=hb["l0_wc"], bq_r=hb["l0_bq"], bk_r=hb["l0_bk"],
                             bv_r=hb["l0_bv"], bc_r=hb["l0_bc"], xo=hb["x_a"])
    add(setup_l0a(nc, h, "A_"), "l0a")
    h = dict(base); h.update(x=hb["x_a"], w1i=hb["f0_w1i"], w2i=hb["f0_w2i"], stmU=hb["f0_stmU"],
                             stmD=hb["f0_stmD"], ubias_p=hb["f0_ubias"], bcon_r=hb["f0_bcon"],
                             dbias_r=hb["f0_dbias"], xo=hb["x_b"], scr=hb["f0scr"])
    add(setup_l0f(nc, h, "B_"), "l0f")
    h = dict(base); h.update(x=hb["x_b"], wqkv=hb["l1_wqkv"], bqkv=hb["l1_bqkv"],
                             qp_o=hb["qp_s"], kv_o=hb["kv_loc"], ks_o=hb["ks_loc"])
    add(setup_l1a(nc, h, "C_"), "l1a")
    add(("CC", "AllReduce", [hb["kv_loc"], hb["kv_sum"]]), "cc_kv")
    add(("CC", "AllReduce", [hb["ks_loc"], hb["ks_sum"]]), "cc_ks")
    h = dict(base); h.update(x=hb["x_b"], qp=hb["qp_s"], kv0=hb["kv_sum"], ks0=hb["ks_sum"],
                             wg=hb["l1_wg"], wu=hb["l1_wu"], wd=hb["l1_wd"], bg=hb["l1_bg"],
                             bu=hb["l1_bu"], bd=hb["l1_bd"], g1=hb["l1_g1"], b1=hb["l1_b1"],
                             g2=hb["l1_g2"], b2=hb["l1_b2"], xo=hb["x_a"])
    add(setup_l1b(nc, h, "D_", single=True), "l1b")
    h = dict(base); h.update(x=hb["x_a"], wgT=hb["f1_wgT"], wuT=hb["f1_wuT"], wdT=hb["f1_wdT"],
                             xo=hb["x_b"])
    add(setup_l1f(nc, h, "E_"), "l1f")
    add(("CC", "AllGather", [hb["x_b"], hb["xg_a"]]), "cc_agx4")

    for ell in range(4):
        h = dict(base)
        h.update(wa=hb[f"m{ell}_wa"], xpT=hb[f"m{ell}_xpT"], xpTc=hb[f"m{ell}_xpTc"],
                 cw=hb[f"m{ell}_cw"], cb=hb[f"m{ell}_cb"],
                 xc_o=hb["xc_s"], sz_o=hb["sgz_s"],
                 db_o=hb["dbc_loc"][0:80, :], c_o=hb["dbc_loc"][80:96, :])
        if ell == 0:
            h["x"] = split_view(hb["xg_a"])
            add(setup_mka(nc, h, f"M{ell}a_", mode="ln"), f"mka{ell}")
        else:
            h["p0"] = split_view(hb["op_sum"])
            add(setup_mka(nc, h, f"M{ell}a_", mode="plain"), f"mka{ell}")
        add(("CC", "AllReduce", [hb["dbc_loc"], hb["dbc_sum"]]), f"cc_dbc{ell}")
        h = dict(base)
        outv = hb["op_loc"]
        def out_fn(t0, t1, v=outv):
            if t1 <= Tc:
                return fm_ap(v[0:D, :], t0, t1)
            return fm_ap(v[D:2 * D, :], t0 - Tc, t1 - Tc)
        h.update(xc=hb["xc_s"], sgz=hb["sgz_s"], db0=hb["dbc_sum"][0:80, :],
                 c0=hb["dbc_sum"][80:96, :],
                 dtwT=hb[f"m{ell}_dtwT"], dtb=hb[f"m{ell}_dtb"], alog=hb[f"m{ell}_alog"],
                 dcol=hb[f"m{ell}_dcol"], owT=hb[f"m{ell}_owT"], op_o=out_fn,
                 dls=hb["dls"], dus=hb["dus"], bts=hb["bts"], cts=hb["cts"], ybs=hb["ybs"])
        add(setup_mkb2(nc, h, f"M{ell}b_", momentum=False), f"mkb{ell}")
        if ell < 3:
            add(("CC", "AllReduce", [hb["op_loc"], hb["op_sum"]]), f"cc_op{ell}")
    add(("CC", "ReduceScatter", [hb["op_loc"], hb["op_rs"]]), "cc_rs2")
    add(("HALO",), "halo")
    add(("CC", "AllGather", [hb["halo_loc"], hb["halo_ag"]]), "cc_halo")

    h = dict(base); h.update(x=hb["x_b"], xhalo=hb["f2_hflag"], wrT=hb["f2_wrT"], wkT=hb["f2_wkT"],
                             wvT=hb["f2_wvT"], mur=hb["f2_mur"], muk=hb["f2_muk"],
                             pp0=hb["op_rs"], hh0=hb["halo_ag"][0:128, :], hflag=hb["f2_hflag"],
                             xo=hb["x_a"])
    add(setup_l2f(nc, h, "F_", add_parts=1), "l2f")
    add(("CC", "AllGather", [hb["x_a"], hb["xg_b"]]), "cc_agx6")

    h = dict(base)
    h.update(wa=hb["m4_wa"], xpT=hb["m4_xpT"], xpTc=hb["m4_xpTc"], cw=hb["m4_cw"], cb=hb["m4_cb"],
             xc_o=hb["xc_s"], sz_o=hb["sgz_s"], db_o=hb["dbc_loc"][0:80, :], c_o=hb["dbc_loc"][80:96, :])
    h["x"] = split_view(hb["xg_b"])
    add(setup_mka(nc, h, "Ga_", mode="ln"), "mka3")
    add(("CC", "AllReduce", [hb["dbc_loc"], hb["dbc_sum"]]), "cc_dbc4")
    h = dict(base)
    outv = hb["op_loc"]
    def out_fn3(t0, t1, v=outv):
        if t1 <= Tc:
            return fm_ap(v[0:D, :], t0, t1)
        return fm_ap(v[D:2 * D, :], t0 - Tc, t1 - Tc)
    h.update(xc=hb["xc_s"], sgz=hb["sgz_s"], db0=hb["dbc_sum"][0:80, :], c0=hb["dbc_sum"][80:96, :],
             dtwT=hb["m4_dtwT"], dtb=hb["m4_dtb"], alog=hb["m4_alog"],
             dcol=hb["m4_dcol"], owT=hb["m4_owT"], op_o=out_fn3,
             dls=hb["dls"], dus=hb["dus"], bts=hb["bts"], cts=hb["cts"], ybs=hb["ybs"])
    add(setup_mkb2(nc, h, "Gb_", momentum=True), "mkb3")
    add(("CC", "ReduceScatter", [hb["op_loc"], hb["op_rs3"]]), "cc_rs3")

    h = dict(base_l3f); h.update(x=hb["x_a"], gwT=hb["f3_gwT"], ewT=hb["f3_ewT"],
                             pp0=hb["op_rs3"], xo=out_ap)
    add(setup_l3f(nc, h, "H_", add_parts=1), "l3f")

    with contextlib.ExitStack() as ctx:
        halo_sb = ctx.enter_context(nc.sbuf_tensor("sb_halo", [128, 24], F32))

        def body(s):
            for em, tag in emitters:
                if isinstance(em, tuple) and em[0] == "CC":
                    _, kind, (cin, cout) = em
                    for e in ("q", "g", "v", "s", "t"):
                        s.wait(e)
                    s.op("g", lambda E, kind=kind, cin=cin, cout=cout: E.collective_compute(
                        kind, ALU.add if kind != "AllGather" else ALU.bypass,
                        replica_groups=RG_PAIRS, ins=[cin.opt()], outs=[cout.opt()]), k=1)
                    for e in ("q", "g", "v", "s", "t"):
                        s.wait(e)
                elif isinstance(em, tuple) and em[0] == "HALO":
                    s.op("q", lambda E: E.dma_start(
                        out=halo_sb[:, 0:8],
                        in_=hb["x_b"][:, Tc - 1:Tc].rearrange("(c p) o -> p c o", p=128),
                        allow_slow_non_contiguous=True))
                    s.op("q", lambda E: E.dma_start(
                        out=halo_sb[:, 8:16],
                        in_=hb["op_rs"][:, Tc - 1:Tc].rearrange("(c p) o -> p c o", p=128),
                        allow_slow_non_contiguous=True))
                    s.wait("v", ["ld"])
                    s.op("v", lambda E: E.tensor_tensor(out=halo_sb[:, 16:24], in0=halo_sb[:, 0:8],
                                                        in1=halo_sb[:, 8:16], op=ALU.add))
                    s.wait("q", ["sv"])
                    s.op("q", lambda E: E.dma_start(out=hb["halo_loc"], in_=halo_sb[:, 16:24]))
                else:
                    em(s)
                    for e in ("q", "g", "v", "s", "t"):
                        s.wait(e)

        run_sched(nc, ctx, body)
    return nc


def fused_inmaps(ins):
    consts = make_consts()
    stmU, stmD = l0f_stage_tiles(ins["L0_ffn_up_weights"], ins["L0_ffn_down_weights"])
    perm = _rmaj_perm()
    w1i = np.ascontiguousarray(np.concatenate(
        [np.asarray(ins["L0_ffn_W_exp"]).T, np.asarray(ins["L0_ffn_b_exp"]).reshape(-1, 1)], axis=1).astype(np.float32))
    w2i = np.ascontiguousarray(np.asarray(ins["L0_ffn_W_con"]).astype(np.float32))
    ub_p = np.ascontiguousarray(np.asarray(ins["L0_ffn_up_bias"])[perm].reshape(4, 1024))
    wg = _pad_cols(np.asarray(ins["L1_attn_Wg"]), 2816)
    wu = _pad_cols(np.asarray(ins["L1_attn_Wu"]), 2816)
    wd = _pad_rows(np.asarray(ins["L1_attn_Wd"]), 2816)
    bg = _pad_cols(host_row(ins["L1_attn_bg"]), 2816)
    bu = _pad_cols(host_row(ins["L1_attn_bu"]), 2816)
    x = np.asarray(ins["L0_attn_x"])
    ew = np.asarray(ins["L3_ffn_expert_w"])
    ewT = np.ascontiguousarray(np.transpose(ew, (0, 2, 1)).reshape(8 * D, D))
    maps = []
    for c in range(NCORE):
        b, eh = c // 2, c % 2
        m = dict(consts)
        m["x_in"] = host_fm(x[b, eh * 4096:(eh + 1) * 4096])
        m["summ_in"] = host_fm(np.roll(x[b, ::256, :], -16 * eh, axis=0))
        m["l0_wq"] = np.ascontiguousarray(np.asarray(ins["L0_attn_wq"]).T)
        m["l0_wk"] = np.ascontiguousarray(np.asarray(ins["L0_attn_wk"]).T)
        m["l0_wv"] = np.ascontiguousarray(np.asarray(ins["L0_attn_wv"]).T)
        m["l0_wc"] = np.ascontiguousarray(np.asarray(ins["L0_attn_wc"]).T)
        m["l0_bq"] = host_row(ins["L0_attn_bq"]); m["l0_bk"] = host_row(ins["L0_attn_bk"])
        m["l0_bv"] = host_row(ins["L0_attn_bv"]); m["l0_bc"] = host_row(ins["L0_attn_bc"])
        m["f0_w1i"] = w1i; m["f0_w2i"] = w2i; m["f0_stmU"] = stmU; m["f0_stmD"] = stmD
        m["f0_ubias"] = ub_p
        m["f0_bcon"] = host_row(ins["L0_ffn_b_con"]); m["f0_dbias"] = host_row(ins["L0_ffn_down_bias"])
        m["l1_wqkv"] = np.ascontiguousarray(np.asarray(ins["L1_attn_Wqkv"]).astype(np.float32))
        m["l1_bqkv"] = host_row(ins["L1_attn_bqkv"])
        m["l1_wg"] = wg; m["l1_wu"] = wu; m["l1_wd"] = wd
        m["l1_bg"] = bg; m["l1_bu"] = bu; m["l1_bd"] = host_row(ins["L1_attn_bd"])
        m["l1_g1"] = host_col(ins["L1_attn_g1"]); m["l1_b1"] = host_col(ins["L1_attn_b1"])
        m["l1_g2"] = host_col(ins["L1_attn_g2"]); m["l1_b2"] = host_col(ins["L1_attn_b2"])
        m["f1_wgT"] = np.ascontiguousarray(np.asarray(ins["L1_ffn_w_gate"]).T)
        m["f1_wuT"] = np.ascontiguousarray(np.asarray(ins["L1_ffn_w_up"]).T)
        m["f1_wdT"] = np.ascontiguousarray(np.asarray(ins["L1_ffn_w_down"]).T)
        rows = np.r_[eh * 1024:(eh + 1) * 1024, 2048 + eh * 1024: 2048 + (eh + 1) * 1024]
        for mm in range(5):
            if mm < 4:
                in_w = np.asarray(ins["L2_attn_in_proj_w"][mm]); xp_w = np.asarray(ins["L2_attn_x_proj_w"][mm])
                cwf = np.asarray(ins["L2_attn_conv_w"][mm]); cbf = np.asarray(ins["L2_attn_conv_b"][mm])
                dtw = np.asarray(ins["L2_attn_dt_proj_w"][mm]); dtb = np.asarray(ins["L2_attn_dt_proj_b"][mm])
                alog = np.asarray(ins["L2_attn_A_log"][mm]); dv = np.asarray(ins["L2_attn_D"][mm])
                oww = np.asarray(ins["L2_attn_out_proj_w"][mm])
            else:
                in_w = np.asarray(ins["L3_attn_W_in"]); xp_w = np.asarray(ins["L3_attn_W_x"])
                cwf = np.asarray(ins["L3_attn_conv_w"]); cbf = np.asarray(ins["L3_attn_conv_b"])
                dtw = np.asarray(ins["L3_attn_W_dt"]); dtb = np.asarray(ins["L3_attn_b_dt"])
                alog = np.asarray(ins["L3_attn_A_log"]); dv = np.asarray(ins["L3_attn_D"])
                oww = np.asarray(ins["L3_attn_W_out"])
            sl = slice(eh * 1024, (eh + 1) * 1024)
            m[f"m{mm}_wa"] = np.ascontiguousarray(in_w[rows, :].T)
            xs = xp_w[:, sl]
            m[f"m{mm}_xpT"] = np.ascontiguousarray(xs[0:80, :].T)
            m[f"m{mm}_xpTc"] = np.ascontiguousarray(xs[80:96, :].T)
            cw2 = cwf[sl, :]
            m[f"m{mm}_cw"] = np.ascontiguousarray(cw2.reshape(8, 128, 4).transpose(1, 0, 2).reshape(128, 32))
            m[f"m{mm}_cb"] = host_col(cbf[sl])
            m[f"m{mm}_dtwT"] = np.ascontiguousarray(dtw[sl, :].T)
            m[f"m{mm}_dtb"] = host_row(dtb[sl])
            m[f"m{mm}_alog"] = np.ascontiguousarray(alog[sl, :].reshape(8, 128, 16).transpose(1, 0, 2).reshape(128, 128))
            m[f"m{mm}_dcol"] = host_col(dv[sl])
            m[f"m{mm}_owT"] = np.ascontiguousarray(oww[:, sl].T)
        m["f2_wrT"] = np.ascontiguousarray(np.asarray(ins["L2_ffn_W_r"]).T)
        m["f2_wkT"] = np.ascontiguousarray(np.asarray(ins["L2_ffn_W_k"]).T)
        m["f2_wvT"] = np.ascontiguousarray(np.asarray(ins["L2_ffn_W_v"]).T)
        m["f2_mur"] = host_col(ins["L2_ffn_mu_r"]); m["f2_muk"] = host_col(ins["L2_ffn_mu_k"])
        m["f2_hflag"] = np.full((128, 8), float(eh), dtype=np.float32)
        m["f3_gwT"] = np.ascontiguousarray(np.asarray(ins["L3_ffn_gate_w"]).T)
        m["f3_ewT"] = ewT
        maps.append(m)
    return maps


def kernel_fused(**inputs):
    ins = {k: np.asarray(v) for k, v in inputs.items()}
    nc = fused_build()
    maps = fused_inmaps(ins)
    res = run_bass_kernel_spmd(nc, maps, core_ids=list(range(NCORE)))
    return gather_fm(res.results, key="out")


kernel = kernel_fused


# version 05
# speedup vs baseline: 1.1771x; 1.0071x over previous
import contextlib
import numpy as np
import ml_dtypes
import concourse.bass as bass
import concourse.mybir as mybir
from concourse.bass_utils import run_bass_kernel_spmd

F32 = mybir.dt.float32
BF16 = mybir.dt.bfloat16
I32 = mybir.dt.int32
AF = mybir.ActivationFunctionType
ALU = mybir.AluOpType
BF = ml_dtypes.bfloat16

NCORE = 8
D = 1024
DCH = D // 128
TT = 512
LN_EPS = 1e-5

ENGSEM = {"q": "ld", "g": "lg", "v": "sv", "s": "ss", "t": "sm"}
ALLSEMS = ["ld", "lg", "sv", "ss", "sm"]



def host_fm(x_td):
    return np.ascontiguousarray(np.asarray(x_td).T)


def host_col(vec, p=128):
    v = np.asarray(vec)
    n = v.shape[0]
    return np.ascontiguousarray(v.reshape(n // p, p).T)


def host_row(vec):
    return np.ascontiguousarray(np.asarray(vec).reshape(1, -1))


def make_consts():
    eye = np.eye(128, dtype=BF)
    ones_bf = np.ones((128, TT), dtype=BF)
    ones_f32 = np.ones((128, 128), dtype=np.float32)
    cvec = np.zeros((128, 8), dtype=np.float32)
    cvec[:, 0] = LN_EPS
    cvec[:, 1] = 1e-6
    cvec[:, 2] = 1.0
    cvec[:, 3] = 0.0
    cvec[:, 4] = -0.5
    cvec[:, 5] = 1.0 / 8.0
    cvec[:, 6] = 1.0 / 127.0
    hot32 = np.zeros((32, 32 * 128), dtype=np.float32)
    for k in range(32):
        hot32[k, k * 128:(k + 1) * 128] = 1.0
    hot16 = np.zeros((16, 16 * 128), dtype=BF)
    for k in range(16):
        hot16[k, k * 128:(k + 1) * 128] = 1.0
    return {"c_eye": eye, "c_ones_bf": ones_bf, "c_ones_f32": ones_f32, "c_cvec": cvec,
            "c_eyef": np.eye(128, dtype=np.float32), "c_hot32": hot32, "c_hot16": hot16}


CONST_SPECS = [("c_eye", [128, 128], BF16), ("c_ones_bf", [128, TT], BF16),
               ("c_ones_f32", [128, 128], F32), ("c_cvec", [128, 8], F32),
               ("c_eyef", [128, 128], F32), ("c_hot32", [32, 32 * 128], F32),
               ("c_hot16", [16, 16 * 128], BF16)]


def fm_ap(hbm_ap, t0, t1):
    return hbm_ap[:, t0:t1].rearrange("(c p) t -> p c t", p=128)


def sb3(sb_ap, t, c=DCH):
    return sb_ap.rearrange("p (c t) -> p c t", c=c)


def fm_ap_w(hbm_ap, t0, t1):
    return hbm_ap[:, t0:t1].rearrange("(c p) t -> p (c t)", p=128)



class Sched:

    def __init__(self, who, E, semh):
        self.who = who
        self.E = E
        self.semh = semh
        self.cnt = {k: 0 for k in semh}

    def op(self, e, fn, sem=None, k=None):
        s = sem or ENGSEM[e]
        if k is None:
            k = 16 if e in ("q", "g") else 1
        if self.who == e:
            if self.cnt[s] > 0:
                self.E.wait_ge(self.semh[s], self.cnt[s])
            instr = fn(self.E)
            instr.then_inc(self.semh[s], k)
        self.cnt[s] += k

    def silent(self, e, fn):
        if self.who == e:
            fn(self.E)

    def wait(self, e, sems=None):
        sems = sems or [s for s in ALLSEMS if s != ENGSEM[e]]
        if self.who == e:
            for s in sems:
                if self.cnt[s] > 0:
                    self.E.wait_ge(self.semh[s], self.cnt[s])

    def wait_at(self, e, sem, n):
        if self.who == e:
            self.E.wait_ge(self.semh[sem], n)


def run_sched(nc, ctx, emit_body):
    semh = {k: ctx.enter_context(nc.semaphore(k)) for k in ALLSEMS}
    block = ctx.enter_context(nc.Block())
    finals = {}

    def replay(who):
        def f(E):
            s = Sched(who, E, semh)
            emit_body(s)
            finals.update(s.cnt)
        return f

    block.sync(replay("q"))
    block.vector(replay("v"))
    block.scalar(replay("s"))
    block.tensor(replay("t"))
    block.gpsimd(replay("g"))
    return finals



BASE_CONSTS = ["c_eye", "c_ones_bf", "c_ones_f32", "c_cvec"]


def emit_load_consts(s, hb, sb):
    for name, _, _ in CONST_SPECS:
        if name in hb:
            s.op("q", lambda E, n=name: E.dma_start(out=sb[n][:], in_=hb[n]))


def consts_subset(consts, names):
    return {k: consts[k] for k in names}


def emit_cast_weight(s, hbm_chunks, stg, dst_slices):
    s.wait("q", ["sv"])
    sv0 = s.cnt["sv"]
    for k, (src, dst) in enumerate(zip(hbm_chunks, dst_slices)):
        ncols = src.shape[1]
        if k >= 2:
            s.wait_at("q", "sv", sv0 + k - 1)
        s.op("q", lambda E, sr=src, kk=k, nc_=ncols: E.dma_start(out=stg[kk % 2][:, :nc_], in_=sr))
        s.wait_at("v", "ld", s.cnt["ld"])
        s.op("v", lambda E, d=dst, kk=k, nc_=ncols: E.tensor_copy(out=d, in_=stg[kk % 2][:, :nc_]))


def emit_ln(s, ps, sb, x_sl, xhat_sl, sq_sl, t, name="ln", dch=DCH):
    for c in range(dch):
        s.op("s", lambda E, c=c: E.activation(out=sq_sl[:, c * t:(c + 1) * t],
                                              in_=x_sl[:, c * t:(c + 1) * t], func=AF.Square))
    s.wait("t", ["ss", "sv", "ld"])
    for c in range(dch):
        s.op("t", lambda E, c=c: E.matmul(out=ps["stat"][0:1, :t], lhsT=sb["c_ones_f32"][:, 0:1],
                                          rhs=x_sl[:, c * t:(c + 1) * t],
                                          start=(c == 0), stop=(c == dch - 1)))
    for c in range(dch):
        s.op("t", lambda E, c=c: E.matmul(out=ps["stat"][64:65, :t], lhsT=sb["c_ones_bf"][:, 0:1],
                                          rhs=sq_sl[:, c * t:(c + 1) * t],
                                          start=(c == 0), stop=(c == dch - 1)))
    s.wait("v", ["sm"])
    s.op("v", lambda E: E.tensor_scalar(out=sb["row0"][0:1, :t], in0=ps["stat"][0:1, :t],
                                        scalar1=1.0 / D, scalar2=None, op0=ALU.mult))
    s.op("v", lambda E: E.tensor_tensor(out=sb["row1"][0:1, :t], in0=sb["row0"][0:1, :t],
                                        in1=sb["row0"][0:1, :t], op=ALU.mult))
    s.op("v", lambda E: E.scalar_tensor_tensor(out=sb["row1"][0:1, :t], in0=ps["stat"][64:65, :t],
                                               scalar=1.0 / D, in1=sb["row1"][0:1, :t],
                                               op0=ALU.mult, op1=ALU.subtract))
    s.wait("s", ["sv"])
    s.op("s", lambda E: E.activation(out=sb["row1"][0:1, :t], in_=sb["row1"][0:1, :t],
                                     func=AF.Ln, bias=sb["c_cvec"][0:1, 0:1], scale=sb["c_cvec"][0:1, 2:3]))
    s.op("s", lambda E: E.activation(out=sb["row1"][0:1, :t], in_=sb["row1"][0:1, :t],
                                     func=AF.Exp, bias=sb["c_cvec"][0:1, 3:4], scale=sb["c_cvec"][0:1, 4:5]))
    s.wait("t", ["ss", "sv"])
    s.op("t", lambda E: E.matmul(out=ps["rep0"][:, :t], lhsT=sb["c_ones_f32"][0:1, :],
                                 rhs=sb["row0"][0:1, :t], start=True, stop=True))
    s.op("t", lambda E: E.matmul(out=ps["rep1"][:, :t], lhsT=sb["c_ones_f32"][0:1, :],
                                 rhs=sb["row1"][0:1, :t], start=True, stop=True))
    s.wait("v", ["sm"])
    for c in range(dch):
        s.op("v", lambda E, c=c: E.tensor_tensor(out=sq_sl[:, c * t:(c + 1) * t],
                                                 in0=x_sl[:, c * t:(c + 1) * t],
                                                 in1=ps["rep0"][:, :t], op=ALU.subtract))
    for c in range(dch):
        s.op("v", lambda E, c=c: E.tensor_tensor(out=xhat_sl[:, c * t:(c + 1) * t],
                                                 in0=sq_sl[:, c * t:(c + 1) * t],
                                                 in1=ps["rep1"][:, :t], op=ALU.mult))




def emit_mm_pipe(s, groups, pbanks, ev_engine="v"):
    ev_sem = ENGSEM[ev_engine]
    ev_marks = {}
    mm_marks = {}
    for g, (mms, evac) in enumerate(groups):
        pb = pbanks[g % 2]
        if g >= 2:
            s.wait_at("t", ev_sem, ev_marks[g - 2])
        for f in mms:
            s.op("t", lambda E, f=f, pb=pb: f(E, pb))
        mm_marks[g] = s.cnt["sm"]
        s.wait_at(ev_engine, "sm", mm_marks[g])
        s.op(ev_engine, lambda E, evac=evac, pb=pb: evac(E, pb))
        ev_marks[g] = s.cnt[ev_sem]




def emit_mm_pipe2(s, groups, pbanks):
    nb = len(pbanks)
    ev_marks = {}
    for g, item in enumerate(groups):
        mms, evac, ev_e = item[0], item[1], item[2]
        extra = item[3] if len(item) > 3 else []
        pb = pbanks[g % nb]
        if g >= nb:
            e2, m2 = ev_marks[g - nb]
            s.wait_at("t", ENGSEM[e2], m2)
        for f in mms:
            s.op("t", lambda E, f=f, pb=pb: f(E, pb))
        mm_mark = s.cnt["sm"]
        s.wait_at(ev_e, "sm", mm_mark)
        for sem in extra:
            s.wait(ev_e, [sem])
        s.op(ev_e, lambda E, evac=evac, pb=pb: evac(E, pb))
        ev_marks[g] = (ev_e, s.cnt[ENGSEM[ev_e]])



L0A_SCALE = 1.0 / np.sqrt(64.0)


def setup_l0a(nc, hb, prefix="", Tc=4096, debug=False):
    NT = Tc // TT
    ctx = contextlib.ExitStack()
    with ctx:
        sb = {}

        def S(name, shape, dt=F32):
            sb[name] = ctx.enter_context(nc.sbuf_tensor("sb_" + prefix + name, shape, dt))
            return sb[name]

        for name, shp, dt in CONST_SPECS:
            if name in hb:
                S(name, shp, dt)
        S("wstg0", [128, 1024], F32); S("wstg1", [128, 1024], F32)
        for w in ("wq", "wk", "wv", "wc"):
            S(w + "_bf", [128, 8 * 1024], BF16)
        S("bq_r", [1, D]); S("bk_r", [1, D])
        S("bv_r", [1, D]); S("bc_r", [1, D])
        S("bq_bf", [1, D], BF16); S("bk_bf", [1, D], BF16)
        S("bv_bf", [1, D], BF16); S("bc_bf", [1, D], BF16)
        S("x0", [128, DCH * TT]); S("x1", [128, DCH * TT])
        S("xhat", [128, DCH * TT], BF16)
        S("sq", [128, DCH * TT], BF16)
        sb["fused"] = sb["sq"]
        S("qT", [128, DCH * TT], BF16); S("kT", [128, DCH * TT], BF16)
        S("vtok", [128, 4 * 1024], BF16)
        S("probs", [128, 512], BF16)
        S("attnT", [128, 512], BF16)
        S("row0", [1, TT]); S("row1", [1, TT])
        S("nm", [128, 4]); S("rs", [128, 4])
        S("summ", [128, DCH * 32]); S("shat", [128, DCH * 32], BF16)
        S("ssq", [128, DCH * 32], BF16)
        S("qgT", [128, DCH * 32], BF16); S("kgT", [128, DCH * 32], BF16)
        S("vgtok", [32, 1024], BF16)
        S("pg", [32, 32], BF16); S("agT", [32, 32], BF16)
        S("gout", [128, DCH * 32])
        S("nmg", [32, 2]); S("rsg", [32, 2])

        ps = {}
        for name, shape, dt in [("stat", [128, TT], F32), ("rep0", [128, TT], F32),
                                ("rep1", [128, TT], F32), ("pq", [128, TT], F32),
                                ("pvt", [128, 1024], F32), ("psc", [128, TT], F32),
                                ("pT", [128, 256], BF16)]:
            ps[name] = ctx.enter_context(nc.psum_tensor(prefix + name, shape, dt))

        def body(s):
            emit_load_consts(s, hb, sb)
            for small in ("bq_r", "bk_r", "bv_r", "bc_r", "summ"):
                src = hb[small] if small != "summ" else fm_ap(hb["summ"], 0, 32)
                s.op("q", lambda E, d=sb[small], sr=src, sm=small: E.dma_start(
                    out=(sb3(d[:], 32) if sm == "summ" else d[:]), in_=sr))
            chunks, dsts = [], []
            for w in ("wq", "wk", "wv", "wc"):
                for ci in range(DCH):
                    chunks.append(hb[w][ci * 128:(ci + 1) * 128, :])
                    dsts.append(sb[w + "_bf"][:, ci * 1024:(ci + 1) * 1024])
            emit_cast_weight(s, chunks, [sb["wstg0"], sb["wstg1"]], dsts)
            s.wait("v", ["ld"])
            for bn in ("bq", "bk", "bv", "bc"):
                s.op("v", lambda E, bn=bn: E.tensor_copy(out=sb[bn + "_bf"][:], in_=sb[bn + "_r"][:]))

            cv = sb["c_cvec"]
            one_ap = cv[:, 2:3]

            s.wait("s", ["ld"])
            emit_ln(s, ps, sb, sb["summ"][:], sb["shat"][:], sb["ssq"][:], 32)
            for (wname, bname, dst) in (("wq_bf", "bq_bf", "qgT"), ("wk_bf", "bk_bf", "kgT")):
                for co in range(DCH):
                    s.wait("t", ["sv", "ss"])
                    for ci in range(DCH):
                        s.op("t", lambda E, ci=ci, co=co, w=wname: E.matmul(
                            out=ps["pq"][:, 0:32],
                            lhsT=sb[w][:, ci * 1024 + co * 128: ci * 1024 + (co + 1) * 128],
                            rhs=sb["shat"][:, ci * 32:(ci + 1) * 32],
                            start=(ci == 0), stop=False))
                    s.op("t", lambda E, co=co, b=bname: E.matmul(
                        out=ps["pq"][:, 0:32], lhsT=sb[b][0:1, co * 128:(co + 1) * 128],
                        rhs=sb["c_ones_bf"][0:1, 0:32], start=False, stop=True))
                    s.wait("s", ["sm"])
                    s.op("s", lambda E, co=co, d=dst: E.copy(
                        out=sb[d][:, co * 32:(co + 1) * 32], in_=ps["pq"][:, 0:32]))
            s.wait("t", ["ss", "sv"])
            for h in range(2):
                for ci in range(DCH):
                    s.op("t", lambda E, ci=ci, h=h: E.matmul(
                        out=ps["pvt"][0:32, h * 512:(h + 1) * 512],
                        lhsT=sb["shat"][:, ci * 32:(ci + 1) * 32],
                        rhs=sb["wv_bf"][:, ci * 1024 + h * 512: ci * 1024 + (h + 1) * 512],
                        start=(ci == 0), stop=False))
                s.op("t", lambda E, h=h: E.matmul(
                    out=ps["pvt"][0:32, h * 512:(h + 1) * 512], lhsT=sb["c_ones_bf"][0:1, 0:32],
                    rhs=sb["bv_bf"][0:1, h * 512:(h + 1) * 512], start=False, stop=True))
            s.wait("v", ["sm"])
            s.op("v", lambda E: E.tensor_copy(out=sb["vgtok"][:], in_=ps["pvt"][0:32, :]))
            s.wait("t", ["sv"])
            for ci in range(DCH):
                s.op("t", lambda E, ci=ci: E.matmul(
                    out=ps["psc"][0:32, 0:32], lhsT=sb["qgT"][:, ci * 32:(ci + 1) * 32],
                    rhs=sb["kgT"][:, ci * 32:(ci + 1) * 32],
                    start=(ci == 0), stop=(ci == DCH - 1)))
            s.wait("v", ["sm"])
            s.op("v", lambda E: E.reduce_max(out=sb["nmg"][0:32, 0:1], in_=ps["psc"][0:32, 0:32],
                                             axis=mybir.AxisListType.X, negate=True))
            s.op("v", lambda E: E.tensor_scalar(out=sb["nmg"][0:32, 1:2], in0=sb["nmg"][0:32, 0:1],
                                                scalar1=L0A_SCALE, scalar2=None, op0=ALU.mult))
            s.wait("s", ["sv"])
            s.op("s", lambda E: E.activation(out=sb["pg"][:], in_=ps["psc"][0:32, 0:32],
                                             func=AF.Exp, bias=sb["nmg"][0:32, 1:2],
                                             scale=cv[0:32, 5:6], accum_out=sb["rsg"][0:32, 0:1]))
            s.wait("v", ["ss"])
            s.op("v", lambda E: E.reciprocal(out=sb["rsg"][0:32, 1:2], in_=sb["rsg"][0:32, 0:1]))
            s.op("v", lambda E: E.tensor_scalar(out=sb["pg"][:], in0=sb["pg"][:],
                                                scalar1=sb["rsg"][0:32, 1:2], scalar2=None,
                                                op0=ALU.mult))
            s.wait("t", ["sv"])
            s.op("t", lambda E: E.transpose(out=ps["pT"][0:32, 0:32], in_=sb["pg"][:],
                                            identity=sb["c_eye"][0:32, 0:32]))
            s.wait("v", ["sm"])
            s.op("v", lambda E: E.tensor_copy(out=sb["agT"][:], in_=ps["pT"][0:32, 0:32]))
            s.wait("t", ["sv"])
            for co in range(DCH):
                s.op("t", lambda E, co=co: E.matmul(
                    out=ps["rep0"][:, 0:32], lhsT=sb["vgtok"][:, co * 128:(co + 1) * 128],
                    rhs=sb["agT"][:], start=True, stop=True))
                s.wait("v", ["sm"])
                s.op("v", lambda E, co=co: E.tensor_copy(out=sb["gout"][:, co * 32:(co + 1) * 32],
                                                         in_=ps["rep0"][:, 0:32]))
                s.wait("t", ["sv"])

            if debug:
                s.wait("g", ["sv", "ss", "sm"])
                s.op("g", lambda E: E.dma_start(out=hb["dbg_shat"], in_=sb["shat"][:]))
                s.op("g", lambda E: E.dma_start(out=hb["dbg_rows"][0:1, 0:32], in_=sb["row0"][0:1, 0:32]))
                s.op("g", lambda E: E.dma_start(out=hb["dbg_rows"][1:2, 0:32], in_=sb["row1"][0:1, 0:32]))
                s.op("g", lambda E: E.dma_start(out=hb["dbg_gout"], in_=sb["gout"][:]))
                for e_ in ("v", "s", "t"):
                    s.wait(e_, ["lg"])
            marks = []
            marks_lg = []
            for i in range(NT):
                t0 = i * TT
                xsb = sb["x0"] if i % 2 == 0 else sb["x1"]
                osb = xsb
                if i >= 2:
                    s.wait_at("q", "sv", marks[i - 2])
                    s.wait_at("q", "lg", marks_lg[i - 2])
                s.op("q", lambda E, t0=t0, xsb=xsb: E.dma_start(out=sb3(xsb[:], TT), in_=fm_ap(hb["x"], t0, t0 + TT)))
                s.wait("s", ["ld", "sm", "sv"])
                s.wait("v", ["ld"])
                emit_ln(s, ps, sb, xsb[:], sb["xhat"][:], sb["sq"][:], TT)
                if debug and i == 0:
                    s.wait("g", ["sv", "ss", "sm"])
                    s.op("g", lambda E: E.dma_start(out=hb["dbg_xhat"], in_=sb["xhat"][:]))
                    for e_ in ("v", "s", "t"):
                        s.wait(e_, ["lg"])
                s.wait("t", ["sv", "ss"])
                groups = []
                for (wname, bname, dst) in (("wq_bf", "bq_bf", "qT"), ("wk_bf", "bk_bf", "kT")):
                    for co in range(DCH):
                        def mkq(co, w, b, d):
                            def mm(E, pb, ci):
                                return E.matmul(
                                    out=pb[:],
                                    lhsT=sb[w][:, ci * 1024 + co * 128: ci * 1024 + (co + 1) * 128],
                                    rhs=sb["xhat"][:, ci * TT:(ci + 1) * TT],
                                    start=(ci == 0), stop=False)
                            mms = [lambda E, pb, ci=ci: mm(E, pb, ci) for ci in range(DCH)]
                            mms.append(lambda E, pb: E.matmul(
                                out=pb[:], lhsT=sb[b][0:1, co * 128:(co + 1) * 128],
                                rhs=sb["c_ones_bf"][0:1, :], start=False, stop=True))
                            ev = lambda E, pb: E.copy(out=sb[d][:, co * TT:(co + 1) * TT], in_=pb[:])
                            return (mms, ev, "s")
                        groups.append(mkq(co, wname, bname, dst))
                emit_mm_pipe2(s, groups, [ps["pq"], ps["psc"]])
                if debug and i == 0:
                    s.wait("g", ["sv", "ss", "sm"])
                    s.op("g", lambda E: E.dma_start(out=hb["dbg_qT"], in_=sb["qT"][:]))
                    for e_ in ("v", "s", "t"):
                        s.wait(e_, ["lg"])
                for j in range(4):
                    s.wait("t", ["sv", "ss"])
                    for h in range(2):
                        for ci in range(DCH):
                            s.op("t", lambda E, ci=ci, j=j, h=h: E.matmul(
                                out=ps["pvt"][:, h * 512:(h + 1) * 512],
                                lhsT=sb["xhat"][:, ci * TT + j * 128: ci * TT + (j + 1) * 128],
                                rhs=sb["wv_bf"][:, ci * 1024 + h * 512: ci * 1024 + (h + 1) * 512],
                                start=(ci == 0), stop=False))
                        s.op("t", lambda E, h=h: E.matmul(
                            out=ps["pvt"][:, h * 512:(h + 1) * 512], lhsT=sb["c_ones_bf"][0:1, 0:128],
                            rhs=sb["bv_bf"][0:1, h * 512:(h + 1) * 512], start=False, stop=True))
                    s.wait("v", ["sm"])
                    s.op("v", lambda E, j=j: E.tensor_copy(out=sb["vtok"][:, j * 1024:(j + 1) * 1024],
                                                           in_=ps["pvt"][:]))
                for seg in range(2):
                    sg_idx = i * 2 + seg
                    s.wait("t", ["sv"])
                    for t1b in range(2):
                        for ci in range(DCH):
                            s.op("t", lambda E, ci=ci, t1b=t1b, seg=seg: E.matmul(
                                out=ps["psc"][:, t1b * 256:(t1b + 1) * 256],
                                lhsT=sb["qT"][:, ci * TT + seg * 256 + t1b * 128: ci * TT + seg * 256 + (t1b + 1) * 128],
                                rhs=sb["kT"][:, ci * TT + seg * 256: ci * TT + seg * 256 + 256],
                                start=(ci == 0), stop=(ci == DCH - 1)))
                    s.wait("v", ["sm"])
                    for t1b in range(2):
                        s.op("v", lambda E, t1b=t1b: E.reduce_max(
                            out=sb["nm"][:, t1b:t1b + 1], in_=ps["psc"][:, t1b * 256:(t1b + 1) * 256],
                            axis=mybir.AxisListType.X, negate=True))
                        s.op("v", lambda E, t1b=t1b: E.tensor_scalar(
                            out=sb["nm"][:, 2 + t1b:3 + t1b], in0=sb["nm"][:, t1b:t1b + 1],
                            scalar1=L0A_SCALE, scalar2=None, op0=ALU.mult))
                    s.wait("s", ["sv"])
                    for t1b in range(2):
                        s.op("s", lambda E, t1b=t1b: E.activation(
                            out=sb["probs"][:, t1b * 256:(t1b + 1) * 256],
                            in_=ps["psc"][:, t1b * 256:(t1b + 1) * 256],
                            func=AF.Exp, bias=sb["nm"][:, 2 + t1b:3 + t1b], scale=cv[:, 5:6],
                            accum_out=sb["rs"][:, t1b:t1b + 1]))
                    s.wait("v", ["ss"])
                    for t1b in range(2):
                        s.op("v", lambda E, t1b=t1b: E.reciprocal(
                            out=sb["rs"][:, 2 + t1b:3 + t1b], in_=sb["rs"][:, t1b:t1b + 1]))
                        s.op("v", lambda E, t1b=t1b: E.tensor_scalar(
                            out=sb["probs"][:, t1b * 256:(t1b + 1) * 256],
                            in0=sb["probs"][:, t1b * 256:(t1b + 1) * 256],
                            scalar1=sb["rs"][:, 2 + t1b:3 + t1b], scalar2=None, op0=ALU.mult))
                    for t1b in range(2):
                        for t2b in range(2):
                            s.wait("t", ["sv"])
                            s.op("t", lambda E, t1b=t1b, t2b=t2b: E.transpose(
                                out=ps["pT"][:, t1b * 128:(t1b + 1) * 128],
                                in_=sb["probs"][:, t1b * 256 + t2b * 128: t1b * 256 + (t2b + 1) * 128],
                                identity=sb["c_eye"][:]))
                            s.wait("v", ["sm"])
                            s.op("v", lambda E, t1b=t1b, t2b=t2b: E.tensor_copy(
                                out=sb["attnT"][:, t2b * 256 + t1b * 128: t2b * 256 + (t1b + 1) * 128],
                                in_=ps["pT"][:, t1b * 128:(t1b + 1) * 128]))
                    for co in range(DCH):
                        s.wait("t", ["sv"])
                        for t2b in range(2):
                            s.op("t", lambda E, co=co, t2b=t2b, seg=seg: E.matmul(
                                out=ps["rep1"][:, 0:256],
                                lhsT=sb["vtok"][:, (seg * 2 + t2b) * 1024 + co * 128: (seg * 2 + t2b) * 1024 + (co + 1) * 128],
                                rhs=sb["attnT"][:, t2b * 256:(t2b + 1) * 256],
                                start=(t2b == 0), stop=(t2b == 1)))
                        s.wait("v", ["sm"])
                        s.op("v", lambda E, co=co, seg=seg, sg_idx=sg_idx: E.tensor_tensor(
                            out=sb["fused"][:, co * TT + seg * 256: co * TT + seg * 256 + 256],
                            in0=ps["rep1"][:, 0:256],
                            in1=sb["gout"][:, co * 32 + sg_idx: co * 32 + sg_idx + 1].broadcast_to((128, 256)),
                            op=ALU.add))
                if debug and i == 0:
                    s.wait("g", ["sv", "ss", "sm"])
                    s.op("g", lambda E: E.dma_start(out=hb["dbg_fused"], in_=sb["fused"][:]))
                    s.op("g", lambda E: E.dma_start(out=hb["dbg_vtok"], in_=sb["vtok"][:]))
                    s.op("g", lambda E: E.dma_start(out=hb["dbg_probs"], in_=sb["probs"][:]))
                    for e_ in ("v", "s", "t"):
                        s.wait(e_, ["lg"])
                if i >= 2:
                    s.wait_at("v", "lg", marks_lg[i - 2])
                s.wait("t", ["sv", "ss"])
                groups = []
                for co in range(DCH):
                    def mkoo(co):
                        def mm(E, pb, ci):
                            return E.matmul(
                                out=pb[:],
                                lhsT=sb["wc_bf"][:, ci * 1024 + co * 128: ci * 1024 + (co + 1) * 128],
                                rhs=sb["fused"][:, ci * TT:(ci + 1) * TT],
                                start=(ci == 0), stop=False)
                        mms = [lambda E, pb, ci=ci: mm(E, pb, ci) for ci in range(DCH)]
                        mms.append(lambda E, pb: E.matmul(
                            out=pb[:], lhsT=sb["bc_bf"][0:1, co * 128:(co + 1) * 128],
                            rhs=sb["c_ones_bf"][0:1, :], start=False, stop=True))
                        ev = lambda E, pb: E.tensor_tensor(
                            out=osb[:, co * TT:(co + 1) * TT], in0=pb[:],
                            in1=xsb[:, co * TT:(co + 1) * TT], op=ALU.add)
                        return (mms, ev, "v")
                    groups.append(mkoo(co))
                emit_mm_pipe2(s, groups, [ps["pq"], ps["psc"]])
                s.wait("g", ["sv"])
                s.op("g", lambda E, t0=t0, osb=osb: E.dma_start(out=fm_ap(hb["xo"], t0, t0 + TT), in_=sb3(osb[:], TT)))
                marks.append(s.cnt["sv"])
                marks_lg.append(s.cnt["lg"])
            s.wait_at("g", "lg", s.cnt["lg"])

    return body



def build_l0a(Tc=4096, debug=False):
    use_consts = BASE_CONSTS
    nc = bass.Bass("TRN2", target_bir_lowering=False)
    NT = Tc // TT
    hb = {}
    hb["x"] = nc.dram_tensor("x", [D, Tc], F32, kind="ExternalInput").ap()
    hb["summ"] = nc.dram_tensor("summ", [D, 32], F32, kind="ExternalInput").ap()
    for w in ("wq", "wk", "wv", "wc"):
        hb[w] = nc.dram_tensor(w, [D, D], F32, kind="ExternalInput").ap()
    hb["bq_r"] = nc.dram_tensor("bq_r", [1, D], F32, kind="ExternalInput").ap()
    hb["bk_r"] = nc.dram_tensor("bk_r", [1, D], F32, kind="ExternalInput").ap()
    hb["bv_r"] = nc.dram_tensor("bv_r", [1, D], F32, kind="ExternalInput").ap()
    hb["bc_r"] = nc.dram_tensor("bc_r", [1, D], F32, kind="ExternalInput").ap()
    for name, shp, dt in CONST_SPECS:
        if name in use_consts:
            hb[name] = nc.dram_tensor(name, shp, dt, kind="ExternalInput").ap()
    hb["xo"] = nc.dram_tensor("xo", [D, Tc], F32, kind="ExternalOutput").ap()
    if debug:
        hb["dbg_shat"] = nc.dram_tensor("dbg_shat", [128, DCH * 32], F32, kind="ExternalOutput").ap()
        hb["dbg_rows"] = nc.dram_tensor("dbg_rows", [2, 512], F32, kind="ExternalOutput").ap()
        hb["dbg_xhat"] = nc.dram_tensor("dbg_xhat", [128, DCH * TT], F32, kind="ExternalOutput").ap()
        hb["dbg_qT"] = nc.dram_tensor("dbg_qT", [128, DCH * TT], F32, kind="ExternalOutput").ap()
        hb["dbg_gout"] = nc.dram_tensor("dbg_gout", [128, DCH * 32], F32, kind="ExternalOutput").ap()
        hb["dbg_fused"] = nc.dram_tensor("dbg_fused", [128, DCH * TT], F32, kind="ExternalOutput").ap()
        hb["dbg_vtok"] = nc.dram_tensor("dbg_vtok", [128, 4 * 1024], F32, kind="ExternalOutput").ap()
        hb["dbg_probs"] = nc.dram_tensor("dbg_probs", [128, 512], F32, kind="ExternalOutput").ap()

    body = setup_l0a(nc, hb, "", Tc, debug)
    with contextlib.ExitStack() as c2:
        run_sched(nc, c2, body)
    return nc

def l0a_inmaps(ins, consts):
    consts = consts_subset(consts, BASE_CONSTS)
    x = np.asarray(ins["L0_attn_x"])
    maps = []
    for c in range(NCORE):
        b, h = c // 2, c % 2
        m = dict(consts)
        m["x"] = host_fm(x[b, h * 4096:(h + 1) * 4096])
        summ = x[b, ::256, :]
        m["summ"] = host_fm(np.roll(summ, -16 * h, axis=0))
        m["wq"] = np.ascontiguousarray(np.asarray(ins["L0_attn_wq"]).T)
        m["wk"] = np.ascontiguousarray(np.asarray(ins["L0_attn_wk"]).T)
        m["wv"] = np.ascontiguousarray(np.asarray(ins["L0_attn_wv"]).T)
        m["wc"] = np.ascontiguousarray(np.asarray(ins["L0_attn_wc"]).T)
        m["bq_r"] = host_row(ins["L0_attn_bq"])
        m["bk_r"] = host_row(ins["L0_attn_bk"])
        m["bv_r"] = host_row(ins["L0_attn_bv"])
        m["bc_r"] = host_row(ins["L0_attn_bc"])
        maps.append(m)
    return maps


def gather_fm(results, key="xo"):
    out = np.empty((4, 8192, D), dtype=np.float32)
    for c in range(NCORE):
        b, h = c // 2, c % 2
        out[b, h * 4096:(h + 1) * 4096] = np.asarray(results[c][key]).T
    return out



HCH = 32


def _stage_matrix(w_s, s):
    st = 2 ** s
    S = np.zeros((4096, 4096), dtype=np.float32)
    k = np.arange(2048)
    n = k // st
    j = k % st
    ia = n * 2 * st + j
    ib = ia + st
    S[ia, ia] = w_s[:, 0, 0]
    S[ia, ib] = w_s[:, 0, 1]
    S[ib, ia] = w_s[:, 1, 0]
    S[ib, ib] = w_s[:, 1, 1]
    return S


def _rmaj_perm():
    ht = np.arange(4096)
    return (ht % 64) * 64 + (ht // 64)


def _tiles_of(S):
    out = np.zeros((32, 128, 128), dtype=np.float32)
    for T in range(32):
        out[T] = S[T * 128:(T + 1) * 128, T * 128:(T + 1) * 128]
    return out


def l0f_stage_tiles(up_w, dn_w):
    perm = _rmaj_perm()
    stmU = np.zeros((12, 32, 128, 128), dtype=np.float32)
    stmD = np.zeros((12, 32, 128, 128), dtype=np.float32)
    for s in range(12):
        Su = _stage_matrix(np.asarray(up_w[s]), s)
        Sd = _stage_matrix(np.asarray(dn_w[s]), s)
        if s >= 6:
            Su = Su[np.ix_(perm, perm)]
            Sd = Sd[np.ix_(perm, perm)]
        stmU[s] = _tiles_of(Su.T)
        stmD[s] = _tiles_of(Sd)
    return stmU.reshape(12 * 32 * 128, 128), stmD.reshape(12 * 32 * 128, 128)


def setup_l0f(nc, hb, prefix="", Tc=4096):
    TL = 256
    NT = Tc // TL
    ctx = contextlib.ExitStack()
    with ctx:
        sb = {}

        def S(name, shape, dt=F32):
            sb[name] = ctx.enter_context(nc.sbuf_tensor("sb_" + prefix + name, shape, dt))
            return sb[name]

        for name, shp, dt in CONST_SPECS:
            if name in hb:
                S(name, shp, dt)
        S("wbuf", [128, HCH * 1025], BF16)
        S("wexp", [128, DCH * 4096], BF16)
        S("row0", [1, TL]); S("row1", [1, TL])
        S("bup_bf", [1, 4096], BF16)
        S("ubias_bf", [1, 4096], BF16)
        S("bcon_bf", [1, D], BF16); S("dbias_bf", [1, D], BF16)
        tmp = contextlib.ExitStack()
        for nm, shp, dt in [("wstg0", [128, 1025], F32), ("wstg1", [128, 1025], F32),
                            ("stg0", [128, 32 * 128], BF16), ("stg1", [128, 32 * 128], BF16)]:
            sb[nm] = tmp.enter_context(nc.sbuf_tensor("sb_" + nm, shp, dt))
        tmp.close()
        S("x0", [128, DCH * TL])
        S("xhat", [128, DCH * TL], BF16)
        S("sq", [128, DCH * TT], BF16)
        S("g", [128, HCH * TL], BF16)

        ps = {}
        for name, shape, dt in [("stat", [128, TT], F32), ("rep0", [128, TT], F32),
                                ("rep1", [128, TT], F32), ("pq", [128, TT], F32),
                                ("pw", [128, 1024], F32), ("pw1", [128, 2], F32),
                                ("pT", [128, 128], BF16)]:
            ps[name] = ctx.enter_context(nc.psum_tensor(prefix + name, shape, dt))

        def body(s):
            emit_load_consts(s, hb, sb)
            row_jobs = [(hb["ubias_p"][k:k + 1, :], sb["ubias_bf"][0:1, k * 1024:(k + 1) * 1024])
                        for k in range(4)]
            row_jobs += [(hb["bcon_r"], sb["bcon_bf"][0:1, :]), (hb["dbias_r"], sb["dbias_bf"][0:1, :])]
            for src, dst in row_jobs:
                s.wait("q", ["sv"])
                s.op("q", lambda E, sr=src: E.dma_start(out=sb["wstg0"][0:1, 0:1024], in_=sr))
                s.wait("v", ["ld"])
                s.op("v", lambda E, d=dst: E.tensor_copy(out=d, in_=sb["wstg0"][0:1, 0:1024]))

            def cast_into_wbuf(hbm, ncols):
                chunks = [hbm[T * 128:(T + 1) * 128, :] for T in range(HCH)]
                dsts = [sb["wbuf"][:, T * 1025: T * 1025 + ncols] for T in range(HCH)]
                emit_cast_weight(s, chunks, [sb["wstg0"], sb["wstg1"]], dsts)

            def load_stage_tiles(sidx, down, stgbuf):
                src = hb["stmD"] if down else hb["stmU"]
                for half in range(8):
                    chunk = src[(sidx * 32 + half * 4) * 128: (sidx * 32 + half * 4 + 4) * 128, :]
                    s.wait("q", ["sv"])
                    s.op("q", lambda E, c=chunk, h=half: E.dma_start(
                        out=sb["wstg0" if h % 2 == 0 else "wstg1"][:, 0:512].rearrange("p (T c) -> p T c", T=4),
                        in_=c.rearrange("(T p) c -> p T c", p=128)))
                    s.wait_at("v", "ld", s.cnt["ld"])
                    s.op("v", lambda E, h=half, d=stgbuf: E.tensor_copy(
                        out=d[:, h * 512:(h + 1) * 512],
                        in_=sb["wstg0" if h % 2 == 0 else "wstg1"][:, 0:512]))

            def stage_apply(sidx, down, ncols):
                stgbuf = sb["stg0"] if sidx % 2 == 0 else sb["stg1"]
                load_stage_tiles(sidx, down, stgbuf)
                for T in range(HCH):
                    s.wait("t", ["sv", "ss"])
                    nmm = (ncols + 511) // 512
                    for m in range(nmm):
                        c0, c1 = m * 512, min(ncols, (m + 1) * 512)
                        s.op("t", lambda E, T=T, c0=c0, c1=c1, d=stgbuf: E.matmul(
                            out=ps["pw"][:, c0:c1] if c1 - c0 > 2 else ps["pw1"][:, 0:c1 - c0],
                            lhsT=d[:, T * 128:(T + 1) * 128],
                            rhs=sb["wbuf"][:, T * 1025 + c0: T * 1025 + c1],
                            start=True, stop=True))
                    s.wait("v", ["sm"])
                    for m in range(nmm):
                        c0, c1 = m * 512, min(ncols, (m + 1) * 512)
                        s.op("v", lambda E, T=T, c0=c0, c1=c1: E.tensor_copy(
                            out=sb["wbuf"][:, T * 1025 + c0: T * 1025 + c1],
                            in_=ps["pw"][:, c0:c1] if c1 - c0 > 2 else ps["pw1"][:, 0:c1 - c0]))

            def bounce(ncols, to_rmaj):
                s.wait("q", ["sv", "sm", "ss"])
                for T in range(HCH):
                    s.op("q", lambda E, T=T: E.dma_start(
                        out=hb["scr"][T * 128:(T + 1) * 128, 0:ncols],
                        in_=sb["wbuf"][:, T * 1025: T * 1025 + ncols]))
                src = hb["scr"].rearrange("(X y) c -> y X c", X=64)
                s.wait_at("q", "ld", s.cnt["ld"])
                for T in range(HCH):
                    for rr in range(2):
                        s.op("q", lambda E, T=T, rr=rr: E.dma_start(
                            out=sb["wbuf"][rr * 64:(rr + 1) * 64, T * 1025: T * 1025 + ncols],
                            in_=src[2 * T + rr, :, 0:ncols]))
                s.wait("t", ["ld"]); s.wait("v", ["ld"]); s.wait("s", ["ld"])

            cast_into_wbuf(hb["w1i"], 1025)
            for sidx in range(6):
                stage_apply(sidx, False, 1025)
            bounce(1025, True)
            for sidx in range(6, 12):
                stage_apply(sidx, False, 1025)
            for T in range(HCH):
                for co in range(DCH):
                    s.wait("t", ["sv", "ss"])
                    s.op("t", lambda E, T=T, co=co: E.transpose(
                        out=ps["pT"][:], in_=sb["wbuf"][:, T * 1025 + co * 128: T * 1025 + (co + 1) * 128],
                        identity=sb["c_eye"][:]))
                    s.wait("v", ["sm"])
                    s.op("v", lambda E, T=T, co=co: E.tensor_copy(
                        out=sb["wexp"][:, co * 4096 + T * 128: co * 4096 + (T + 1) * 128],
                        in_=ps["pT"][:]))
                s.wait("t", ["sv", "ss"])
                s.op("t", lambda E, T=T: E.transpose(
                    out=ps["pT"][0:1, :], in_=sb["wbuf"][:, T * 1025 + 1024: T * 1025 + 1025],
                    identity=sb["c_eye"][:]))
                s.wait("v", ["sm"])
                s.op("v", lambda E, T=T: E.tensor_copy(out=sb["bup_bf"][0:1, T * 128:(T + 1) * 128],
                                                       in_=ps["pT"][0:1, :]))
            s.wait("v", ["ld"])
            s.op("v", lambda E: E.tensor_add(out=sb["bup_bf"][:], in0=sb["bup_bf"][:], in1=sb["ubias_bf"][:]))

            s.wait("v", ["sm", "ss"])
            cast_into_wbuf(hb["w2i"], 1024)
            bounce(1024, True)
            for sidx in range(11, 5, -1):
                stage_apply(sidx, True, 1024)
            bounce(1024, False)
            for sidx in range(5, -1, -1):
                stage_apply(sidx, True, 1024)
            bounce(1024, True)

            marks = []
            marks_lg = []
            for i in range(NT):
                t0 = i * TL
                xsb = sb["x0"]
                if i >= 1:
                    s.wait_at("q", "sv", marks[i - 1])
                    s.wait_at("q", "lg", marks_lg[i - 1])
                s.op("q", lambda E, t0=t0, xsb=xsb: E.dma_start(out=sb3(xsb[:], TL), in_=fm_ap(hb["x"], t0, t0 + TL)))
                s.wait("s", ["ld", "sm", "sv"])
                s.wait("v", ["ld"])
                emit_ln(s, ps, sb, xsb[:], sb["xhat"][:], sb["sq"][:], TL)
                s.wait("t", ["sv", "ss"])
                groups = []
                for co in range(HCH):
                    def mku(co):
                        def mm(E, pb, ci):
                            return E.matmul(
                                out=pb[:, 0:TL],
                                lhsT=sb["wexp"][:, ci * 4096 + co * 128: ci * 4096 + (co + 1) * 128],
                                rhs=sb["xhat"][:, ci * TL:(ci + 1) * TL],
                                start=(ci == 0), stop=False)
                        mms = [lambda E, pb, ci=ci: mm(E, pb, ci) for ci in range(DCH)]
                        mms.append(lambda E, pb: E.matmul(
                            out=pb[:, 0:TL], lhsT=sb["bup_bf"][0:1, co * 128:(co + 1) * 128],
                            rhs=sb["c_ones_bf"][0:1, 0:TL], start=False, stop=True))
                        ev = lambda E, pb: E.activation(
                            out=sb["g"][:, co * TL:(co + 1) * TL], in_=pb[:, 0:TL], func=AF.Gelu)
                        return (mms, ev, "s")
                    groups.append(mku(co))
                emit_mm_pipe2(s, groups, [ps["pq"], ps["rep0"], ps["rep1"]])
                s.wait("t", ["ss", "sv", "ld"])
                groups = []
                for co in range(DCH):
                    def mkd(co):
                        def mm(E, pb, ci):
                            return E.matmul(
                                out=pb[:, 0:TL],
                                lhsT=sb["wbuf"][:, ci * 1025 + co * 128: ci * 1025 + (co + 1) * 128],
                                rhs=sb["g"][:, ci * TL:(ci + 1) * TL],
                                start=(ci == 0), stop=False)
                        mms = [lambda E, pb, ci=ci: mm(E, pb, ci) for ci in range(HCH)]
                        mms.append(lambda E, pb: E.matmul(
                            out=pb[:, 0:TL], lhsT=sb["bcon_bf"][0:1, co * 128:(co + 1) * 128],
                            rhs=sb["c_ones_bf"][0:1, 0:TL], start=False, stop=False))
                        mms.append(lambda E, pb: E.matmul(
                            out=pb[:, 0:TL], lhsT=sb["dbias_bf"][0:1, co * 128:(co + 1) * 128],
                            rhs=sb["c_ones_bf"][0:1, 0:TL], start=False, stop=True))
                        ev = lambda E, pb: E.tensor_tensor(
                            out=xsb[:, co * TL:(co + 1) * TL], in0=pb[:, 0:TL],
                            in1=xsb[:, co * TL:(co + 1) * TL], op=ALU.add)
                        return (mms, ev, "v")
                    groups.append(mkd(co))
                emit_mm_pipe2(s, groups, [ps["pq"], ps["rep0"]])
                s.wait("g", ["sv"])
                s.op("g", lambda E, t0=t0, xsb=xsb: E.dma_start(out=fm_ap(hb["xo"], t0, t0 + TL), in_=sb3(xsb[:], TL)))
                marks.append(s.cnt["sv"])
                marks_lg.append(s.cnt["lg"])
            s.wait_at("g", "lg", s.cnt["lg"])

    return body



def build_l0f(Tc=4096):
    use_consts = BASE_CONSTS
    nc = bass.Bass("TRN2", target_bir_lowering=False)
    TL = 256
    NT = Tc // TL
    hb = {}
    hb["x"] = nc.dram_tensor("x", [D, Tc], F32, kind="ExternalInput").ap()
    hb["w1i"] = nc.dram_tensor("w1i", [4096, 1025], F32, kind="ExternalInput").ap()
    hb["w2i"] = nc.dram_tensor("w2i", [4096, 1024], F32, kind="ExternalInput").ap()
    hb["stmU"] = nc.dram_tensor("stmU", [12 * 32 * 128, 128], F32, kind="ExternalInput").ap()
    hb["stmD"] = nc.dram_tensor("stmD", [12 * 32 * 128, 128], F32, kind="ExternalInput").ap()
    hb["ubias_p"] = nc.dram_tensor("ubias_p", [4, 1024], F32, kind="ExternalInput").ap()
    hb["bcon_r"] = nc.dram_tensor("bcon_r", [1, D], F32, kind="ExternalInput").ap()
    hb["dbias_r"] = nc.dram_tensor("dbias_r", [1, D], F32, kind="ExternalInput").ap()
    for name, shp, dt in CONST_SPECS:
        if name in use_consts:
            hb[name] = nc.dram_tensor(name, shp, dt, kind="ExternalInput").ap()
    hb["xo"] = nc.dram_tensor("xo", [D, Tc], F32, kind="ExternalOutput").ap()
    hb["scr"] = nc.dram_tensor("scr", [4096, 1025], BF16, kind="Internal").ap()

    body = setup_l0f(nc, hb, "", Tc)
    with contextlib.ExitStack() as c2:
        run_sched(nc, c2, body)
    return nc

def l0f_inmaps(ins, consts):
    consts = consts_subset(consts, BASE_CONSTS)
    stmU, stmD = l0f_stage_tiles(ins["L0_ffn_up_weights"], ins["L0_ffn_down_weights"])
    perm = _rmaj_perm()
    w1i = np.concatenate([np.asarray(ins["L0_ffn_W_exp"]).T,
                          np.asarray(ins["L0_ffn_b_exp"]).reshape(-1, 1)], axis=1)
    w1i = np.ascontiguousarray(w1i.astype(np.float32))
    w2i = np.ascontiguousarray(np.asarray(ins["L0_ffn_W_con"]).astype(np.float32))
    ub_p = np.ascontiguousarray(np.asarray(ins["L0_ffn_up_bias"])[perm].reshape(1, -1))
    maps = []
    for c in range(NCORE):
        m = dict(consts)
        m["w1i"] = w1i
        m["w2i"] = w2i
        m["stmU"] = stmU
        m["stmD"] = stmD
        m["ubias_p"] = ub_p.reshape(4, 1024)
        m["bcon_r"] = host_row(ins["L0_ffn_b_con"])
        m["dbias_r"] = host_row(ins["L0_ffn_down_bias"])
        maps.append(m)
    return maps



def setup_l1f(nc, hb, prefix="", Tc=4096):
    HC = 24
    NT = Tc // TT
    ctx = contextlib.ExitStack()
    with ctx:
        sb = {}

        def S(name, shape, dt=F32):
            sb[name] = ctx.enter_context(nc.sbuf_tensor("sb_" + prefix + name, shape, dt))
            return sb[name]

        for name, shp, dt in CONST_SPECS:
            if name in hb:
                S(name, shp, dt)
        S("wg", [128, DCH * 3072], BF16)
        S("wu", [128, DCH * 3072], BF16)
        S("wd", [128, HC * 1024], BF16)
        S("wstg0", [128, 1024], I32); S("wstg1", [128, 1024], I32)
        S("x0", [128, DCH * TT])
        S("xhat", [128, DCH * TT], BF16)
        S("hbuf", [128, HC * TT], BF16)
        sb["sq"] = sb["hbuf"]
        S("sg0", [128, TT], BF16)
        S("sg1", [128, TT], BF16)
        S("row0", [1, TT]); S("row1", [1, TT])

        ps = {}
        for name, shape, dt in [("stat", [128, TT], F32), ("rep0", [128, TT], F32),
                                ("rep1", [128, TT], F32), ("pq", [128, TT], F32),
                                ("pu", [128, TT], F32)]:
            ps[name] = ctx.enter_context(nc.psum_tensor(prefix + name, shape, dt))

        def body(s):
            emit_load_consts(s, hb, sb)
            chunks, dsts = [], []
            for w, hbk, nch, ncol in (("wg", "wgT", DCH, 3072), ("wu", "wuT", DCH, 3072), ("wd", "wdT", HC, 1024)):
                for ci in range(nch):
                    for half in range(ncol // 1024):
                        chunks.append(hb[hbk][ci * 128:(ci + 1) * 128, half * 1024:(half + 1) * 1024])
                        dsts.append(sb[w][:, ci * ncol + half * 1024: ci * ncol + (half + 1) * 1024])
            emit_cast_weight(s, chunks, [sb["wstg0"], sb["wstg1"]], dsts)

            cv = sb["c_cvec"]
            marks = []
            marks_lg = []
            for i in range(NT):
                t0 = i * TT
                xsb = sb["x0"]
                if i >= 1:
                    s.wait_at("q", "sv", marks[i - 1])
                    s.wait_at("q", "lg", marks_lg[i - 1])
                s.op("q", lambda E, t0=t0: E.dma_start(out=sb3(sb["x0"][:], TT), in_=fm_ap(hb["x"], t0, t0 + TT)))
                s.wait("s", ["ld", "sm", "sv"])
                s.wait("v", ["ld"])
                emit_ln(s, ps, sb, xsb[:], sb["xhat"][:], sb["sq"][:], TT)
                s.wait("t", ["sv", "ss"])
                groups = []
                for co in range(HC):
                    def mkg(co, w, par):
                        def mm(E, pb, ci):
                            return E.matmul(
                                out=pb[:],
                                lhsT=sb[w][:, ci * 3072 + co * 128: ci * 3072 + (co + 1) * 128],
                                rhs=sb["xhat"][:, ci * TT:(ci + 1) * TT],
                                start=(ci == 0), stop=(ci == DCH - 1))
                        mms = [lambda E, pb, ci=ci: mm(E, pb, ci) for ci in range(DCH)]
                        if w == "wg":
                            ev = lambda E, pb: E.activation(out=sb["sg" + str(par)][:], in_=pb[:],
                                                            func=AF.Silu, bias=cv[:, 3:4], scale=cv[:, 6:7])
                            return (mms, ev, "s")
                        ev = lambda E, pb: E.scalar_tensor_tensor(
                            out=sb["hbuf"][:, co * TT:(co + 1) * TT], in0=pb[:],
                            scalar=1.0 / 127.0, in1=sb["sg" + str(par)][:], op0=ALU.mult, op1=ALU.mult)
                        return (mms, ev, "v", ["ss"])
                    groups.append(mkg(co, "wg", co % 2))
                    groups.append(mkg(co, "wu", co % 2))
                emit_mm_pipe2(s, groups, [ps["pq"], ps["pu"], ps["stat"], ps["rep0"]])
                groups = []
                for co in range(DCH):
                    def mkd(co):
                        def mm(E, pb, ci):
                            return E.matmul(
                                out=pb[:],
                                lhsT=sb["wd"][:, ci * 1024 + co * 128: ci * 1024 + (co + 1) * 128],
                                rhs=sb["hbuf"][:, ci * TT:(ci + 1) * TT],
                                start=(ci == 0), stop=(ci == HC - 1))
                        mms = [lambda E, pb, ci=ci: mm(E, pb, ci) for ci in range(HC)]
                        ev = lambda E, pb: E.scalar_tensor_tensor(
                            out=xsb[:, co * TT:(co + 1) * TT], in0=pb[:],
                            scalar=1.0 / 127.0, in1=xsb[:, co * TT:(co + 1) * TT],
                            op0=ALU.mult, op1=ALU.add)
                        return (mms, ev, "v")
                    groups.append(mkd(co))
                emit_mm_pipe2(s, groups, [ps["pq"], ps["pu"]])
                s.wait("g", ["sv"])
                s.op("g", lambda E, t0=t0, xsb=xsb: E.dma_start(out=fm_ap(hb["xo"], t0, t0 + TT), in_=sb3(xsb[:], TT)))
                marks.append(s.cnt["sv"])
                marks_lg.append(s.cnt["lg"])
            s.wait_at("g", "lg", s.cnt["lg"])

    return body



def build_l1f(Tc=4096):
    use_consts = BASE_CONSTS
    nc = bass.Bass("TRN2", target_bir_lowering=False)
    NT = Tc // TT
    HC = 24
    hb = {}
    hb["x"] = nc.dram_tensor("x", [D, Tc], F32, kind="ExternalInput").ap()
    hb["wgT"] = nc.dram_tensor("wgT", [D, 3072], I32, kind="ExternalInput").ap()
    hb["wuT"] = nc.dram_tensor("wuT", [D, 3072], I32, kind="ExternalInput").ap()
    hb["wdT"] = nc.dram_tensor("wdT", [3072, D], I32, kind="ExternalInput").ap()
    for name, shp, dt in CONST_SPECS:
        if name in use_consts:
            hb[name] = nc.dram_tensor(name, shp, dt, kind="ExternalInput").ap()
    hb["xo"] = nc.dram_tensor("xo", [D, Tc], F32, kind="ExternalOutput").ap()

    body = setup_l1f(nc, hb, "", Tc)
    with contextlib.ExitStack() as c2:
        run_sched(nc, c2, body)
    return nc

def l1f_inmaps(ins, consts):
    consts = consts_subset(consts, BASE_CONSTS)
    wgT = np.ascontiguousarray(np.asarray(ins["L1_ffn_w_gate"]).T)
    wuT = np.ascontiguousarray(np.asarray(ins["L1_ffn_w_up"]).T)
    wdT = np.ascontiguousarray(np.asarray(ins["L1_ffn_w_down"]).T)
    maps = []
    for c in range(NCORE):
        m = dict(consts)
        m["wgT"] = wgT
        m["wuT"] = wuT
        m["wdT"] = wdT
        maps.append(m)
    return maps



def setup_l2f(nc, hb, prefix="", Tc=4096, add_parts=False):
    NT = Tc // TT
    ctx = contextlib.ExitStack()
    with ctx:
        sb = {}

        def S(name, shape, dt=F32):
            sb[name] = ctx.enter_context(nc.sbuf_tensor("sb_" + prefix + name, shape, dt))
            return sb[name]

        for name, shp, dt in CONST_SPECS:
            if name in hb:
                S(name, shp, dt)
        for w in ("wr", "wk", "wv"):
            S(w, [128, DCH * 1024], BF16)
        S("wstg0", [128, 1024], F32); S("wstg1", [128, 1024], F32)
        S("x0", [128, DCH * TT])
        S("xhat", [128, DCH * TT], BF16)
        S("sq", [128, DCH * TT], BF16)
        S("ext", [128, DCH * (TT + 1)], BF16)
        S("xrk", [128, DCH * TT], BF16)
        S("rbuf", [128, DCH * TT], BF16)
        S("kact", [128, DCH * TT], BF16)
        S("mur", [128, 8]); S("muk", [128, 8])
        S("omr", [128, 8]); S("omk", [128, 8])
        S("hx", [128, DCH])
        S("hx2", [128, DCH]); S("hx3", [128, DCH])
        S("px", [128, DCH * TT])
        S("kvb", [128, TT], BF16)
        S("gv", [128, TT], F32)
        S("row0", [1, TT]); S("row1", [1, TT])

        ps = {}
        for name, shape, dt in [("stat", [128, TT], F32), ("rep0", [128, TT], F32),
                                ("rep1", [128, TT], F32), ("pq", [128, TT], F32)]:
            ps[name] = ctx.enter_context(nc.psum_tensor(prefix + name, shape, dt))

        def body(s):
            emit_load_consts(s, hb, sb)
            for small in ("mur", "muk"):
                s.op("q", lambda E, d=sb[small], sr=hb[small]: E.dma_start(out=d[:], in_=sr))
            s.op("q", lambda E: E.dma_start(out=sb["hx"][:], in_=hb["xhalo"]))
            if add_parts == 2:
                s.op("q", lambda E: E.dma_start(out=sb["hx2"][:], in_=hb["hh0"]))
                s.op("q", lambda E: E.dma_start(out=sb["hx3"][:], in_=hb["hh1"]))
                s.wait("v", ["ld"])
                s.op("v", lambda E: E.tensor_tensor(out=sb["hx"][:], in0=sb["hx"][:], in1=sb["hx2"][:], op=ALU.add))
                s.op("v", lambda E: E.tensor_tensor(out=sb["hx"][:], in0=sb["hx"][:], in1=sb["hx3"][:], op=ALU.add))
            elif add_parts == 1:
                s.op("q", lambda E: E.dma_start(out=sb["hx2"][:], in_=hb["hh0"]))
                s.op("q", lambda E: E.dma_start(out=sb["hx3"][:], in_=hb["hflag"]))
                s.wait("v", ["ld"])
                s.op("v", lambda E: E.tensor_tensor(out=sb["hx"][:], in0=sb["hx2"][:], in1=sb["hx3"][:], op=ALU.mult))
            chunks, dsts = [], []
            for w in ("wr", "wk", "wv"):
                for ci in range(DCH):
                    chunks.append(hb[w + "T"][ci * 128:(ci + 1) * 128, :])
                    dsts.append(sb[w][:, ci * 1024:(ci + 1) * 1024])
            emit_cast_weight(s, chunks, [sb["wstg0"], sb["wstg1"]], dsts)
            s.wait("v", ["ld"])
            s.op("v", lambda E: E.scalar_tensor_tensor(out=sb["omr"][:], in0=sb["mur"][:],
                                                       scalar=-1.0, in1=sb["c_cvec"][:, 2:3].broadcast_to((128, 8)),
                                                       op0=ALU.mult, op1=ALU.add))
            s.op("v", lambda E: E.scalar_tensor_tensor(out=sb["omk"][:], in0=sb["muk"][:],
                                                       scalar=-1.0, in1=sb["c_cvec"][:, 2:3].broadcast_to((128, 8)),
                                                       op0=ALU.mult, op1=ALU.add))
            s.wait("s", ["ld"])
            emit_ln(s, ps, sb, sb["hx"][:], sb["sq"][:, 0:DCH], sb["sq"][:, DCH:2 * DCH], 1)
            s.wait("v", ["ld", "ss", "sm"])
            for c in range(DCH):
                s.op("v", lambda E, c=c: E.tensor_copy(
                    out=sb["ext"][:, c * (TT + 1): c * (TT + 1) + 1],
                    in_=sb["sq"][:, c:c + 1]))
            marks = []
            marks_lg = []
            for i in range(NT):
                t0 = i * TT
                xsb = sb["x0"]
                if i >= 1:
                    s.wait_at("q", "sv", marks[i - 1])
                    s.wait_at("q", "lg", marks_lg[i - 1])
                s.op("q", lambda E, t0=t0: E.dma_start(out=sb3(sb["x0"][:], TT), in_=fm_ap(hb["x"], t0, t0 + TT)))
                if add_parts:
                    for pk in (("pp0", "pp1") if add_parts == 2 else ("pp0",)):
                        s.op("q", lambda E, t0=t0, pk=pk: E.dma_start(out=sb3(sb["px"][:], TT), in_=(hb[pk](t0, t0 + TT) if callable(hb[pk]) else fm_ap(hb[pk], t0, t0 + TT))))
                        s.wait("v", ["ld", "sm", "ss"])
                        for c in range(DCH):
                            s.op("v", lambda E, c=c: E.tensor_tensor(
                                out=sb["x0"][:, c * TT:(c + 1) * TT], in0=sb["x0"][:, c * TT:(c + 1) * TT],
                                in1=sb["px"][:, c * TT:(c + 1) * TT], op=ALU.add))
                s.wait("s", ["ld", "sm", "sv"])
                s.wait("v", ["ld"])
                emit_ln(s, ps, sb, xsb[:], sb["xhat"][:], sb["sq"][:], TT)
                s.wait("v", ["sm", "ss"])
                for c in range(DCH):
                    s.op("v", lambda E, c=c: E.tensor_copy(
                        out=sb["ext"][:, c * (TT + 1) + 1: (c + 1) * (TT + 1)],
                        in_=sb["xhat"][:, c * TT:(c + 1) * TT]))
                for which, mu, om, wname, dst in (("r", "mur", "omr", "wr", "rbuf"), ("k", "muk", "omk", "wk", "kact")):
                    s.wait("v", ["sm"])
                    for c in range(DCH):
                        s.op("v", lambda E, c=c, mu=mu: E.tensor_scalar(
                            out=sb["xrk"][:, c * TT:(c + 1) * TT], in0=sb["xhat"][:, c * TT:(c + 1) * TT],
                            scalar1=sb[mu][:, c:c + 1], scalar2=None, op0=ALU.mult))
                    for c in range(DCH):
                        s.op("v", lambda E, c=c, om=om: E.scalar_tensor_tensor(
                            out=sb["xrk"][:, c * TT:(c + 1) * TT],
                            in0=sb["ext"][:, c * (TT + 1): c * (TT + 1) + TT],
                            scalar=sb[om][:, c:c + 1],
                            in1=sb["xrk"][:, c * TT:(c + 1) * TT], op0=ALU.mult, op1=ALU.add))
                    s.wait("t", ["sv", "ss"])
                    groups = []
                    for co in range(DCH):
                        def mkr(co, w, d, fn):
                            def mm(E, pb, ci):
                                return E.matmul(
                                    out=pb[:],
                                    lhsT=sb[w][:, ci * 1024 + co * 128: ci * 1024 + (co + 1) * 128],
                                    rhs=sb["xrk"][:, ci * TT:(ci + 1) * TT],
                                    start=(ci == 0), stop=(ci == DCH - 1))
                            mms = [lambda E, pb, ci=ci: mm(E, pb, ci) for ci in range(DCH)]
                            ev = lambda E, pb: E.activation(
                                out=sb[d][:, co * TT:(co + 1) * TT], in_=pb[:], func=fn)
                            return (mms, ev, "s")
                        groups.append(mkr(co, wname, dst, AF.Sigmoid if which == "r" else AF.Relu))
                    emit_mm_pipe2(s, groups, [ps["pq"], ps["rep0"]])
                s.wait("s", ["sv"])
                for c in range(DCH):
                    s.op("s", lambda E, c=c: E.activation(
                        out=sb["kact"][:, c * TT:(c + 1) * TT], in_=sb["kact"][:, c * TT:(c + 1) * TT],
                        func=AF.Square))
                s.wait("v", ["ss"])
                for c in range(DCH):
                    s.op("v", lambda E, c=c: E.tensor_copy(
                        out=sb["ext"][:, c * (TT + 1): c * (TT + 1) + 1],
                        in_=sb["xhat"][:, (c + 1) * TT - 1: (c + 1) * TT]))
                for co in range(DCH):
                    s.wait("t", ["sv", "ss"])
                    for ci in range(DCH):
                        s.op("t", lambda E, ci=ci, co=co: E.matmul(
                            out=ps["pq"][:],
                            lhsT=sb["wv"][:, ci * 1024 + co * 128: ci * 1024 + (co + 1) * 128],
                            rhs=sb["kact"][:, ci * TT:(ci + 1) * TT],
                            start=(ci == 0), stop=(ci == DCH - 1)))
                    s.wait("s", ["sm"])
                    s.op("s", lambda E: E.copy(out=sb["kvb"][:], in_=ps["pq"][:]))
                    s.wait("v", ["ss"])
                    s.op("v", lambda E, co=co: E.tensor_tensor(
                        out=sb["gv"][:], in0=sb["kvb"][:],
                        in1=sb["rbuf"][:, co * TT:(co + 1) * TT], op=ALU.mult))
                    s.op("v", lambda E, co=co, xsb=xsb: E.tensor_tensor(
                        out=xsb[:, co * TT:(co + 1) * TT], in0=sb["gv"][:],
                        in1=xsb[:, co * TT:(co + 1) * TT], op=ALU.add))
                s.wait("g", ["sv"])
                s.op("g", lambda E, t0=t0, xsb=xsb: E.dma_start(out=fm_ap(hb["xo"], t0, t0 + TT), in_=sb3(xsb[:], TT)))
                marks.append(s.cnt["sv"])
                marks_lg.append(s.cnt["lg"])
            s.wait_at("g", "lg", s.cnt["lg"])

    return body



def build_l2f(Tc=4096, add_parts=False):
    use_consts = BASE_CONSTS
    nc = bass.Bass("TRN2", target_bir_lowering=False)
    NT = Tc // TT
    hb = {}
    hb["x"] = nc.dram_tensor("x", [D, Tc], F32, kind="ExternalInput").ap()
    hb["xhalo"] = nc.dram_tensor("xhalo", [128, 8], F32, kind="ExternalInput").ap()
    if add_parts:
        hb["pp0"] = nc.dram_tensor("pp0", [D, Tc], F32, kind="ExternalInput").ap()
        hb["pp1"] = nc.dram_tensor("pp1", [D, Tc], F32, kind="ExternalInput").ap()
        hb["hh0"] = nc.dram_tensor("hh0", [128, 8], F32, kind="ExternalInput").ap()
        hb["hh1"] = nc.dram_tensor("hh1", [128, 8], F32, kind="ExternalInput").ap()
    for w in ("wrT", "wkT", "wvT"):
        hb[w] = nc.dram_tensor(w, [D, D], F32, kind="ExternalInput").ap()
    hb["mur"] = nc.dram_tensor("mur", [128, 8], F32, kind="ExternalInput").ap()
    hb["muk"] = nc.dram_tensor("muk", [128, 8], F32, kind="ExternalInput").ap()
    for name, shp, dt in CONST_SPECS:
        if name in use_consts:
            hb[name] = nc.dram_tensor(name, shp, dt, kind="ExternalInput").ap()
    hb["xo"] = nc.dram_tensor("xo", [D, Tc], F32, kind="ExternalOutput").ap()

    body = setup_l2f(nc, hb, "", Tc, add_parts)
    with contextlib.ExitStack() as c2:
        run_sched(nc, c2, body)
    return nc

def l2f_inmaps(ins, consts, x_prev_full=None):
    consts = consts_subset(consts, BASE_CONSTS)
    maps = []
    for c in range(NCORE):
        b, h = c // 2, c % 2
        m = dict(consts)
        m["wrT"] = np.ascontiguousarray(np.asarray(ins["L2_ffn_W_r"]).T)
        m["wkT"] = np.ascontiguousarray(np.asarray(ins["L2_ffn_W_k"]).T)
        m["wvT"] = np.ascontiguousarray(np.asarray(ins["L2_ffn_W_v"]).T)
        m["mur"] = host_col(ins["L2_ffn_mu_r"])
        m["muk"] = host_col(ins["L2_ffn_mu_k"])
        if h == 0 or x_prev_full is None:
            m["xhalo"] = np.zeros((128, 8), dtype=np.float32)
        else:
            m["xhalo"] = host_col(x_prev_full[b, 4095])
        maps.append(m)
    return maps



def setup_l3f(nc, hb, prefix="", Tc=4096, add_parts=False):
    NT = Tc // TT
    ctx = contextlib.ExitStack()
    with ctx:
        sb = {}

        def S(name, shape, dt=F32):
            sb[name] = ctx.enter_context(nc.sbuf_tensor("sb_" + prefix + name, shape, dt))
            return sb[name]

        for name, shp, dt in CONST_SPECS:
            if name in hb:
                S(name, shp, dt)
        S("we", [128, 64 * 1024], BF16)
        S("gw", [128, DCH * 8], BF16)
        S("wstg0", [128, 1024], F32); S("wstg1", [128, 1024], F32)
        S("x0", [128, DCH * TT])
        S("xhat", [128, DCH * TT], BF16)
        S("xe", [128, DCH * TT], BF16)
        sb["sq"] = sb["xe"]
        S("wrep", [128, 8 * TT], BF16)
        S("wfull", [128, 32])
        S("msk", [128, 32])
        S("tk", [128, 16])
        S("wt", [32, 128])
        S("row0", [1, TT]); S("row1", [1, TT])

        ps = {}
        pool = []
        for k in range(8):
            pool.append(ctx.enter_context(nc.psum_tensor(f"{prefix}psl{k}", [128, TT], F32)))
        ps["stat"] = pool[0]; ps["rep0"] = pool[1]; ps["rep1"] = pool[2]

        def body(s):
            emit_load_consts(s, hb, sb)
            chunks = [hb["ewT"][k * 128:(k + 1) * 128, :] for k in range(64)]
            dsts = [sb["we"][:, k * 1024:(k + 1) * 1024] for k in range(64)]
            emit_cast_weight(s, chunks, [sb["wstg0"], sb["wstg1"]], dsts)
            s.wait("q", ["sv"])
            s.op("q", lambda E: E.dma_start(out=sb["wstg0"][:, 0:64].rearrange("p (c g) -> p c g", c=8),
                                            in_=hb["gwT"].rearrange("(c p) g -> p c g", p=128)))
            s.wait("v", ["ld"])
            s.op("v", lambda E: E.tensor_copy(out=sb["gw"][:], in_=sb["wstg0"][:, 0:64]))
            cv = sb["c_cvec"]

            marks = []
            marks_lg = []
            for i in range(NT):
                t0 = i * TT
                xsb = sb["x0"]
                if i >= 1:
                    s.wait_at("q", "sv", marks[i - 1])
                    s.wait_at("q", "lg", marks_lg[i - 1])
                s.op("q", lambda E, t0=t0: E.dma_start(out=sb3(sb["x0"][:], TT), in_=fm_ap(hb["x"], t0, t0 + TT)))
                if add_parts:
                    for pk in (("pp0", "pp1") if add_parts == 2 else ("pp0",)):
                        for g in range(4):
                            s.wait("q", ["sv"])
                            s.op("q", lambda E, t0=t0, pk=pk, g=g: E.dma_start(
                                out=sb["wstg0"][:].rearrange("p (c t) -> p c t", c=2),
                                in_=(hb[pk](t0, t0 + TT) if callable(hb[pk]) else fm_ap(hb[pk], t0, t0 + TT))[:, g * 2:(g + 1) * 2, :]))
                            s.wait("v", ["ld", "sm", "ss"])
                            for c2 in range(2):
                                c = g * 2 + c2
                                s.op("v", lambda E, c=c, c2=c2: E.tensor_tensor(
                                    out=sb["x0"][:, c * TT:(c + 1) * TT], in0=sb["x0"][:, c * TT:(c + 1) * TT],
                                    in1=sb["wstg0"][:, c2 * TT:(c2 + 1) * TT], op=ALU.add))
                s.wait("s", ["ld", "sm", "sv"])
                emit_ln(s, ps, sb, xsb[:], sb["xhat"][:], sb["sq"][:], TT)
                s.wait("t", ["sv", "ss"])
                for j in range(4):
                    for ci in range(DCH):
                        s.op("t", lambda E, j=j, ci=ci: E.matmul(
                            out=pool[3][:, j * 8:(j + 1) * 8],
                            lhsT=sb["xhat"][:, ci * TT + j * 128: ci * TT + (j + 1) * 128],
                            rhs=sb["gw"][:, ci * 8:(ci + 1) * 8],
                            start=(ci == 0), stop=(ci == DCH - 1)))
                s.wait("v", ["sm"])
                for j in range(4):
                    pj = pool[3][:, j * 8:(j + 1) * 8]
                    s.op("v", lambda E, pj=pj, j=j: E.reduce_max(
                        out=sb["tk"][:, j:j + 1], in_=pj, axis=mybir.AxisListType.X))
                    s.op("v", lambda E, pj=pj, j=j: E.tensor_scalar(
                        out=sb["msk"][:, j * 8:(j + 1) * 8], in0=pj,
                        scalar1=sb["tk"][:, j:j + 1], scalar2=None, op0=ALU.is_ge))
                    s.op("v", lambda E, pj=pj, j=j: E.scalar_tensor_tensor(
                        out=sb["wfull"][:, j * 8:(j + 1) * 8], in0=sb["msk"][:, j * 8:(j + 1) * 8],
                        scalar=-1e30, in1=pj, op0=ALU.mult, op1=ALU.add))
                    s.op("v", lambda E, j=j: E.reduce_max(
                        out=sb["tk"][:, 4 + j:5 + j], in_=sb["wfull"][:, j * 8:(j + 1) * 8],
                        axis=mybir.AxisListType.X))
                    s.op("v", lambda E, j=j: E.tensor_scalar(
                        out=sb["wfull"][:, j * 8:(j + 1) * 8], in0=sb["wfull"][:, j * 8:(j + 1) * 8],
                        scalar1=sb["tk"][:, 4 + j:5 + j], scalar2=None, op0=ALU.is_ge))
                    s.op("v", lambda E, j=j: E.tensor_tensor(
                        out=sb["tk"][:, 8 + j:9 + j], in0=sb["tk"][:, 4 + j:5 + j],
                        in1=sb["tk"][:, j:j + 1], op=ALU.subtract))
                s.wait("s", ["sv"])
                for j in range(4):
                    s.op("s", lambda E, j=j: E.activation(
                        out=sb["tk"][:, 12 + j:13 + j], in_=sb["tk"][:, 8 + j:9 + j], func=AF.Sigmoid))
                s.wait("v", ["ss"])
                for j in range(4):
                    s.op("v", lambda E, j=j: E.scalar_tensor_tensor(
                        out=sb["tk"][:, 8 + j:9 + j], in0=sb["tk"][:, 12 + j:13 + j],
                        scalar=-1.0, in1=cv[:, 2:3], op0=ALU.mult, op1=ALU.add))
                    s.op("v", lambda E, j=j: E.tensor_scalar(
                        out=sb["wfull"][:, j * 8:(j + 1) * 8], in0=sb["wfull"][:, j * 8:(j + 1) * 8],
                        scalar1=sb["tk"][:, 12 + j:13 + j], scalar2=None, op0=ALU.mult))
                    s.op("v", lambda E, j=j: E.scalar_tensor_tensor(
                        out=sb["wfull"][:, j * 8:(j + 1) * 8], in0=sb["msk"][:, j * 8:(j + 1) * 8],
                        scalar=sb["tk"][:, 8 + j:9 + j], in1=sb["wfull"][:, j * 8:(j + 1) * 8],
                        op0=ALU.mult, op1=ALU.add))
                s.wait("t", ["sv"])
                s.op("t", lambda E: E.transpose(out=pool[4][0:32, 0:128], in_=sb["wfull"][:],
                                                identity=sb["c_eyef"][:]))
                s.wait("v", ["sm"])
                s.op("v", lambda E: E.tensor_copy(out=sb["wt"][:], in_=pool[4][0:32, 0:128]))
                for e in range(8):
                    s.wait("t", ["sv"])
                    for j in range(4):
                        s.op("t", lambda E, e=e, j=j: E.matmul(
                            out=pool[5][:, j * 128:(j + 1) * 128],
                            lhsT=sb["c_hot32"][:, (j * 8 + e) * 128:(j * 8 + e + 1) * 128],
                            rhs=sb["wt"][:], start=True, stop=True))
                    s.wait("v", ["sm"])
                    s.op("v", lambda E, e=e: E.tensor_copy(out=sb["wrep"][:, e * TT:(e + 1) * TT],
                                                           in_=pool[5][:, 0:TT]))
                for e in range(8):
                    s.wait("v", ["sm"])
                    for c in range(DCH):
                        s.op("v", lambda E, e=e, c=c: E.scalar_tensor_tensor(
                            out=sb["xe"][:, c * TT:(c + 1) * TT],
                            in0=sb["wrep"][:, e * TT:(e + 1) * TT], scalar=1.0,
                            in1=sb["xhat"][:, c * TT:(c + 1) * TT], op0=ALU.mult, op1=ALU.mult))
                    s.wait("t", ["sv"])
                    for ci in range(DCH):
                        for co in range(DCH):
                            s.op("t", lambda E, e=e, ci=ci, co=co: E.matmul(
                                out=pool[co][:],
                                lhsT=sb["we"][:, (e * 8 + ci) * 1024 + co * 128: (e * 8 + ci) * 1024 + (co + 1) * 128],
                                rhs=sb["xe"][:, ci * TT:(ci + 1) * TT],
                                start=(e == 0 and ci == 0), stop=(e == 7 and ci == 7)))
                s.wait("v", ["sm"])
                for co in range(DCH):
                    s.op("v", lambda E, co=co, xsb=xsb: E.tensor_tensor(
                        out=xsb[:, co * TT:(co + 1) * TT], in0=pool[co][:],
                        in1=xsb[:, co * TT:(co + 1) * TT], op=ALU.add))
                s.wait("g", ["sv"])
                s.op("g", lambda E, t0=t0, xsb=xsb: E.dma_start(out=fm_ap(hb["xo"], t0, t0 + TT), in_=sb3(xsb[:], TT)))
                marks.append(s.cnt["sv"])
                marks_lg.append(s.cnt["lg"])
            s.wait_at("g", "lg", s.cnt["lg"])

    return body



def build_l3f(Tc=4096, add_parts=False):
    use_consts = BASE_CONSTS + ["c_eyef", "c_hot32"]
    nc = bass.Bass("TRN2", target_bir_lowering=False)
    NT = Tc // TT
    hb = {}
    hb["x"] = nc.dram_tensor("x", [D, Tc], F32, kind="ExternalInput").ap()
    hb["gwT"] = nc.dram_tensor("gwT", [D, 8], F32, kind="ExternalInput").ap()
    if add_parts:
        hb["pp0"] = nc.dram_tensor("pp0", [D, Tc], F32, kind="ExternalInput").ap()
        hb["pp1"] = nc.dram_tensor("pp1", [D, Tc], F32, kind="ExternalInput").ap()
    hb["ewT"] = nc.dram_tensor("ewT", [8 * D, D], F32, kind="ExternalInput").ap()
    for name, shp, dt in CONST_SPECS:
        if name in use_consts:
            hb[name] = nc.dram_tensor(name, shp, dt, kind="ExternalInput").ap()
    hb["xo"] = nc.dram_tensor("xo", [D, Tc], F32, kind="ExternalOutput").ap()

    body = setup_l3f(nc, hb, "", Tc, add_parts)
    with contextlib.ExitStack() as c2:
        run_sched(nc, c2, body)
    return nc

def l3f_inmaps(ins, consts):
    consts = consts_subset(consts, BASE_CONSTS + ['c_eyef', 'c_hot32'])
    ew = np.asarray(ins["L3_ffn_expert_w"])
    ewT = np.ascontiguousarray(np.transpose(ew, (0, 2, 1)).reshape(8 * D, D))
    gwT = np.ascontiguousarray(np.asarray(ins["L3_ffn_gate_w"]).T)
    maps = []
    for c in range(NCORE):
        m = dict(consts)
        m["ewT"] = ewT
        m["gwT"] = gwT
        maps.append(m)
    return maps



def emit_elu1(s, sb, ps_src, dst_sl, tmp_f32, n):
    s.wait("v", ["sm"])
    s.op("v", lambda E: E.tensor_scalar(out=tmp_f32[:, 0:n], in0=ps_src, scalar1=0.0,
                                        scalar2=None, op0=ALU.min))
    s.wait("s", ["sv"])
    s.op("s", lambda E: E.activation(out=tmp_f32[:, 0:n], in_=tmp_f32[:, 0:n], func=AF.Exp))
    s.wait("v", ["ss"])
    s.op("v", lambda E: E.tensor_scalar(out=dst_sl, in0=ps_src, scalar1=0.0,
                                        scalar2=None, op0=ALU.max))
    s.op("v", lambda E: E.tensor_tensor(out=dst_sl, in0=dst_sl, in1=tmp_f32[:, 0:n], op=ALU.add))


def setup_l1a(nc, hb, prefix="", Tc=4096):
    NT = Tc // TT
    ctx = contextlib.ExitStack()
    with ctx:
        sb = {}

        def S(name, shape, dt=F32):
            sb[name] = ctx.enter_context(nc.sbuf_tensor("sb_" + prefix + name, shape, dt))
            return sb[name]

        for name, shp, dt in CONST_SPECS:
            if name in hb:
                S(name, shp, dt)
        S("wqkv", [128, DCH * 3072], BF16)
        S("bq_bf", [1, 3072], BF16)
        S("wstg0", [128, 1024], F32); S("wstg1", [128, 1024], F32)
        S("x0", [128, DCH * TT])
        S("xhat", [128, DCH * TT], BF16)
        S("sq", [128, DCH * TT], BF16)
        S("qp", [128, DCH * TT], BF16)
        S("kp", [128, 4 * 1024], BF16)
        S("vt", [128, 4 * 1024], BF16)
        S("kvacc", [128, DCH * 1024])
        S("ksacc", [128, 8])
        S("tmp", [128, 1024])
        S("row0", [1, TT]); S("row1", [1, TT])

        ps = {}
        for name, shape, dt in [("stat", [128, TT], F32), ("rep0", [128, TT], F32),
                                ("rep1", [128, TT], F32), ("pq", [128, TT], F32),
                                ("pvt", [128, 1024], F32), ("pks", [128, 8], F32)]:
            ps[name] = ctx.enter_context(nc.psum_tensor(prefix + name, shape, dt))

        def body(s):
            emit_load_consts(s, hb, sb)
            chunks, dsts = [], []
            for ci in range(DCH):
                for half in range(3):
                    chunks.append(hb["wqkv"][ci * 128:(ci + 1) * 128, half * 1024:(half + 1) * 1024])
                    dsts.append(sb["wqkv"][:, ci * 3072 + half * 1024: ci * 3072 + (half + 1) * 1024])
            emit_cast_weight(s, chunks, [sb["wstg0"], sb["wstg1"]], dsts)
            s.wait("q", ["sv"])
            s.op("q", lambda E: E.dma_start(out=sb["wstg0"][0:1, :], in_=hb["bqkv"][0:1, 0:1024]))
            s.op("q", lambda E: E.dma_start(out=sb["wstg1"][0:1, :], in_=hb["bqkv"][0:1, 1024:2048]))
            s.wait("v", ["ld"])
            s.op("v", lambda E: E.tensor_copy(out=sb["bq_bf"][0:1, 0:1024], in_=sb["wstg0"][0:1, :]))
            s.op("v", lambda E: E.tensor_copy(out=sb["bq_bf"][0:1, 1024:2048], in_=sb["wstg1"][0:1, :]))
            s.wait("q", ["sv"])
            s.op("q", lambda E: E.dma_start(out=sb["wstg0"][0:1, :], in_=hb["bqkv"][0:1, 2048:3072]))
            s.wait("v", ["ld"])
            s.op("v", lambda E: E.tensor_copy(out=sb["bq_bf"][0:1, 2048:3072], in_=sb["wstg0"][0:1, :]))
            s.op("v", lambda E: E.memset(sb["kvacc"][:], 0.0))
            s.op("v", lambda E: E.memset(sb["ksacc"][:], 0.0))

            marks = []
            marks_lg = []
            for i in range(NT):
                t0 = i * TT
                xsb = sb["x0"]
                if i >= 1:
                    s.wait_at("q", "sv", marks[i - 1])
                    s.wait_at("q", "lg", marks_lg[i - 1])
                s.op("q", lambda E, t0=t0: E.dma_start(out=sb3(sb["x0"][:], TT), in_=fm_ap(hb["x"], t0, t0 + TT)))
                s.wait("s", ["ld", "sm", "sv"])
                s.wait("v", ["ld"])
                emit_ln(s, ps, sb, xsb[:], sb["xhat"][:], sb["sq"][:], TT)
                for co in range(DCH):
                    s.wait("t", ["sv", "ss"])
                    for ci in range(DCH):
                        s.op("t", lambda E, ci=ci, co=co: E.matmul(
                            out=ps["pq"][:],
                            lhsT=sb["wqkv"][:, ci * 3072 + co * 128: ci * 3072 + (co + 1) * 128],
                            rhs=sb["xhat"][:, ci * TT:(ci + 1) * TT],
                            start=(ci == 0), stop=False))
                    s.op("t", lambda E, co=co: E.matmul(
                        out=ps["pq"][:], lhsT=sb["bq_bf"][0:1, co * 128:(co + 1) * 128],
                        rhs=sb["c_ones_bf"][0:1, :], start=False, stop=True))
                    emit_elu1(s, sb, ps["pq"][:], sb["qp"][:, co * TT:(co + 1) * TT], sb["tmp"], TT)
                s.wait("g", ["sv"])
                s.op("g", lambda E, t0=t0: E.dma_start(out=fm_ap(hb["qp_o"], t0, t0 + TT), in_=sb3(sb["qp"][:], TT)))
                for j in range(4):
                    for which, base, dst in (("k", 1024, "kp"), ("v", 2048, "vt")):
                        s.wait("t", ["sv", "ss"])
                        for h in range(2):
                            for ci in range(DCH):
                                s.op("t", lambda E, ci=ci, j=j, h=h, base=base: E.matmul(
                                    out=ps["pvt"][:, h * 512:(h + 1) * 512],
                                    lhsT=sb["xhat"][:, ci * TT + j * 128: ci * TT + (j + 1) * 128],
                                    rhs=sb["wqkv"][:, ci * 3072 + base + h * 512: ci * 3072 + base + (h + 1) * 512],
                                    start=(ci == 0), stop=False))
                            s.op("t", lambda E, j=j, h=h, base=base: E.matmul(
                                out=ps["pvt"][:, h * 512:(h + 1) * 512],
                                lhsT=sb["c_ones_bf"][0:1, 0:128],
                                rhs=sb["bq_bf"][0:1, base + h * 512: base + (h + 1) * 512],
                                start=False, stop=True))
                        if which == "k":
                            emit_elu1(s, sb, ps["pvt"][:], sb["kp"][:, j * 1024:(j + 1) * 1024], sb["tmp"], 1024)
                        else:
                            s.wait("v", ["sm"])
                            s.op("v", lambda E, j=j: E.tensor_copy(out=sb["vt"][:, j * 1024:(j + 1) * 1024],
                                                                   in_=ps["pvt"][:]))
                s.wait("t", ["sv"])
                for co in range(DCH):
                    for j in range(4):
                        s.op("t", lambda E, j=j, co=co: E.matmul(
                            out=ps["pks"][:, co:co + 1],
                            lhsT=sb["kp"][:, j * 1024 + co * 128: j * 1024 + (co + 1) * 128],
                            rhs=sb["c_ones_bf"][:, 0:1],
                            start=(j == 0), stop=(j == 3)))
                s.wait("v", ["sm"])
                s.op("v", lambda E: E.tensor_tensor(out=sb["ksacc"][:], in0=ps["pks"][:],
                                                    in1=sb["ksacc"][:], op=ALU.add))
                for co in range(DCH):
                    s.wait("t", ["sv", "ss"])
                    for h in range(2):
                        for j in range(4):
                            s.op("t", lambda E, co=co, j=j, h=h: E.matmul(
                                out=ps["pvt"][:, h * 512:(h + 1) * 512],
                                lhsT=sb["kp"][:, j * 1024 + co * 128: j * 1024 + (co + 1) * 128],
                                rhs=sb["vt"][:, j * 1024 + h * 512: j * 1024 + (h + 1) * 512],
                                start=(j == 0), stop=(j == 3)))
                    s.wait("v", ["sm"])
                    s.op("v", lambda E, co=co: E.tensor_tensor(
                        out=sb["kvacc"][:, co * 1024:(co + 1) * 1024], in0=ps["pvt"][:],
                        in1=sb["kvacc"][:, co * 1024:(co + 1) * 1024], op=ALU.add))
                marks.append(s.cnt["sv"])
                marks_lg.append(s.cnt["lg"])
            s.wait("g", ["sv"])
            for co in range(DCH):
                s.op("g", lambda E, co=co: E.dma_start(out=hb["kv_o"][co * 128:(co + 1) * 128, :],
                                                       in_=sb["kvacc"][:, co * 1024:(co + 1) * 1024]))
            s.op("g", lambda E: E.dma_start(out=hb["ks_o"], in_=sb["ksacc"][:]))
            s.wait_at("g", "lg", s.cnt["lg"])

    return body



def build_l1a(Tc=4096):
    use_consts = BASE_CONSTS
    nc = bass.Bass("TRN2", target_bir_lowering=False)
    NT = Tc // TT
    hb = {}
    hb["x"] = nc.dram_tensor("x", [D, Tc], F32, kind="ExternalInput").ap()
    hb["wqkv"] = nc.dram_tensor("wqkv", [D, 3072], F32, kind="ExternalInput").ap()
    hb["bqkv"] = nc.dram_tensor("bqkv", [1, 3072], F32, kind="ExternalInput").ap()
    for name, shp, dt in CONST_SPECS:
        if name in use_consts:
            hb[name] = nc.dram_tensor(name, shp, dt, kind="ExternalInput").ap()
    hb["qp_o"] = nc.dram_tensor("qp_o", [D, Tc], BF16, kind="ExternalOutput").ap()
    hb["kv_o"] = nc.dram_tensor("kv_o", [D, 1024], F32, kind="ExternalOutput").ap()
    hb["ks_o"] = nc.dram_tensor("ks_o", [128, 8], F32, kind="ExternalOutput").ap()

    body = setup_l1a(nc, hb, "", Tc)
    with contextlib.ExitStack() as c2:
        run_sched(nc, c2, body)
    return nc

def l1a_inmaps(ins, consts):
    consts = consts_subset(consts, BASE_CONSTS)
    maps = []
    for c in range(NCORE):
        m = dict(consts)
        m["wqkv"] = np.ascontiguousarray(np.asarray(ins["L1_attn_Wqkv"]).astype(np.float32))
        m["bqkv"] = host_row(ins["L1_attn_bqkv"])
        maps.append(m)
    return maps



def emit_ln2(s, ps, sb, x_sl, out_sl, sq_sl, t, gcol=None, bcol=None, out2_sl=None):
    for c in range(DCH):
        s.op("s", lambda E, c=c: E.activation(out=sq_sl[:, c * t:(c + 1) * t],
                                              in_=x_sl[:, c * t:(c + 1) * t], func=AF.Square))
    s.wait("t", ["ss", "sv", "ld"])
    for c in range(DCH):
        s.op("t", lambda E, c=c: E.matmul(out=ps["stat"][0:1, :t], lhsT=sb["c_ones_f32"][:, 0:1],
                                          rhs=x_sl[:, c * t:(c + 1) * t],
                                          start=(c == 0), stop=(c == DCH - 1)))
    for c in range(DCH):
        s.op("t", lambda E, c=c: E.matmul(out=ps["stat"][64:65, :t], lhsT=sb["c_ones_bf"][:, 0:1],
                                          rhs=sq_sl[:, c * t:(c + 1) * t],
                                          start=(c == 0), stop=(c == DCH - 1)))
    s.wait("v", ["sm"])
    s.op("v", lambda E: E.tensor_scalar(out=sb["row0"][0:1, :t], in0=ps["stat"][0:1, :t],
                                        scalar1=1.0 / D, scalar2=None, op0=ALU.mult))
    s.op("v", lambda E: E.tensor_tensor(out=sb["row1"][0:1, :t], in0=sb["row0"][0:1, :t],
                                        in1=sb["row0"][0:1, :t], op=ALU.mult))
    s.op("v", lambda E: E.scalar_tensor_tensor(out=sb["row1"][0:1, :t], in0=ps["stat"][64:65, :t],
                                               scalar=1.0 / D, in1=sb["row1"][0:1, :t],
                                               op0=ALU.mult, op1=ALU.subtract))
    s.wait("s", ["sv"])
    s.op("s", lambda E: E.activation(out=sb["row1"][0:1, :t], in_=sb["row1"][0:1, :t],
                                     func=AF.Ln, bias=sb["c_cvec"][0:1, 0:1], scale=sb["c_cvec"][0:1, 2:3]))
    s.op("s", lambda E: E.activation(out=sb["row1"][0:1, :t], in_=sb["row1"][0:1, :t],
                                     func=AF.Exp, bias=sb["c_cvec"][0:1, 3:4], scale=sb["c_cvec"][0:1, 4:5]))
    s.wait("t", ["ss", "sv"])
    s.op("t", lambda E: E.matmul(out=ps["rep0"][:, :t], lhsT=sb["c_ones_f32"][0:1, :],
                                 rhs=sb["row0"][0:1, :t], start=True, stop=True))
    s.op("t", lambda E: E.matmul(out=ps["rep1"][:, :t], lhsT=sb["c_ones_f32"][0:1, :],
                                 rhs=sb["row1"][0:1, :t], start=True, stop=True))
    s.wait("v", ["sm"])
    for c in range(DCH):
        s.op("v", lambda E, c=c: E.tensor_tensor(out=sq_sl[:, c * t:(c + 1) * t],
                                                 in0=x_sl[:, c * t:(c + 1) * t],
                                                 in1=ps["rep0"][:, :t], op=ALU.subtract))
    for c in range(DCH):
        s.op("v", lambda E, c=c: E.tensor_tensor(out=out_sl[:, c * t:(c + 1) * t],
                                                 in0=sq_sl[:, c * t:(c + 1) * t],
                                                 in1=ps["rep1"][:, :t], op=ALU.mult))
    if gcol is not None:
        for c in range(DCH):
            s.op("v", lambda E, c=c: E.tensor_scalar(
                out=out_sl[:, c * t:(c + 1) * t], in0=out_sl[:, c * t:(c + 1) * t],
                scalar1=gcol[:, c:c + 1], scalar2=bcol[:, c:c + 1], op0=ALU.mult, op1=ALU.add))
    if out2_sl is not None:
        for c in range(DCH):
            s.op("v", lambda E, c=c: E.tensor_copy(out=out2_sl[:, c * t:(c + 1) * t],
                                                   in_=out_sl[:, c * t:(c + 1) * t]))


def setup_l1b(nc, hb, prefix="", Tc=4096, single=False):
    TL = 128
    FH = 22
    NT = Tc // TL
    ctx = contextlib.ExitStack()
    with ctx:
        sb = {}

        def S(name, shape, dt=F32):
            sb[name] = ctx.enter_context(nc.sbuf_tensor("sb_" + prefix + name, shape, dt))
            return sb[name]

        for name, shp, dt in CONST_SPECS:
            if name in hb:
                S(name, shp, dt)
        S("wgb", [128, DCH * 2816], BF16)
        S("wub", [128, DCH * 2816], BF16)
        S("wdb", [128, FH * 1024], BF16)
        S("kvb", [128, DCH * 1024], BF16)
        S("ksb", [128, 8], BF16)
        S("bgr", [1, 2816], BF16); S("bur", [1, 2816], BF16); S("bdr", [1, 1024], BF16)
        S("g1c", [128, 8]); S("b1c", [128, 8]); S("g2c", [128, 8]); S("b2c", [128, 8])
        S("wstg0", [128, 1024], F32); S("wstg1", [128, 1024], F32)
        S("xf", [128, DCH * TL])
        S("bb", [128, DCH * TL])
        S("cc", [128, DCH * TL])
        S("xb", [128, DCH * TL], BF16)
        S("qpt", [128, DCH * TL], BF16)
        S("hbuf", [128, FH * TL], BF16)
        sb["sq"] = sb["hbuf"]
        S("sg0", [128, TL], BF16)
        S("sg1", [128, TL], BF16)
        S("dsb", [128, TL])
        S("row0", [1, TL]); S("row1", [1, TL])

        ps = {}
        for name, shape, dt in [("stat", [128, TL], F32), ("rep0", [128, TL], F32),
                                ("rep1", [128, TL], F32), ("pq", [128, TL], F32),
                                ("pu", [128, TL], F32), ("pd", [128, TL], F32)]:
            ps[name] = ctx.enter_context(nc.psum_tensor(prefix + name, shape, dt))

        def body(s):
            emit_load_consts(s, hb, sb)
            for small in ("g1", "b1", "g2", "b2"):
                s.op("q", lambda E, d=sb[small + "c"], sr=hb[small]: E.dma_start(out=d[:], in_=sr))
            chunks, dsts = [], []
            for w, hbk, nch, ncol in (("wgb", "wg", DCH, 2816), ("wub", "wu", DCH, 2816), ("wdb", "wd", FH, 1024)):
                for ci in range(nch):
                    nh = (ncol + 1023) // 1024
                    for half in range(nh):
                        c0, c1 = half * 1024, min(ncol, (half + 1) * 1024)
                        chunks.append(hb[hbk][ci * 128:(ci + 1) * 128, c0:c1])
                        dsts.append(sb[w][:, ci * ncol + c0: ci * ncol + c1])
            emit_cast_weight(s, chunks, [sb["wstg0"], sb["wstg1"]], dsts)
            for co in range(DCH):
                s.wait("q", ["sv"])
                s.op("q", lambda E, co=co: E.dma_start(out=sb["wstg0"][:], in_=hb["kv0"][co * 128:(co + 1) * 128, :]))
                if not single:
                    s.op("q", lambda E, co=co: E.dma_start(out=sb["wstg1"][:], in_=hb["kv1"][co * 128:(co + 1) * 128, :]))
                s.wait("v", ["ld"])
                if single:
                    s.op("v", lambda E, co=co: E.tensor_copy(out=sb["kvb"][:, co * 1024:(co + 1) * 1024],
                                                             in_=sb["wstg0"][:]))
                else:
                    s.op("v", lambda E, co=co: E.tensor_tensor(out=sb["kvb"][:, co * 1024:(co + 1) * 1024],
                                                               in0=sb["wstg0"][:], in1=sb["wstg1"][:], op=ALU.add))
            s.wait("q", ["sv"])
            s.op("q", lambda E: E.dma_start(out=sb["wstg0"][:, 0:8], in_=hb["ks0"]))
            if not single:
                s.op("q", lambda E: E.dma_start(out=sb["wstg1"][:, 0:8], in_=hb["ks1"]))
            s.wait("v", ["ld"])
            if single:
                s.op("v", lambda E: E.tensor_copy(out=sb["ksb"][:], in_=sb["wstg0"][:, 0:8]))
            else:
                s.op("v", lambda E: E.tensor_tensor(out=sb["ksb"][:], in0=sb["wstg0"][:, 0:8],
                                                   in1=sb["wstg1"][:, 0:8], op=ALU.add))
            for src, dst, n in ((hb["bg"], sb["bgr"], 2816), (hb["bu"], sb["bur"], 2816), (hb["bd"], sb["bdr"], 1024)):
                nh = (n + 1023) // 1024
                for half in range(nh):
                    c0, c1 = half * 1024, min(n, (half + 1) * 1024)
                    s.wait("q", ["sv"])
                    s.op("q", lambda E, src=src, c0=c0, c1=c1: E.dma_start(
                        out=sb["wstg0"][0:1, 0:c1 - c0], in_=src[0:1, c0:c1]))
                    s.wait("v", ["ld"])
                    s.op("v", lambda E, dst=dst, c0=c0, c1=c1: E.tensor_copy(
                        out=dst[0:1, c0:c1], in_=sb["wstg0"][0:1, 0:c1 - c0]))

            cv = sb["c_cvec"]
            marks = []
            marks_lg = []
            for i in range(NT):
                t0 = i * TL
                if i >= 1:
                    s.wait_at("q", "sv", marks[i - 1])
                    s.wait_at("q", "lg", marks_lg[i - 1])
                s.op("q", lambda E, t0=t0: E.dma_start(out=sb3(sb["xf"][:], TL), in_=fm_ap(hb["x"], t0, t0 + TL)))
                s.op("q", lambda E, t0=t0: E.dma_start(out=sb3(sb["qpt"][:], TL), in_=fm_ap(hb["qp"], t0, t0 + TL)))
                s.wait("s", ["ld", "sm", "sv"])
                s.wait("v", ["ld"])
                emit_ln2(s, ps, sb, sb["xf"][:], sb["bb"][:], sb["sq"][:], TL)
                s.wait("t", ["sv", "ss", "ld"])
                for ci in range(DCH):
                    s.op("t", lambda E, ci=ci: E.matmul(
                        out=ps["pd"][0:1, :], lhsT=sb["ksb"][:, ci:ci + 1],
                        rhs=sb["qpt"][:, ci * TL:(ci + 1) * TL],
                        start=(ci == 0), stop=(ci == DCH - 1)))
                s.wait("v", ["sm"])
                s.op("v", lambda E: E.tensor_scalar(out=sb["row0"][0:1, :], in0=ps["pd"][0:1, :],
                                                    scalar1=1e-6, scalar2=None, op0=ALU.add))
                s.op("v", lambda E: E.reciprocal(out=sb["row1"][0:1, :], in_=sb["row0"][0:1, :]))
                s.wait("t", ["sv"])
                s.op("t", lambda E: E.matmul(out=ps["pd"][:], lhsT=sb["c_ones_f32"][0:1, :],
                                             rhs=sb["row1"][0:1, :], start=True, stop=True))
                s.wait("v", ["sm"])
                s.op("v", lambda E: E.tensor_copy(out=sb["dsb"][:], in_=ps["pd"][:]))
                s.wait("t", ["sv", "ss"])
                groups = []
                for co in range(DCH):
                    def mkn(co):
                        def mm(E, pb, ci):
                            return E.matmul(
                                out=pb[:],
                                lhsT=sb["kvb"][:, ci * 1024 + co * 128: ci * 1024 + (co + 1) * 128],
                                rhs=sb["qpt"][:, ci * TL:(ci + 1) * TL],
                                start=(ci == 0), stop=(ci == DCH - 1))
                        mms = [lambda E, pb, ci=ci: mm(E, pb, ci) for ci in range(DCH)]
                        ev = lambda E, pb: E.tensor_tensor(
                            out=sb["cc"][:, co * TL:(co + 1) * TL], in0=pb[:],
                            in1=sb["dsb"][:], op=ALU.mult)
                        return (mms, ev, "v")
                    groups.append(mkn(co))
                emit_mm_pipe2(s, groups, [ps["pq"], ps["pu"]])
                for co in range(DCH):
                    s.op("v", lambda E, co=co: E.tensor_tensor(
                        out=sb["cc"][:, co * TL:(co + 1) * TL], in0=sb["cc"][:, co * TL:(co + 1) * TL],
                        in1=sb["bb"][:, co * TL:(co + 1) * TL], op=ALU.add))
                s.wait("s", ["sv"])
                emit_ln2(s, ps, sb, sb["cc"][:], sb["bb"][:], sb["sq"][:], TL,
                         gcol=sb["g1c"], bcol=sb["b1c"], out2_sl=sb["xb"][:])
                s.wait("t", ["sv", "ss"])
                groups = []
                for co in range(FH):
                    def mks(co, w, b, par):
                        def mm(E, pb, ci):
                            return E.matmul(
                                out=pb[:],
                                lhsT=sb[w][:, ci * 2816 + co * 128: ci * 2816 + (co + 1) * 128],
                                rhs=sb["xb"][:, ci * TL:(ci + 1) * TL],
                                start=(ci == 0), stop=False)
                        mms = [lambda E, pb, ci=ci: mm(E, pb, ci) for ci in range(DCH)]
                        mms.append(lambda E, pb, b=b: E.matmul(
                            out=pb[:], lhsT=sb[b][0:1, co * 128:(co + 1) * 128],
                            rhs=sb["c_ones_bf"][0:1, 0:TL], start=False, stop=True))
                        if w == "wgb":
                            ev = lambda E, pb: E.activation(out=sb["sg" + str(par)][:], in_=pb[:], func=AF.Silu)
                            return (mms, ev, "s")
                        ev = lambda E, pb: E.scalar_tensor_tensor(
                            out=sb["hbuf"][:, co * TL:(co + 1) * TL], in0=pb[:],
                            scalar=1.0, in1=sb["sg" + str(par)][:], op0=ALU.mult, op1=ALU.mult)
                        return (mms, ev, "v", ["ss"])
                    groups.append(mks(co, "wgb", "bgr", co % 2))
                    groups.append(mks(co, "wub", "bur", co % 2))
                emit_mm_pipe2(s, groups, [ps["pq"], ps["pu"], ps["stat"], ps["rep0"]])
                groups = []
                for co in range(DCH):
                    def mkd2(co):
                        def mm(E, pb, ci):
                            return E.matmul(
                                out=pb[:],
                                lhsT=sb["wdb"][:, ci * 1024 + co * 128: ci * 1024 + (co + 1) * 128],
                                rhs=sb["hbuf"][:, ci * TL:(ci + 1) * TL],
                                start=(ci == 0), stop=False)
                        mms = [lambda E, pb, ci=ci: mm(E, pb, ci) for ci in range(FH)]
                        mms.append(lambda E, pb: E.matmul(
                            out=pb[:], lhsT=sb["bdr"][0:1, co * 128:(co + 1) * 128],
                            rhs=sb["c_ones_bf"][0:1, 0:TL], start=False, stop=True))
                        ev = lambda E, pb: E.tensor_tensor(
                            out=sb["cc"][:, co * TL:(co + 1) * TL], in0=pb[:],
                            in1=sb["bb"][:, co * TL:(co + 1) * TL], op=ALU.add)
                        return (mms, ev, "v")
                    groups.append(mkd2(co))
                emit_mm_pipe2(s, groups, [ps["pq"], ps["pu"]])
                s.wait("s", ["sv"])
                emit_ln2(s, ps, sb, sb["cc"][:], sb["bb"][:], sb["sq"][:], TL,
                         gcol=sb["g2c"], bcol=sb["b2c"])
                s.wait("v", ["ss", "sm"])
                for co in range(DCH):
                    s.op("v", lambda E, co=co: E.tensor_tensor(
                        out=sb["xf"][:, co * TL:(co + 1) * TL], in0=sb["bb"][:, co * TL:(co + 1) * TL],
                        in1=sb["xf"][:, co * TL:(co + 1) * TL], op=ALU.add))
                s.wait("g", ["sv"])
                s.op("g", lambda E, t0=t0: E.dma_start(out=fm_ap(hb["xo"], t0, t0 + TL), in_=sb3(sb["xf"][:], TL)))
                marks.append(s.cnt["sv"])
                marks_lg.append(s.cnt["lg"])
            s.wait_at("g", "lg", s.cnt["lg"])

    return body



def build_l1b(Tc=4096):
    use_consts = BASE_CONSTS
    nc = bass.Bass("TRN2", target_bir_lowering=False)
    TL = 128
    NT = Tc // TL
    FH = 22
    hb = {}
    hb["x"] = nc.dram_tensor("x", [D, Tc], F32, kind="ExternalInput").ap()
    hb["qp"] = nc.dram_tensor("qp", [D, Tc], BF16, kind="ExternalInput").ap()
    hb["kv0"] = nc.dram_tensor("kv0", [D, 1024], F32, kind="ExternalInput").ap()
    hb["kv1"] = nc.dram_tensor("kv1", [D, 1024], F32, kind="ExternalInput").ap()
    hb["ks0"] = nc.dram_tensor("ks0", [128, 8], F32, kind="ExternalInput").ap()
    hb["ks1"] = nc.dram_tensor("ks1", [128, 8], F32, kind="ExternalInput").ap()
    hb["wg"] = nc.dram_tensor("wg", [D, 2816], F32, kind="ExternalInput").ap()
    hb["wu"] = nc.dram_tensor("wu", [D, 2816], F32, kind="ExternalInput").ap()
    hb["wd"] = nc.dram_tensor("wd", [2816, D], F32, kind="ExternalInput").ap()
    hb["bg"] = nc.dram_tensor("bg", [1, 2816], F32, kind="ExternalInput").ap()
    hb["bu"] = nc.dram_tensor("bu", [1, 2816], F32, kind="ExternalInput").ap()
    hb["bd"] = nc.dram_tensor("bd", [1, 1024], F32, kind="ExternalInput").ap()
    hb["g1"] = nc.dram_tensor("g1", [128, 8], F32, kind="ExternalInput").ap()
    hb["b1"] = nc.dram_tensor("b1", [128, 8], F32, kind="ExternalInput").ap()
    hb["g2"] = nc.dram_tensor("g2", [128, 8], F32, kind="ExternalInput").ap()
    hb["b2"] = nc.dram_tensor("b2", [128, 8], F32, kind="ExternalInput").ap()
    for name, shp, dt in CONST_SPECS:
        if name in use_consts:
            hb[name] = nc.dram_tensor(name, shp, dt, kind="ExternalInput").ap()
    hb["xo"] = nc.dram_tensor("xo", [D, Tc], F32, kind="ExternalOutput").ap()

    body = setup_l1b(nc, hb, "", Tc)
    with contextlib.ExitStack() as c2:
        run_sched(nc, c2, body)
    return nc

def _pad_cols(a, n):
    out = np.zeros((a.shape[0], n), dtype=np.float32)
    out[:, :a.shape[1]] = a
    return out


def _pad_rows(a, n):
    out = np.zeros((n, a.shape[1]), dtype=np.float32)
    out[:a.shape[0], :] = a
    return out


def l1b_inmaps(ins, consts, qp_full, kv_parts, ks_parts):
    wg = _pad_cols(np.asarray(ins["L1_attn_Wg"]), 2816)
    wu = _pad_cols(np.asarray(ins["L1_attn_Wu"]), 2816)
    wd = _pad_rows(np.asarray(ins["L1_attn_Wd"]), 2816)
    bg = _pad_cols(host_row(ins["L1_attn_bg"]), 2816)
    bu = _pad_cols(host_row(ins["L1_attn_bu"]), 2816)
    consts = consts_subset(consts, BASE_CONSTS)
    maps = []
    for c in range(NCORE):
        b, h = c // 2, c % 2
        m = dict(consts)
        m["qp"] = qp_full[c]
        m["kv0"] = kv_parts[b * 2]
        m["kv1"] = kv_parts[b * 2 + 1]
        m["ks0"] = ks_parts[b * 2]
        m["ks1"] = ks_parts[b * 2 + 1]
        m["wg"] = wg; m["wu"] = wu; m["wd"] = wd
        m["bg"] = bg; m["bu"] = bu
        m["bd"] = host_row(ins["L1_attn_bd"])
        m["g1"] = host_col(ins["L1_attn_g1"]); m["b1"] = host_col(ins["L1_attn_b1"])
        m["g2"] = host_col(ins["L1_attn_g2"]); m["b2"] = host_col(ins["L1_attn_b2"])
        maps.append(m)
    return maps



def setup_mka(nc, hb, prefix="", mode="ln", Tc=8192):
    NT = Tc // TT
    def _apfn(key):
        v = hb.get(key)
        if callable(v):
            return v
        return (lambda t0, t1, v=v: fm_ap(v, t0, t1)) if v is not None else None
    get_x = _apfn("x") if mode == "ln" else None
    get_p0 = _apfn("p0")
    ctx = contextlib.ExitStack()
    with ctx:
        sb = {}

        def S(name, shape, dt=F32):
            sb[name] = ctx.enter_context(nc.sbuf_tensor("sb_" + prefix + name, shape, dt))
            return sb[name]

        for name, shp, dt in CONST_SPECS:
            if name in hb:
                S(name, shp, dt)
        S("wa", [128, DCH * 2048], BF16)
        S("xp", [128, DCH * 96], BF16)
        S("cw", [128, 32]); S("cb", [128, 8])
        S("wstg0", [128, 1024], F32); S("wstg1", [128, 1024], F32)
        S("x0", [128, DCH * TT])
        S("x1", [128, DCH * TT])
        S("xhat", [128, DCH * TT], BF16)
        S("sq", [128, DCH * TT], BF16)
        S("ext", [128, DCH * (TT + 3)], BF16)
        S("xc", [128, DCH * TT], BF16)
        S("sgz", [128, DCH * TT], BF16)
        S("cacc", [128, TT])
        S("row0", [1, TT]); S("row1", [1, TT])

        ps = {}
        for name, shape, dt in [("stat", [128, TT], F32), ("rep0", [128, TT], F32),
                                ("rep1", [128, TT], F32), ("pq", [128, TT], F32),
                                ("pdb", [80, TT], F32), ("pc", [16, TT], F32)]:
            ps[name] = ctx.enter_context(nc.psum_tensor(prefix + name, shape, dt))

        def body(s):
            emit_load_consts(s, hb, sb)
            for small in ("cw", "cb"):
                s.op("q", lambda E, d=sb[small], sr=hb[small]: E.dma_start(out=d[:], in_=sr))
            chunks, dsts = [], []
            for ci in range(DCH):
                for half in range(2):
                    chunks.append(hb["wa"][ci * 128:(ci + 1) * 128, half * 1024:(half + 1) * 1024])
                    dsts.append(sb["wa"][:, ci * 2048 + half * 1024: ci * 2048 + (half + 1) * 1024])
            for ci in range(DCH):
                chunks.append(hb["xpT"][ci * 128:(ci + 1) * 128, :])
                dsts.append(sb["xp"][:, ci * 96: ci * 96 + 80])
            for ci in range(DCH):
                chunks.append(hb["xpTc"][ci * 128:(ci + 1) * 128, :])
                dsts.append(sb["xp"][:, ci * 96 + 80: ci * 96 + 96])
            emit_cast_weight(s, chunks, [sb["wstg0"], sb["wstg1"]], dsts)
            s.wait("v", ["ld"])
            s.op("v", lambda E: E.memset(sb["ext"][:], 0.0))

            marks = []
            marks_lg = []
            for i in range(NT):
                t0 = i * TT
                if i >= 1:
                    s.wait_at("q", "sv", marks[i - 1])
                    s.wait_at("q", "lg", marks_lg[i - 1])
                if mode == "ln":
                    s.op("q", lambda E, t0=t0: E.dma_start(out=sb3(sb["x0"][:], TT), in_=get_x(t0, t0 + TT)))
                    s.wait("s", ["ld", "sm", "sv"])
                    s.wait("v", ["ld"])
                    emit_ln(s, ps, sb, sb["x0"][:], sb["xhat"][:], sb["sq"][:], TT)
                elif mode == "plain":
                    s.op("q", lambda E, t0=t0: E.dma_start(out=sb3(sb["x0"][:], TT), in_=get_p0(t0, t0 + TT)))
                    s.wait("v", ["ld", "sm", "ss"])
                    for c in range(DCH):
                        s.op("v", lambda E, c=c: E.tensor_copy(
                            out=sb["xhat"][:, c * TT:(c + 1) * TT], in_=sb["x0"][:, c * TT:(c + 1) * TT]))
                else:
                    s.op("q", lambda E, t0=t0: E.dma_start(out=sb3(sb["x0"][:], TT), in_=get_p0(t0, t0 + TT)))
                    s.op("q", lambda E, t0=t0: E.dma_start(out=sb3(sb["x1"][:], TT), in_=fm_ap(hb["p1"], t0, t0 + TT)))
                    s.wait("v", ["ld", "sm", "ss"])
                    for c in range(DCH):
                        s.op("v", lambda E, c=c: E.tensor_tensor(
                            out=sb["xhat"][:, c * TT:(c + 1) * TT], in0=sb["x0"][:, c * TT:(c + 1) * TT],
                            in1=sb["x1"][:, c * TT:(c + 1) * TT], op=ALU.add))
                s.wait("t", ["sv", "ss"])
                groups = []
                for co in range(16):
                    def mk(co):
                        def mm(E, pb, ci):
                            return E.matmul(
                                out=pb[:],
                                lhsT=sb["wa"][:, ci * 2048 + co * 128: ci * 2048 + (co + 1) * 128],
                                rhs=sb["xhat"][:, ci * TT:(ci + 1) * TT],
                                start=(ci == 0), stop=(ci == DCH - 1))
                        mms = [lambda E, pb, ci=ci: mm(E, pb, ci) for ci in range(DCH)]
                        if co < 8:
                            ev = lambda E, pb: E.tensor_copy(
                                out=sb["ext"][:, co * (TT + 3) + 3: (co + 1) * (TT + 3)], in_=pb[:])
                            return mms, ev, "v"
                        ev = lambda E, pb: E.activation(
                            out=sb["sgz"][:, (co - 8) * TT:(co - 7) * TT], in_=pb[:], func=AF.Silu)
                        return mms, ev, "s"
                    groups.append(mk(co))
                emit_mm_pipe2(s, groups, [ps["pq"], ps["rep0"]])
                s.wait("v", ["ss", "sm"])
                for c in range(DCH):
                    s.op("v", lambda E, c=c: E.tensor_scalar(
                        out=sb["cacc"][:], in0=sb["ext"][:, c * (TT + 3): c * (TT + 3) + TT],
                        scalar1=sb["cw"][:, c * 4: c * 4 + 1], scalar2=None, op0=ALU.mult))
                    for k in range(1, 4):
                        s.op("v", lambda E, c=c, k=k: E.scalar_tensor_tensor(
                            out=sb["cacc"][:], in0=sb["ext"][:, c * (TT + 3) + k: c * (TT + 3) + k + TT],
                            scalar=sb["cw"][:, c * 4 + k: c * 4 + k + 1], in1=sb["cacc"][:],
                            op0=ALU.mult, op1=ALU.add))
                    s.op("v", lambda E, c=c: E.tensor_scalar(
                        out=sb["cacc"][:], in0=sb["cacc"][:],
                        scalar1=sb["cb"][:, c:c + 1], scalar2=None, op0=ALU.add))
                    s.wait("s", ["sv"])
                    s.op("s", lambda E, c=c: E.activation(
                        out=sb["xc"][:, c * TT:(c + 1) * TT], in_=sb["cacc"][:], func=AF.Silu))
                    s.wait("v", ["ss"])
                for c in range(DCH):
                    s.op("v", lambda E, c=c: E.tensor_copy(
                        out=sb["ext"][:, c * (TT + 3): c * (TT + 3) + 3],
                        in_=sb["ext"][:, c * (TT + 3) + TT: c * (TT + 3) + TT + 3]))
                s.wait("t", ["ss", "sv"])
                for ci in range(DCH):
                    s.op("t", lambda E, ci=ci: E.matmul(
                        out=ps["pdb"][:], lhsT=sb["xp"][:, ci * 96: ci * 96 + 80],
                        rhs=sb["xc"][:, ci * TT:(ci + 1) * TT],
                        start=(ci == 0), stop=(ci == DCH - 1)))
                for ci in range(DCH):
                    s.op("t", lambda E, ci=ci: E.matmul(
                        out=ps["pc"][:], lhsT=sb["xp"][:, ci * 96 + 80: ci * 96 + 96],
                        rhs=sb["xc"][:, ci * TT:(ci + 1) * TT],
                        start=(ci == 0), stop=(ci == DCH - 1)))
                s.wait("v", ["sm"])
                s.op("v", lambda E: E.tensor_copy(out=sb["x0"][0:80, 0:TT], in_=ps["pdb"][:]))
                s.op("v", lambda E: E.tensor_copy(out=sb["x1"][0:16, 0:TT], in_=ps["pc"][:]))
                s.wait("g", ["sv", "ss"])
                s.op("g", lambda E, t0=t0: E.dma_start(out=fm_ap(hb["xc_o"], t0, t0 + TT), in_=sb3(sb["xc"][:], TT)))
                s.op("g", lambda E, t0=t0: E.dma_start(out=fm_ap(hb["sz_o"], t0, t0 + TT), in_=sb3(sb["sgz"][:], TT)))
                s.op("g", lambda E, t0=t0: E.dma_start(out=hb["db_o"][:, t0:t0 + TT], in_=sb["x0"][0:80, 0:TT]))
                s.op("g", lambda E, t0=t0: E.dma_start(out=hb["c_o"][:, t0:t0 + TT], in_=sb["x1"][0:16, 0:TT]))
                marks.append(s.cnt["sv"])
                marks_lg.append(s.cnt["lg"])
            s.wait_at("g", "lg", s.cnt["lg"])

    return body



def build_mka(mode, Tc=8192):
    use_consts = BASE_CONSTS
    nc = bass.Bass("TRN2", target_bir_lowering=False)
    NT = Tc // TT
    hb = {}
    if mode == "ln":
        hb["x"] = nc.dram_tensor("x", [D, Tc], F32, kind="ExternalInput").ap()
    else:
        hb["p0"] = nc.dram_tensor("p0", [D, Tc], F32, kind="ExternalInput").ap()
        hb["p1"] = nc.dram_tensor("p1", [D, Tc], F32, kind="ExternalInput").ap()
    hb["wa"] = nc.dram_tensor("wa", [D, 2048], F32, kind="ExternalInput").ap()
    hb["xpT"] = nc.dram_tensor("xpT", [D, 80], F32, kind="ExternalInput").ap()
    hb["xpTc"] = nc.dram_tensor("xpTc", [D, 16], F32, kind="ExternalInput").ap()
    hb["cw"] = nc.dram_tensor("cw", [128, 32], F32, kind="ExternalInput").ap()
    hb["cb"] = nc.dram_tensor("cb", [128, 8], F32, kind="ExternalInput").ap()
    for name, shp, dt in CONST_SPECS:
        if name in use_consts:
            hb[name] = nc.dram_tensor(name, shp, dt, kind="ExternalInput").ap()
    hb["xc_o"] = nc.dram_tensor("xc_o", [D, Tc], BF16, kind="ExternalOutput").ap()
    hb["sz_o"] = nc.dram_tensor("sz_o", [D, Tc], BF16, kind="ExternalOutput").ap()
    hb["db_o"] = nc.dram_tensor("db_o", [80, Tc], F32, kind="ExternalOutput").ap()
    hb["c_o"] = nc.dram_tensor("c_o", [16, Tc], F32, kind="ExternalOutput").ap()

    body = setup_mka(nc, hb, "", mode, Tc)
    with contextlib.ExitStack() as c2:
        run_sched(nc, c2, body)
    return nc

def mka_inmaps(consts, in_w, xp_w, conv_w, conv_b, x_full=None, parts=None):
    consts = consts_subset(consts, BASE_CONSTS)
    maps = []
    in_w = np.asarray(in_w); xp_w = np.asarray(xp_w)
    conv_w = np.asarray(conv_w); conv_b = np.asarray(conv_b)
    for c in range(NCORE):
        b, eh = c // 2, c % 2
        m = dict(consts)
        rows = np.r_[eh * 1024:(eh + 1) * 1024, 2048 + eh * 1024: 2048 + (eh + 1) * 1024]
        m["wa"] = np.ascontiguousarray(in_w[rows, :].T)
        xs = xp_w[:, eh * 1024:(eh + 1) * 1024]
        m["xpT"] = np.ascontiguousarray(xs[0:80, :].T)
        m["xpTc"] = np.ascontiguousarray(xs[80:96, :].T)
        cw = conv_w[eh * 1024:(eh + 1) * 1024, :]
        m["cw"] = np.ascontiguousarray(cw.reshape(8, 128, 4).transpose(1, 0, 2).reshape(128, 32))
        m["cb"] = host_col(conv_b[eh * 1024:(eh + 1) * 1024])
        if x_full is not None:
            m["x"] = host_fm(x_full[b])
        else:
            m["p0"], m["p1"] = parts[c]
        maps.append(m)
    return maps



def setup_mkb(nc, hb, prefix="", momentum=False, Tc=8192, single=False):
    NT = Tc // TT
    ov = hb["op_o"]
    get_out = ov if callable(ov) else (lambda t0, t1, v=ov: fm_ap(v, t0, t1))
    ctx = contextlib.ExitStack()
    with ctx:
        sb = {}

        def S(name, shape, dt=F32):
            sb[name] = ctx.enter_context(nc.sbuf_tensor("sb_" + prefix + name, shape, dt))
            return sb[name]

        for name, shp, dt in CONST_SPECS:
            if name in hb:
                S(name, shp, dt)
        S("ow", [128, DCH * 1024], BF16)
        S("dtw", [64, 1024], BF16)
        S("dtb_bf", [1, 1024], BF16)
        S("acol", [128, 128])
        S("dcol", [128, 8])
        S("wstg0", [128, 1024], F32); S("wstg1", [128, 1024], F32)
        S("xct", [128, DCH * TT], BF16)
        S("szt", [128, DCH * TT], BF16)
        S("du", [128, DCH * TT], BF16)
        S("delta", [128, DCH * TT])
        S("db_a", [80, TT]); S("db_b", [80, TT])
        S("ct_a", [16, TT]); S("ct_b", [16, TT])
        S("dtc", [64, TT], BF16)
        S("bt", [16, TT], BF16); S("ctb", [16, TT], BF16)
        S("Bb", [128, 16 * TT], BF16)
        S("Cb", [128, 16 * TT], BF16)
        S("dab", [128, TT], BF16)
        S("qb", [128, TT], BF16)
        S("hb", [128, TT], BF16)
        S("vb", [128, TT], BF16)
        S("yacc", [128, DCH * TT], BF16)
        S("ybuf", [128, DCH * TT], BF16)
        S("ot", [128, DCH * TT])
        S("hcar", [128, 128]); S("vcar", [128, 128])
        S("spa", [128, TT]); S("spb", [128, TT])
        S("beta", [128, TT], BF16)

        ps = {}
        for name, shape, dt in [("pq", [128, TT], F32), ("prep", [128, TT], F32)]:
            ps[name] = ctx.enter_context(nc.psum_tensor(prefix + name, shape, dt))

        def body(s):
            emit_load_consts(s, hb, sb)
            s.op("q", lambda E: E.dma_start(out=sb["dcol"][:], in_=hb["dcol"]))
            s.op("q", lambda E: E.dma_start(out=sb["wstg0"][0:64, :], in_=hb["dtwT"]))
            s.op("q", lambda E: E.dma_start(out=sb["wstg1"][0:1, :], in_=hb["dtb"]))
            s.wait("v", ["ld"])
            s.op("v", lambda E: E.tensor_copy(out=sb["dtw"][:], in_=sb["wstg0"][0:64, :]))
            s.op("v", lambda E: E.tensor_copy(out=sb["dtb_bf"][:], in_=sb["wstg1"][0:1, :]))
            s.op("v", lambda E: E.memset(sb["hcar"][:], 0.0))
            s.op("v", lambda E: E.memset(sb["vcar"][:], 0.0))
            s.op("v", lambda E: E.memset(sb["beta"][:], 0.6))
            s.wait("q", ["sv"])
            s.op("q", lambda E: E.dma_start(out=sb["wstg0"][:, 0:128], in_=hb["alog"]))
            s.wait("s", ["ld"])
            s.op("s", lambda E: E.activation(out=sb["acol"][:], in_=sb["wstg0"][:, 0:128], func=AF.Exp))
            s.wait("v", ["ss"])
            s.op("v", lambda E: E.tensor_scalar(out=sb["acol"][:], in0=sb["acol"][:],
                                                scalar1=-1.0, scalar2=None, op0=ALU.mult))
            chunks, dsts = [], []
            for ci in range(DCH):
                chunks.append(hb["owT"][ci * 128:(ci + 1) * 128, :])
                dsts.append(sb["ow"][:, ci * 1024:(ci + 1) * 1024])
            emit_cast_weight(s, chunks, [sb["wstg0"], sb["wstg1"]], dsts)

            cv = sb["c_cvec"]
            marks = []
            marks_lg = []
            for i in range(NT):
                t0 = i * TT
                if i >= 1:
                    s.wait_at("q", "sv", marks[i - 1])
                    s.wait_at("q", "lg", marks_lg[i - 1])
                s.op("q", lambda E, t0=t0: E.dma_start(out=sb3(sb["xct"][:], TT), in_=fm_ap(hb["xc"], t0, t0 + TT)))
                s.op("q", lambda E, t0=t0: E.dma_start(out=sb3(sb["szt"][:], TT), in_=fm_ap(hb["sgz"], t0, t0 + TT)))
                s.op("q", lambda E, t0=t0: E.dma_start(out=sb["db_a"][:], in_=hb["db0"][:, t0:t0 + TT]))
                s.op("q", lambda E, t0=t0: E.dma_start(out=sb["ct_a"][:], in_=hb["c0"][:, t0:t0 + TT]))
                if not single:
                    s.op("q", lambda E, t0=t0: E.dma_start(out=sb["db_b"][:], in_=hb["db1"][:, t0:t0 + TT]))
                    s.op("q", lambda E, t0=t0: E.dma_start(out=sb["ct_b"][:], in_=hb["c1"][:, t0:t0 + TT]))
                s.wait("v", ["ld", "ss", "sm"])
                if not single:
                    s.op("v", lambda E: E.tensor_tensor(out=sb["db_a"][:], in0=sb["db_a"][:], in1=sb["db_b"][:], op=ALU.add))
                    s.op("v", lambda E: E.tensor_tensor(out=sb["ct_a"][:], in0=sb["ct_a"][:], in1=sb["ct_b"][:], op=ALU.add))
                s.op("v", lambda E: E.tensor_copy(out=sb["dtc"][:], in_=sb["db_a"][0:64, :]))
                s.op("v", lambda E: E.tensor_copy(out=sb["bt"][:], in_=sb["db_a"][64:80, :]))
                s.op("v", lambda E: E.tensor_copy(out=sb["ctb"][:], in_=sb["ct_a"][:]))
                for co in range(DCH):
                    s.wait("t", ["sv", "ss"])
                    s.op("t", lambda E, co=co: E.matmul(
                        out=ps["pq"][:], lhsT=sb["dtw"][:, co * 128:(co + 1) * 128],
                        rhs=sb["dtc"][:], start=True, stop=False))
                    s.op("t", lambda E, co=co: E.matmul(
                        out=ps["pq"][:], lhsT=sb["dtb_bf"][0:1, co * 128:(co + 1) * 128],
                        rhs=sb["c_ones_bf"][0:1, :], start=False, stop=True))
                    s.wait("v", ["sm"])
                    s.op("v", lambda E: E.tensor_scalar(out=sb["spa"][:], in0=ps["pq"][:],
                                                        scalar1=0.0, scalar2=None, op0=ALU.min))
                    s.op("v", lambda E: E.tensor_scalar(out=sb["spb"][:], in0=ps["pq"][:],
                                                        scalar1=0.0, scalar2=None, op0=ALU.max))
                    s.op("v", lambda E: E.tensor_tensor(out=sb["spa"][:], in0=sb["spa"][:],
                                                        in1=sb["spb"][:], op=ALU.subtract))
                    s.wait("s", ["sv"])
                    s.op("s", lambda E: E.activation(out=sb["spa"][:], in_=sb["spa"][:], func=AF.Exp))
                    s.op("s", lambda E: E.activation(out=sb["spa"][:], in_=sb["spa"][:],
                                                     func=AF.Ln, bias=cv[0:128, 2:3], scale=cv[0:128, 2:3]))
                    s.wait("v", ["ss"])
                    s.op("v", lambda E, co=co: E.tensor_tensor(out=sb["delta"][:, co * TT:(co + 1) * TT],
                                                               in0=sb["spa"][:], in1=sb["spb"][:], op=ALU.add))
                for c in range(DCH):
                    s.op("v", lambda E, c=c: E.tensor_tensor(
                        out=sb["du"][:, c * TT:(c + 1) * TT], in0=sb["delta"][:, c * TT:(c + 1) * TT],
                        in1=sb["xct"][:, c * TT:(c + 1) * TT], op=ALU.mult))
                for n in range(16):
                    s.wait("t", ["sv"])
                    s.op("t", lambda E, n=n: E.matmul(
                        out=ps["prep"][:], lhsT=sb["c_hot16"][:, n * 128:(n + 1) * 128],
                        rhs=sb["bt"][:], start=True, stop=True))
                    s.wait("v", ["sm"])
                    s.op("v", lambda E, n=n: E.tensor_copy(out=sb["Bb"][:, n * TT:(n + 1) * TT], in_=ps["prep"][:]))
                    s.wait("t", ["sv"])
                    s.op("t", lambda E, n=n: E.matmul(
                        out=ps["prep"][:], lhsT=sb["c_hot16"][:, n * 128:(n + 1) * 128],
                        rhs=sb["ctb"][:], start=True, stop=True))
                    s.wait("v", ["sm"])
                    s.op("v", lambda E, n=n: E.tensor_copy(out=sb["Cb"][:, n * TT:(n + 1) * TT], in_=ps["prep"][:]))
                s.op("v", lambda E: E.memset(sb["yacc"][:], 0.0))
                for n in range(16):
                    for ec in range(DCH):
                        col = ec * 16 + n
                        s.wait("s", ["sv"])
                        s.op("s", lambda E, ec=ec, col=col: E.activation(
                            out=sb["dab"][:], in_=sb["delta"][:, ec * TT:(ec + 1) * TT],
                            func=AF.Exp, bias=cv[:, 3:4], scale=sb["acol"][:, col:col + 1]))
                        s.wait("v", ["ss"])
                        s.op("v", lambda E, ec=ec, n=n: E.tensor_tensor(
                            out=sb["qb"][:], in0=sb["du"][:, ec * TT:(ec + 1) * TT],
                            in1=sb["Bb"][:, n * TT:(n + 1) * TT], op=ALU.mult))
                        if momentum:
                            s.op("v", lambda E, col=col: E.tensor_tensor_scan(
                                out=sb["vb"][:], data0=sb["beta"][:], data1=sb["qb"][:],
                                initial=sb["vcar"][:, col:col + 1], op0=ALU.mult, op1=ALU.add))
                            s.op("v", lambda E, col=col: E.tensor_copy(
                                out=sb["vcar"][:, col:col + 1], in_=sb["vb"][:, TT - 1:TT]))
                            s.op("v", lambda E, col=col: E.tensor_tensor_scan(
                                out=sb["hb"][:], data0=sb["dab"][:], data1=sb["vb"][:],
                                initial=sb["hcar"][:, col:col + 1], op0=ALU.mult, op1=ALU.add))
                        else:
                            s.op("v", lambda E, col=col: E.tensor_tensor_scan(
                                out=sb["hb"][:], data0=sb["dab"][:], data1=sb["qb"][:],
                                initial=sb["hcar"][:, col:col + 1], op0=ALU.mult, op1=ALU.add))
                        s.op("v", lambda E, col=col: E.tensor_copy(
                            out=sb["hcar"][:, col:col + 1], in_=sb["hb"][:, TT - 1:TT]))
                        s.op("v", lambda E, n=n: E.tensor_tensor(
                            out=sb["hb"][:], in0=sb["hb"][:],
                            in1=sb["Cb"][:, n * TT:(n + 1) * TT], op=ALU.mult))
                        s.op("v", lambda E, ec=ec: E.tensor_tensor(
                            out=sb["yacc"][:, ec * TT:(ec + 1) * TT], in0=sb["hb"][:],
                            in1=sb["yacc"][:, ec * TT:(ec + 1) * TT], op=ALU.add))
                for c in range(DCH):
                    s.op("v", lambda E, c=c: E.scalar_tensor_tensor(
                        out=sb["ybuf"][:, c * TT:(c + 1) * TT], in0=sb["xct"][:, c * TT:(c + 1) * TT],
                        scalar=sb["dcol"][:, c:c + 1], in1=sb["yacc"][:, c * TT:(c + 1) * TT],
                        op0=ALU.mult, op1=ALU.add))
                    s.op("v", lambda E, c=c: E.tensor_tensor(
                        out=sb["ybuf"][:, c * TT:(c + 1) * TT], in0=sb["ybuf"][:, c * TT:(c + 1) * TT],
                        in1=sb["szt"][:, c * TT:(c + 1) * TT], op=ALU.mult))
                for co in range(DCH):
                    s.wait("t", ["sv", "ss"])
                    for ci in range(DCH):
                        s.op("t", lambda E, ci=ci, co=co: E.matmul(
                            out=ps["pq"][:],
                            lhsT=sb["ow"][:, ci * 1024 + co * 128: ci * 1024 + (co + 1) * 128],
                            rhs=sb["ybuf"][:, ci * TT:(ci + 1) * TT],
                            start=(ci == 0), stop=(ci == DCH - 1)))
                    s.wait("v", ["sm"])
                    s.op("v", lambda E, co=co: E.tensor_copy(out=sb["ot"][:, co * TT:(co + 1) * TT],
                                                             in_=ps["pq"][:]))
                s.wait("g", ["sv"])
                s.op("g", lambda E, t0=t0: E.dma_start(out=get_out(t0, t0 + TT), in_=sb3(sb["ot"][:], TT)))
                marks.append(s.cnt["sv"])
                marks_lg.append(s.cnt["lg"])
            s.wait_at("g", "lg", s.cnt["lg"])

    return body



def build_mkb(momentum, Tc=8192):
    use_consts = BASE_CONSTS + ["c_hot16"]
    nc = bass.Bass("TRN2", target_bir_lowering=False)
    NT = Tc // TT
    hb = {}
    hb["xc"] = nc.dram_tensor("xc", [D, Tc], BF16, kind="ExternalInput").ap()
    hb["sgz"] = nc.dram_tensor("sgz", [D, Tc], BF16, kind="ExternalInput").ap()
    hb["db0"] = nc.dram_tensor("db0", [80, Tc], F32, kind="ExternalInput").ap()
    hb["db1"] = nc.dram_tensor("db1", [80, Tc], F32, kind="ExternalInput").ap()
    hb["c0"] = nc.dram_tensor("c0", [16, Tc], F32, kind="ExternalInput").ap()
    hb["c1"] = nc.dram_tensor("c1", [16, Tc], F32, kind="ExternalInput").ap()
    hb["dtwT"] = nc.dram_tensor("dtwT", [64, 1024], F32, kind="ExternalInput").ap()
    hb["dtb"] = nc.dram_tensor("dtb", [1, 1024], F32, kind="ExternalInput").ap()
    hb["alog"] = nc.dram_tensor("alog", [128, 128], F32, kind="ExternalInput").ap()
    hb["dcol"] = nc.dram_tensor("dcol", [128, 8], F32, kind="ExternalInput").ap()
    hb["owT"] = nc.dram_tensor("owT", [1024, 1024], F32, kind="ExternalInput").ap()
    for name, shp, dt in CONST_SPECS:
        if name in use_consts:
            hb[name] = nc.dram_tensor(name, shp, dt, kind="ExternalInput").ap()
    hb["op_o"] = nc.dram_tensor("op_o", [D, Tc], F32, kind="ExternalOutput").ap()

    body = setup_mkb(nc, hb, "", momentum, Tc)
    with contextlib.ExitStack() as c2:
        run_sched(nc, c2, body)
    return nc

def mkb_inmaps(consts, dt_proj_w, dt_proj_b, A_log, Dv, out_w, xc_full, sgz_full, db_parts, c_parts):
    consts = consts_subset(consts, BASE_CONSTS + ["c_hot16"])
    dt_proj_w = np.asarray(dt_proj_w); A_log = np.asarray(A_log)
    out_w = np.asarray(out_w)
    maps = []
    for c in range(NCORE):
        b, eh = c // 2, c % 2
        m = dict(consts)
        sl = slice(eh * 1024, (eh + 1) * 1024)
        m["dtwT"] = np.ascontiguousarray(dt_proj_w[sl, :].T)
        m["dtb"] = host_row(np.asarray(dt_proj_b)[sl])
        m["alog"] = np.ascontiguousarray(A_log[sl, :].reshape(8, 128, 16).transpose(1, 0, 2).reshape(128, 128))
        m["dcol"] = host_col(np.asarray(Dv)[sl])
        m["owT"] = np.ascontiguousarray(out_w[:, sl].T)
        m["xc"] = xc_full[c]
        m["sgz"] = sgz_full[c]
        m["db0"], m["db1"] = db_parts[b * 2], db_parts[b * 2 + 1]
        m["c0"], m["c1"] = c_parts[b * 2], c_parts[b * 2 + 1]
        maps.append(m)
    return maps



def setup_mkb2(nc, hb, prefix="", momentum=False, Tc=8192):
    NT = Tc // TT
    ov = hb["op_o"]
    get_out = ov if callable(ov) else (lambda t0, t1, v=ov: fm_ap(v, t0, t1))
    ctx = contextlib.ExitStack()
    with ctx:
        sb = {}

        def S(name, shape, dt=F32):
            sb[name] = ctx.enter_context(nc.sbuf_tensor("sb_" + prefix + name, shape, dt))
            return sb[name]

        for name, shp, dt in CONST_SPECS:
            if name in hb:
                S(name, shp, dt)
        for g in range(10):
            S(f"g{g}", [128, Tc], BF16)
        S("ow", [128, DCH * 1024], BF16)
        S("dtw", [64, 1024], BF16)
        S("dtb_bf", [1, 1024], BF16)
        S("acol", [128, 128])
        S("dcol", [128, 8])
        S("wstg0", [128, 1024], F32)
        S("wstg1", [128, 1024], F32)
        S("db_a", [80, TT])
        S("ct_a", [16, TT])
        S("dtc", [64, TT], BF16)
        S("btbf", [16, TT], BF16); S("ctbf", [16, TT], BF16)
        S("spa", [128, TT]); S("spb", [128, TT])
        S("z0", [128, 1])

        ps = {}
        ps["pq"] = ctx.enter_context(nc.psum_tensor(prefix + "pq", [128, TT], F32))
        ps["pq2"] = ctx.enter_context(nc.psum_tensor(prefix + "pq2", [128, TT], F32))

        def body(s):
            emit_load_consts(s, hb, sb)
            s.op("q", lambda E: E.dma_start(out=sb["dcol"][:], in_=hb["dcol"]))
            s.op("q", lambda E: E.dma_start(out=sb["wstg0"][0:64, :], in_=hb["dtwT"]))
            s.wait("v", ["ld"])
            s.op("v", lambda E: E.tensor_copy(out=sb["dtw"][:], in_=sb["wstg0"][0:64, :]))
            s.op("v", lambda E: E.memset(sb["z0"][:], 0.0))
            s.wait("q", ["sv"])
            s.op("q", lambda E: E.dma_start(out=sb["wstg0"][0:1, :], in_=hb["dtb"]))
            s.wait("v", ["ld"])
            s.op("v", lambda E: E.tensor_copy(out=sb["dtb_bf"][:], in_=sb["wstg0"][0:1, :]))
            s.wait("q", ["sv"])
            s.op("q", lambda E: E.dma_start(out=sb["wstg0"][:, 0:128], in_=hb["alog"]))
            s.wait("s", ["ld"])
            s.op("s", lambda E: E.activation(out=sb["acol"][:], in_=sb["wstg0"][:, 0:128], func=AF.Exp))
            s.wait("v", ["ss"])
            s.op("v", lambda E: E.tensor_scalar(out=sb["acol"][:], in0=sb["acol"][:],
                                                scalar1=-1.0, scalar2=None, op0=ALU.mult))
            chunks = [hb["owT"][ci * 128:(ci + 1) * 128, :] for ci in range(DCH)]
            dsts = [sb["ow"][:, ci * 1024:(ci + 1) * 1024] for ci in range(DCH)]
            emit_cast_weight(s, chunks, [sb["wstg0"], sb["wstg1"]], dsts)
            cv = sb["c_cvec"]

            for i in range(NT):
                t0 = i * TT
                s.wait("q", ["sv", "ss"])
                s.op("q", lambda E, t0=t0: E.dma_start(out=sb["db_a"][:], in_=hb["db0"][:, t0:t0 + TT]))
                s.op("q", lambda E, t0=t0: E.dma_start(out=sb["ct_a"][:], in_=hb["c0"][:, t0:t0 + TT]))
                s.op("q", lambda E, t0=t0: E.dma_start(out=sb3(sb["g0"][:, 0:DCH * TT], TT),
                                                       in_=fm_ap(hb["xc"], t0, t0 + TT)))
                s.wait("v", ["ld", "sm"])
                s.op("v", lambda E: E.tensor_copy(out=sb["dtc"][:], in_=sb["db_a"][0:64, :]))
                s.op("v", lambda E: E.tensor_copy(out=sb["btbf"][:], in_=sb["db_a"][64:80, :]))
                s.op("v", lambda E: E.tensor_copy(out=sb["ctbf"][:], in_=sb["ct_a"][:]))
                s.wait("g", ["sv"])
                s.op("g", lambda E, t0=t0: E.dma_start(out=hb["bts"][:, t0:t0 + TT], in_=sb["btbf"][:]))
                s.op("g", lambda E, t0=t0: E.dma_start(out=hb["cts"][:, t0:t0 + TT], in_=sb["ctbf"][:]))
                for co in range(DCH):
                    s.wait("t", ["sv", "ss"])
                    s.op("t", lambda E, co=co: E.matmul(
                        out=ps["pq"][:], lhsT=sb["dtw"][:, co * 128:(co + 1) * 128],
                        rhs=sb["dtc"][:], start=True, stop=False))
                    s.op("t", lambda E, co=co: E.matmul(
                        out=ps["pq"][:], lhsT=sb["dtb_bf"][0:1, co * 128:(co + 1) * 128],
                        rhs=sb["c_ones_bf"][0:1, :], start=False, stop=True))
                    s.wait("v", ["sm"])
                    s.op("v", lambda E: E.tensor_scalar(out=sb["spa"][:], in0=ps["pq"][:],
                                                        scalar1=0.0, scalar2=None, op0=ALU.min))
                    s.op("v", lambda E: E.tensor_scalar(out=sb["spb"][:], in0=ps["pq"][:],
                                                        scalar1=0.0, scalar2=None, op0=ALU.max))
                    s.op("v", lambda E: E.tensor_tensor(out=sb["spa"][:], in0=sb["spa"][:],
                                                        in1=sb["spb"][:], op=ALU.subtract))
                    s.wait("s", ["sv"])
                    s.op("s", lambda E: E.activation(out=sb["spa"][:], in_=sb["spa"][:], func=AF.Exp))
                    s.op("s", lambda E: E.activation(out=sb["spa"][:], in_=sb["spa"][:],
                                                     func=AF.Ln, bias=cv[0:128, 2:3], scale=cv[0:128, 2:3]))
                    s.wait("v", ["ss"])
                    s.op("v", lambda E: E.tensor_tensor(out=sb["spa"][:], in0=sb["spa"][:],
                                                        in1=sb["spb"][:], op=ALU.add))
                    s.op("v", lambda E, co=co: E.tensor_copy(out=sb["g1"][:, 0:TT], in_=sb["spa"][:]))
                    s.op("v", lambda E, co=co: E.tensor_tensor(
                        out=sb["g2"][:, 0:TT], in0=sb["g1"][:, 0:TT],
                        in1=sb["g0"][:, co * TT:(co + 1) * TT], op=ALU.mult))
                    s.wait("g", ["sv"])
                    s.op("g", lambda E, co=co, t0=t0: E.dma_start(
                        out=hb["dls"][co * 128:(co + 1) * 128, t0:t0 + TT], in_=sb["g1"][:, 0:TT]))
                    s.op("g", lambda E, co=co, t0=t0: E.dma_start(
                        out=hb["dus"][co * 128:(co + 1) * 128, t0:t0 + TT], in_=sb["g2"][:, 0:TT]))

            if momentum:
                s.wait("v", ["lg", "ld"])
                s.op("v", lambda E: E.memset(sb["g9"][:], 0.6))
            for ec in range(DCH):
                s.wait("q", ["sv", "ss", "lg"])
                s.op("q", lambda E, ec=ec: E.dma_start(out=sb["g0"][:], in_=hb["dls"][ec * 128:(ec + 1) * 128, :]))
                s.op("q", lambda E, ec=ec: E.dma_start(out=sb["g1"][:], in_=hb["dus"][ec * 128:(ec + 1) * 128, :]))
                s.wait("v", ["ld"])
                s.op("v", lambda E: E.memset(sb["g8"][:], 0.0))
                for n in range(16):
                    col = ec * 16 + n
                    s.wait("q", ["sv", "ss"])
                    s.op("q", lambda E, n=n: E.dma_start(out=sb["g6"][:], in_=hb["bts"][n:n + 1, :].broadcast_to((128, Tc))))
                    s.op("q", lambda E, n=n: E.dma_start(out=sb["g7"][:], in_=hb["cts"][n:n + 1, :].broadcast_to((128, Tc))))
                    s.wait("s", ["ld", "sv"])
                    s.op("s", lambda E, col=col: E.activation(
                        out=sb["g2"][:], in_=sb["g0"][:], func=AF.Exp,
                        bias=cv[:, 3:4], scale=sb["acol"][:, col:col + 1]))
                    s.wait("v", ["ss", "ld"])
                    s.op("v", lambda E: E.tensor_tensor(out=sb["g3"][:], in0=sb["g1"][:], in1=sb["g6"][:], op=ALU.mult))
                    if momentum:
                        s.op("v", lambda E: E.tensor_tensor_scan(
                            out=sb["g5"][:], data0=sb["g9"][:], data1=sb["g3"][:],
                            initial=sb["z0"][:], op0=ALU.mult, op1=ALU.add))
                        s.op("v", lambda E: E.tensor_tensor_scan(
                            out=sb["g4"][:], data0=sb["g2"][:], data1=sb["g5"][:],
                            initial=sb["z0"][:], op0=ALU.mult, op1=ALU.add))
                    else:
                        s.op("v", lambda E: E.tensor_tensor_scan(
                            out=sb["g4"][:], data0=sb["g2"][:], data1=sb["g3"][:],
                            initial=sb["z0"][:], op0=ALU.mult, op1=ALU.add))
                    s.op("v", lambda E: E.tensor_tensor(out=sb["g4"][:], in0=sb["g4"][:], in1=sb["g7"][:], op=ALU.mult))
                    s.op("v", lambda E: E.tensor_tensor(out=sb["g8"][:], in0=sb["g8"][:], in1=sb["g4"][:], op=ALU.add))
                s.wait("q", ["sv"])
                s.op("q", lambda E, ec=ec: E.dma_start(out=sb["g3"][:], in_=fm_ec(hb["xc"], ec, Tc)))
                s.op("q", lambda E, ec=ec: E.dma_start(out=sb["g4"][:], in_=fm_ec(hb["sgz"], ec, Tc)))
                s.wait("v", ["ld"])
                s.op("v", lambda E, ec=ec: E.scalar_tensor_tensor(
                    out=sb["g5"][:], in0=sb["g3"][:], scalar=sb["dcol"][:, ec:ec + 1],
                    in1=sb["g8"][:], op0=ALU.mult, op1=ALU.add))
                s.op("v", lambda E: E.tensor_tensor(out=sb["g5"][:], in0=sb["g5"][:], in1=sb["g4"][:], op=ALU.mult))
                s.wait("g", ["sv"])
                s.op("g", lambda E, ec=ec: E.dma_start(out=hb["ybs"][ec * 128:(ec + 1) * 128, :], in_=sb["g5"][:]))

            for i in range(NT):
                t0 = i * TT
                s.wait("q", ["sm", "lg", "sv", "ss"])
                s.op("q", lambda E, t0=t0: E.dma_start(out=sb3(sb["g0"][:, 0:DCH * TT], TT),
                                                       in_=fm_ap(hb["ybs"], t0, t0 + TT)))
                otv = sb["g1"][:].bitcast(F32)
                s.wait("t", ["sv", "ss", "ld"])
                groups = []
                for co in range(DCH):
                    def mko(co):
                        def mm(E, pb, ci):
                            return E.matmul(
                                out=pb[:],
                                lhsT=sb["ow"][:, ci * 1024 + co * 128: ci * 1024 + (co + 1) * 128],
                                rhs=sb["g0"][:, ci * TT:(ci + 1) * TT],
                                start=(ci == 0), stop=(ci == DCH - 1))
                        mms = [lambda E, pb, ci=ci: mm(E, pb, ci) for ci in range(DCH)]
                        ev = lambda E, pb: E.tensor_copy(out=otv[:, co * TT:(co + 1) * TT], in_=pb[:])
                        return (mms, ev, "v")
                    groups.append(mko(co))
                emit_mm_pipe2(s, groups, [ps["pq"], ps["pq2"]])
                s.wait("g", ["sv"])
                s.op("g", lambda E, t0=t0, otv=otv: E.dma_start(out=get_out(t0, t0 + TT), in_=sb3(otv, TT)))
            s.wait_at("g", "lg", s.cnt["lg"])
    return body


def fm_ec(hbm_ap, ec, Tc):
    return hbm_ap[ec * 128:(ec + 1) * 128, :]


def setup_mkb3(nc, hb, prefix="", momentum=False, Tc=8192):
    NT = Tc // TT
    H = Tc // 2
    ov = hb["op_o"]
    get_out = ov if callable(ov) else (lambda t0, t1, v=ov: fm_ap(v, t0, t1))
    ctx = contextlib.ExitStack()
    with ctx:
        sb = {}

        def S(name, shape, dt=F32):
            sb[name] = ctx.enter_context(nc.sbuf_tensor("sb_" + prefix + name, shape, dt))
            return sb[name]

        for name, shp, dt in CONST_SPECS:
            if name in hb:
                S(name, shp, dt)
        S("g0", [128, Tc], BF16)
        S("g1", [128, Tc], BF16)
        S("g2", [128, Tc], BF16)
        S("g3", [128, Tc], BF16)
        S("g4", [128, Tc], BF16)
        S("g8", [128, Tc], BF16)
        if momentum:
            S("g5", [128, H], BF16)
            S("g9", [128, H], BF16)
        S("btall", [16, Tc], BF16)
        S("ctall", [16, Tc], BF16)
        S("ow", [128, DCH * 1024], BF16)
        S("dtw", [64, 1024], BF16)
        S("dtb_bf", [1, 1024], BF16)
        S("acol", [128, 128])
        S("dcol", [128, 8])
        S("wstg0", [128, 1024], F32)
        S("wstg1", [128, 1024], F32)
        S("da", [64, 2 * TT])
        S("bt_a", [16, 2 * TT]); S("ct_a", [16, 2 * TT])
        S("dtc", [64, 2 * TT], BF16)
        S("xct", [128, 2 * TT], BF16)
        S("spa", [128, TT]); S("spb", [128, TT])
        S("z0", [128, 1])
        if momentum:
            S("zc", [128, 2])

        ps = {}
        ps["pq"] = ctx.enter_context(nc.psum_tensor(prefix + "pq", [128, TT], F32))
        ps["pq2"] = ctx.enter_context(nc.psum_tensor(prefix + "pq2", [128, TT], F32))
        pb = [ctx.enter_context(nc.psum_tensor(prefix + f"pb{k}", [128, 512], F32))
              for k in range(4)]

        def body(s):
            emit_load_consts(s, hb, sb)
            s.op("q", lambda E: E.dma_start(out=sb["dcol"][:], in_=hb["dcol"]))
            s.op("q", lambda E: E.dma_start(out=sb["wstg0"][0:64, :], in_=hb["dtwT"]))
            s.wait("v", ["ld"])
            s.op("v", lambda E: E.tensor_copy(out=sb["dtw"][:], in_=sb["wstg0"][0:64, :]))
            s.op("v", lambda E: E.memset(sb["z0"][:], 0.0))
            if momentum:
                s.op("v", lambda E: E.memset(sb["g9"][:], 0.6))
            s.wait("q", ["sv"])
            s.op("q", lambda E: E.dma_start(out=sb["wstg0"][0:1, :], in_=hb["dtb"]))
            s.wait("v", ["ld"])
            s.op("v", lambda E: E.tensor_copy(out=sb["dtb_bf"][:], in_=sb["wstg0"][0:1, :]))
            s.wait("q", ["sv"])
            s.op("q", lambda E: E.dma_start(out=sb["wstg0"][:, 0:128], in_=hb["alog"]))
            s.wait("s", ["ld"])
            s.op("s", lambda E: E.activation(out=sb["acol"][:], in_=sb["wstg0"][:, 0:128], func=AF.Exp))
            s.wait("v", ["ss"])
            s.op("v", lambda E: E.tensor_scalar(out=sb["acol"][:], in0=sb["acol"][:],
                                                scalar1=-1.0, scalar2=None, op0=ALU.mult))
            chunks = [hb["owT"][ci * 128:(ci + 1) * 128, :] for ci in range(DCH)]
            dsts = [sb["ow"][:, ci * 1024:(ci + 1) * 1024] for ci in range(DCH)]
            emit_cast_weight(s, chunks, [sb["wstg0"], sb["wstg1"]], dsts)
            cv = sb["c_cvec"]

            btct_marks = {}
            for i in range(NT):
                t0 = i * TT
                pp = i % 2
                sl = slice(pp * TT, (pp + 1) * TT)
                if i >= 2:
                    s.wait_at("q", "sv", btct_marks[i - 2])
                s.op("q", lambda E, t0=t0, sl=sl: E.dma_start(out=sb["bt_a"][:, sl], in_=hb["db0"][64:80, t0:t0 + TT]))
                s.op("q", lambda E, t0=t0, sl=sl: E.dma_start(out=sb["ct_a"][:, sl], in_=hb["c0"][:, t0:t0 + TT]))
                ldm = s.cnt["ld"]
                s.wait_at("v", "ld", ldm)
                s.op("v", lambda E, t0=t0, sl=sl: E.tensor_copy(out=sb["btall"][:, t0:t0 + TT], in_=sb["bt_a"][:, sl]))
                s.op("v", lambda E, t0=t0, sl=sl: E.tensor_copy(out=sb["ctall"][:, t0:t0 + TT], in_=sb["ct_a"][:, sl]))
                btct_marks[i] = s.cnt["sv"]

            bank_mark = [0, 0, 0, 0]
            scan_g2_mark = 0
            last_exp_ss = 0
            lg_fin = 0
            xslot_mark = {}
            dtmm_mark = {}
            for ec in range(8):
                for i in range(NT):
                    t0 = i * TT
                    pp = i % 2
                    sl = slice(pp * TT, (pp + 1) * TT)
                    bank = ps["pq"] if pp == 0 else ps["pq2"]
                    if xslot_mark.get(pp):
                        s.wait_at("q", "sv", xslot_mark[pp])
                    s.op("q", lambda E, t0=t0, sl=sl: E.dma_start(out=sb["da"][:, sl], in_=hb["db0"][0:64, t0:t0 + TT]))
                    s.op("q", lambda E, t0=t0, sl=sl, ec=ec: E.dma_start(
                        out=sb["xct"][:, sl], in_=hb["xc"][ec * 128:(ec + 1) * 128, t0:t0 + TT]))
                    ldm = s.cnt["ld"]
                    s.wait_at("v", "ld", ldm)
                    if dtmm_mark.get(pp):
                        s.wait_at("v", "sm", dtmm_mark[pp])
                    s.op("v", lambda E, sl=sl: E.tensor_copy(out=sb["dtc"][:, sl], in_=sb["da"][:, sl]))
                    cast_m = s.cnt["sv"]
                    if ec == 0 and i == 0:
                        s.wait("t", ["sv", "ld", "ss"])
                    s.wait_at("t", "sv", cast_m)
                    s.op("t", lambda E, sl=sl, ec=ec, bank=bank: E.matmul(
                        out=bank[:], lhsT=sb["dtw"][:, ec * 128:(ec + 1) * 128],
                        rhs=sb["dtc"][:, sl], start=True, stop=False))
                    s.op("t", lambda E, ec=ec, bank=bank: E.matmul(
                        out=bank[:], lhsT=sb["dtb_bf"][0:1, ec * 128:(ec + 1) * 128],
                        rhs=sb["c_ones_bf"][0:1, :], start=False, stop=True))
                    dtmm_mark[pp] = s.cnt["sm"]
                    s.wait_at("v", "sm", dtmm_mark[pp])
                    s.op("v", lambda E, bank=bank: E.tensor_scalar(out=sb["spa"][:], in0=bank[:],
                                                                   scalar1=0.0, scalar2=None, op0=ALU.min))
                    s.op("v", lambda E, bank=bank: E.tensor_scalar(out=sb["spb"][:], in0=bank[:],
                                                                   scalar1=0.0, scalar2=None, op0=ALU.max))
                    s.op("v", lambda E: E.tensor_tensor(out=sb["spa"][:], in0=sb["spa"][:],
                                                        in1=sb["spb"][:], op=ALU.subtract))
                    spsv = s.cnt["sv"]
                    s.wait_at("s", "sv", spsv)
                    s.op("s", lambda E: E.activation(out=sb["spa"][:], in_=sb["spa"][:], func=AF.Exp))
                    s.op("s", lambda E: E.activation(out=sb["spa"][:], in_=sb["spa"][:],
                                                     func=AF.Ln, bias=cv[0:128, 2:3], scale=cv[0:128, 2:3]))
                    sps = s.cnt["ss"]
                    s.wait_at("v", "ss", sps)
                    s.op("v", lambda E: E.tensor_tensor(out=sb["spa"][:], in0=sb["spa"][:],
                                                        in1=sb["spb"][:], op=ALU.add))
                    if i == 0 and last_exp_ss:
                        s.wait_at("v", "ss", last_exp_ss)
                    s.op("v", lambda E, t0=t0: E.tensor_copy(out=sb["g0"][:, t0:t0 + TT], in_=sb["spa"][:]))
                    s.op("v", lambda E, t0=t0, sl=sl: E.tensor_tensor(
                        out=sb["g1"][:, t0:t0 + TT], in0=sb["spa"][:], in1=sb["xct"][:, sl], op=ALU.mult))
                    xslot_mark[pp] = s.cnt["sv"]
                delta_done_sv = s.cnt["sv"]

                for n in range(16):
                    col = ec * 16 + n
                    s.wait_at("s", "sv", delta_done_sv if n == 0 else scan_g2_mark)
                    s.op("s", lambda E, col=col: E.activation(
                        out=sb["g2"][:], in_=sb["g0"][:], func=AF.Exp,
                        bias=cv[:, 3:4], scale=sb["acol"][:, col:col + 1]))
                    exp_ss = s.cnt["ss"]
                    last_exp_ss = exp_ss
                    for jp in range(8):
                        bm = max(bank_mark[(jp * 2) % 4], bank_mark[(jp * 2 + 1) % 4])
                        if bm:
                            s.wait_at("t", "sv", bm)
                        for h2 in range(2):
                            j = jp * 2 + h2
                            bk = j % 4
                            s.op("t", lambda E, n=n, j=j, bk=bk: E.matmul(
                                out=pb[bk][:], lhsT=sb["c_hot16"][:, n * 128:(n + 1) * 128],
                                rhs=sb["btall"][0:16, j * 512:(j + 1) * 512], start=True, stop=True))
                        smm = s.cnt["sm"]
                        s.wait_at("v", "sm", smm)
                        if n == 0 and jp == 0 and lg_fin:
                            s.wait_at("v", "lg", lg_fin)
                        for h2 in range(2):
                            j = jp * 2 + h2
                            bk = j % 4
                            s.op("v", lambda E, j=j, bk=bk: E.tensor_tensor(
                                out=sb["g3"][:, j * 512:(j + 1) * 512], in0=pb[bk][:],
                                in1=sb["g1"][:, j * 512:(j + 1) * 512], op=ALU.mult))
                        for h2 in range(2):
                            bank_mark[(jp * 2 + h2) % 4] = s.cnt["sv"]
                    s.wait_at("v", "ss", exp_ss)
                    if momentum:
                        s.op("v", lambda E: E.tensor_tensor_scan(
                            out=sb["g5"][:], data0=sb["g9"][:], data1=sb["g3"][:, 0:H],
                            initial=sb["z0"][:], op0=ALU.mult, op1=ALU.add))
                        s.op("v", lambda E: E.tensor_copy(out=sb["zc"][:, 0:1], in_=sb["g5"][:, H - 1:H]))
                        s.op("v", lambda E: E.tensor_tensor_scan(
                            out=sb["g4"][:, 0:H], data0=sb["g2"][:, 0:H], data1=sb["g5"][:],
                            initial=sb["z0"][:], op0=ALU.mult, op1=ALU.add))
                        s.op("v", lambda E: E.tensor_copy(out=sb["zc"][:, 1:2], in_=sb["g4"][:, H - 1:H]))
                        s.op("v", lambda E: E.tensor_tensor_scan(
                            out=sb["g5"][:], data0=sb["g9"][:], data1=sb["g3"][:, H:Tc],
                            initial=sb["zc"][:, 0:1], op0=ALU.mult, op1=ALU.add))
                        s.op("v", lambda E: E.tensor_tensor_scan(
                            out=sb["g4"][:, H:Tc], data0=sb["g2"][:, H:Tc], data1=sb["g5"][:],
                            initial=sb["zc"][:, 1:2], op0=ALU.mult, op1=ALU.add))
                    else:
                        s.op("v", lambda E: E.tensor_tensor_scan(
                            out=sb["g4"][:], data0=sb["g2"][:], data1=sb["g3"][:],
                            initial=sb["z0"][:], op0=ALU.mult, op1=ALU.add))
                    scan_g2_mark = s.cnt["sv"]
                    for jp in range(8):
                        bm = max(bank_mark[(jp * 2) % 4], bank_mark[(jp * 2 + 1) % 4])
                        if bm:
                            s.wait_at("t", "sv", bm)
                        for h2 in range(2):
                            j = jp * 2 + h2
                            bk = j % 4
                            s.op("t", lambda E, n=n, j=j, bk=bk: E.matmul(
                                out=pb[bk][:], lhsT=sb["c_hot16"][:, n * 128:(n + 1) * 128],
                                rhs=sb["ctall"][0:16, j * 512:(j + 1) * 512], start=True, stop=True))
                        smm = s.cnt["sm"]
                        s.wait_at("v", "sm", smm)
                        for h2 in range(2):
                            j = jp * 2 + h2
                            bk = j % 4
                            if n == 0:
                                s.op("v", lambda E, j=j, bk=bk: E.tensor_tensor(
                                    out=sb["g8"][:, j * 512:(j + 1) * 512], in0=pb[bk][:],
                                    in1=sb["g4"][:, j * 512:(j + 1) * 512], op=ALU.mult))
                            else:
                                s.op("v", lambda E, j=j, bk=bk: E.tensor_tensor(
                                    out=sb["spa"][:], in0=pb[bk][:],
                                    in1=sb["g4"][:, j * 512:(j + 1) * 512], op=ALU.mult))
                                s.op("v", lambda E, j=j: E.tensor_tensor(
                                    out=sb["g8"][:, j * 512:(j + 1) * 512], in0=sb["g8"][:, j * 512:(j + 1) * 512],
                                    in1=sb["spa"][:], op=ALU.add))
                        for h2 in range(2):
                            bank_mark[(jp * 2 + h2) % 4] = s.cnt["sv"]

                s.wait("q", ["sv", "ss"])
                s.op("q", lambda E, ec=ec: E.dma_start(out=sb["g3"][:], in_=fm_ec(hb["xc"], ec, Tc)))
                s.op("q", lambda E, ec=ec: E.dma_start(out=sb["g2"][:], in_=fm_ec(hb["sgz"], ec, Tc)))
                ldm = s.cnt["ld"]
                s.wait_at("v", "ld", ldm)
                s.op("v", lambda E, ec=ec: E.scalar_tensor_tensor(
                    out=sb["g4"][:], in0=sb["g3"][:], scalar=sb["dcol"][:, ec:ec + 1],
                    in1=sb["g8"][:], op0=ALU.mult, op1=ALU.add))
                s.op("v", lambda E: E.tensor_tensor(out=sb["g3"][:], in0=sb["g4"][:], in1=sb["g2"][:], op=ALU.mult))
                fin_sv = s.cnt["sv"]
                s.wait_at("g", "sv", fin_sv)
                s.op("g", lambda E, ec=ec: E.dma_start(out=hb["ybs"][ec * 128:(ec + 1) * 128, :], in_=sb["g3"][:]))
                lg_fin = s.cnt["lg"]

            for i in range(NT):
                t0 = i * TT
                s.wait("q", ["sm", "lg", "sv", "ss"])
                s.op("q", lambda E, t0=t0: E.dma_start(out=sb3(sb["g0"][:, 0:DCH * TT], TT),
                                                       in_=fm_ap(hb["ybs"], t0, t0 + TT)))
                otv = sb["g1"][:].bitcast(F32)
                s.wait("t", ["sv", "ss", "ld"])
                groups = []
                for co in range(DCH):
                    def mko(co):
                        def mm(E, pbk, ci):
                            return E.matmul(
                                out=pbk[:],
                                lhsT=sb["ow"][:, ci * 1024 + co * 128: ci * 1024 + (co + 1) * 128],
                                rhs=sb["g0"][:, ci * TT:(ci + 1) * TT],
                                start=(ci == 0), stop=(ci == DCH - 1))
                        mms = [lambda E, pbk, ci=ci: mm(E, pbk, ci) for ci in range(DCH)]
                        ev = lambda E, pbk: E.tensor_copy(out=otv[:, co * TT:(co + 1) * TT], in_=pbk[:])
                        return (mms, ev, "v")
                    groups.append(mko(co))
                emit_mm_pipe2(s, groups, [ps["pq"], ps["pq2"]])
                s.wait("g", ["sv"])
                s.op("g", lambda E, t0=t0, otv=otv: E.dma_start(out=get_out(t0, t0 + TT), in_=sb3(otv, TT)))
            s.wait_at("g", "lg", s.cnt["lg"])
    return body




RG_PAIRS = [[0, 1], [2, 3], [4, 5], [6, 7]]


def fused_build():
    nc = bass.Bass("TRN2", target_bir_lowering=False)
    Tc = 4096
    Tf = 8192
    hb = {}

    def inp(name, shape, dt=F32):
        hb[name] = nc.dram_tensor(name, shape, dt, kind="ExternalInput").ap()
        return hb[name]

    def scratch(name, shape, dt=F32):
        hb[name] = nc.dram_tensor(name, shape, dt).ap()
        return hb[name]

    for name, shp, dt in CONST_SPECS:
        inp(name, shp, dt)
    inp("x_in", [D, Tc], F32)
    inp("summ_in", [D, 32], F32)
    for w in ("l0_wq", "l0_wk", "l0_wv", "l0_wc"):
        inp(w, [D, D], F32)
    for b in ("l0_bq", "l0_bk", "l0_bv", "l0_bc"):
        inp(b, [1, D], F32)
    inp("f0_w1i", [4096, 1025]); inp("f0_w2i", [4096, 1024])
    inp("f0_stmU", [12 * 32 * 128, 128]); inp("f0_stmD", [12 * 32 * 128, 128])
    inp("f0_ubias", [4, 1024]); inp("f0_bcon", [1, D]); inp("f0_dbias", [1, D])
    inp("l1_wqkv", [D, 3072]); inp("l1_bqkv", [1, 3072])
    inp("l1_wg", [D, 2816]); inp("l1_wu", [D, 2816]); inp("l1_wd", [2816, D])
    inp("l1_bg", [1, 2816]); inp("l1_bu", [1, 2816]); inp("l1_bd", [1, 1024])
    inp("l1_g1", [128, 8]); inp("l1_b1", [128, 8]); inp("l1_g2", [128, 8]); inp("l1_b2", [128, 8])
    inp("f1_wgT", [D, 3072], I32); inp("f1_wuT", [D, 3072], I32); inp("f1_wdT", [3072, D], I32)
    for m in range(5):
        inp(f"m{m}_wa", [D, 2048]); inp(f"m{m}_xpT", [D, 80]); inp(f"m{m}_xpTc", [D, 16])
        inp(f"m{m}_cw", [128, 32]); inp(f"m{m}_cb", [128, 8])
        inp(f"m{m}_dtwT", [64, 1024]); inp(f"m{m}_dtb", [1, 1024])
        inp(f"m{m}_alog", [128, 128]); inp(f"m{m}_dcol", [128, 8]); inp(f"m{m}_owT", [1024, 1024])
    inp("f2_wrT", [D, D]); inp("f2_wkT", [D, D]); inp("f2_wvT", [D, D])
    inp("f2_mur", [128, 8]); inp("f2_muk", [128, 8]); inp("f2_hflag", [128, 8])
    inp("f3_gwT", [D, 8]); inp("f3_ewT", [8 * D, D])
    out_ap = nc.dram_tensor("out", [D, Tc], F32, kind="ExternalOutput").ap()

    scratch("x_a", [D, Tc]); scratch("x_b", [D, Tc])
    scratch("qp_s", [D, Tc], BF16)
    scratch("kv_loc", [D, 1024]); scratch("kv_sum", [D, 1024])
    scratch("ks_loc", [128, 8]); scratch("ks_sum", [128, 8])
    scratch("xg_a", [2 * D, Tc]); scratch("xg_b", [2 * D, Tc])
    scratch("xc_s", [D, Tf], BF16); scratch("sgz_s", [D, Tf], BF16)
    scratch("dbc_loc", [96, Tf]); scratch("dbc_sum", [96, Tf])
    scratch("op_loc", [2 * D, Tc]); scratch("op_sum", [2 * D, Tc])
    scratch("dls", [D, Tf], BF16); scratch("dus", [D, Tf], BF16)
    scratch("bts", [16, Tf], BF16); scratch("cts", [16, Tf], BF16)
    scratch("ybs", [D, Tf], BF16)
    scratch("op_rs", [D, Tc]); scratch("op_rs3", [D, Tc])
    scratch("halo_loc", [128, 8]); scratch("halo_ag", [256, 8])
    scratch("f0scr", [4096, 1025], BF16)

    def split_view(ap):
        def f(t0, t1):
            if t1 <= Tc:
                return fm_ap(ap[0:D, :], t0, t1)
            return fm_ap(ap[D:2 * D, :], t0 - Tc, t1 - Tc)
        return f

    emitters = []

    def add(emit, tag):
        emitters.append((emit, tag))

    base = {k: hb[k] for k in BASE_CONSTS}
    base_h16 = dict(base, c_hot16=hb["c_hot16"])
    base_l3f = dict(base, c_eyef=hb["c_eyef"], c_hot32=hb["c_hot32"])

    h = dict(base); h.update(x=hb["x_in"], summ=hb["summ_in"], wq=hb["l0_wq"], wk=hb["l0_wk"],
                             wv=hb["l0_wv"], wc=hb["l0_wc"], bq_r=hb["l0_bq"], bk_r=hb["l0_bk"],
                             bv_r=hb["l0_bv"], bc_r=hb["l0_bc"], xo=hb["x_a"])
    add(setup_l0a(nc, h, "A_"), "l0a")
    h = dict(base); h.update(x=hb["x_a"], w1i=hb["f0_w1i"], w2i=hb["f0_w2i"], stmU=hb["f0_stmU"],
                             stmD=hb["f0_stmD"], ubias_p=hb["f0_ubias"], bcon_r=hb["f0_bcon"],
                             dbias_r=hb["f0_dbias"], xo=hb["x_b"], scr=hb["f0scr"])
    add(setup_l0f(nc, h, "B_"), "l0f")
    h = dict(base); h.update(x=hb["x_b"], wqkv=hb["l1_wqkv"], bqkv=hb["l1_bqkv"],
                             qp_o=hb["qp_s"], kv_o=hb["kv_loc"], ks_o=hb["ks_loc"])
    add(setup_l1a(nc, h, "C_"), "l1a")
    add(("CC", "AllReduce", [hb["kv_loc"], hb["kv_sum"]]), "cc_kv")
    add(("CC", "AllReduce", [hb["ks_loc"], hb["ks_sum"]]), "cc_ks")
    h = dict(base); h.update(x=hb["x_b"], qp=hb["qp_s"], kv0=hb["kv_sum"], ks0=hb["ks_sum"],
                             wg=hb["l1_wg"], wu=hb["l1_wu"], wd=hb["l1_wd"], bg=hb["l1_bg"],
                             bu=hb["l1_bu"], bd=hb["l1_bd"], g1=hb["l1_g1"], b1=hb["l1_b1"],
                             g2=hb["l1_g2"], b2=hb["l1_b2"], xo=hb["x_a"])
    add(setup_l1b(nc, h, "D_", single=True), "l1b")
    h = dict(base); h.update(x=hb["x_a"], wgT=hb["f1_wgT"], wuT=hb["f1_wuT"], wdT=hb["f1_wdT"],
                             xo=hb["x_b"])
    add(setup_l1f(nc, h, "E_"), "l1f")
    add(("CC", "AllGather", [hb["x_b"], hb["xg_a"]]), "cc_agx4")

    for ell in range(4):
        h = dict(base)
        h.update(wa=hb[f"m{ell}_wa"], xpT=hb[f"m{ell}_xpT"], xpTc=hb[f"m{ell}_xpTc"],
                 cw=hb[f"m{ell}_cw"], cb=hb[f"m{ell}_cb"],
                 xc_o=hb["xc_s"], sz_o=hb["sgz_s"],
                 db_o=hb["dbc_loc"][0:80, :], c_o=hb["dbc_loc"][80:96, :])
        if ell == 0:
            h["x"] = split_view(hb["xg_a"])
            add(setup_mka(nc, h, f"M{ell}a_", mode="ln"), f"mka{ell}")
        else:
            h["p0"] = split_view(hb["op_sum"])
            add(setup_mka(nc, h, f"M{ell}a_", mode="plain"), f"mka{ell}")
        add(("CC", "AllReduce", [hb["dbc_loc"], hb["dbc_sum"]]), f"cc_dbc{ell}")
        h = dict(base_h16)
        outv = hb["op_loc"]
        def out_fn(t0, t1, v=outv):
            if t1 <= Tc:
                return fm_ap(v[0:D, :], t0, t1)
            return fm_ap(v[D:2 * D, :], t0 - Tc, t1 - Tc)
        h.update(xc=hb["xc_s"], sgz=hb["sgz_s"], db0=hb["dbc_sum"][0:80, :],
                 c0=hb["dbc_sum"][80:96, :],
                 dtwT=hb[f"m{ell}_dtwT"], dtb=hb[f"m{ell}_dtb"], alog=hb[f"m{ell}_alog"],
                 dcol=hb[f"m{ell}_dcol"], owT=hb[f"m{ell}_owT"], op_o=out_fn,
                 ybs=hb["ybs"])
        add(setup_mkb3(nc, h, f"M{ell}b_", momentum=False), f"mkb{ell}")
        if ell < 3:
            add(("CC", "AllReduce", [hb["op_loc"], hb["op_sum"]]), f"cc_op{ell}")
    add(("CC", "ReduceScatter", [hb["op_loc"], hb["op_rs"]]), "cc_rs2")
    add(("HALO",), "halo")
    add(("CC", "AllGather", [hb["halo_loc"], hb["halo_ag"]]), "cc_halo")

    h = dict(base); h.update(x=hb["x_b"], xhalo=hb["f2_hflag"], wrT=hb["f2_wrT"], wkT=hb["f2_wkT"],
                             wvT=hb["f2_wvT"], mur=hb["f2_mur"], muk=hb["f2_muk"],
                             pp0=hb["op_rs"], hh0=hb["halo_ag"][0:128, :], hflag=hb["f2_hflag"],
                             xo=hb["x_a"])
    add(setup_l2f(nc, h, "F_", add_parts=1), "l2f")
    add(("CC", "AllGather", [hb["x_a"], hb["xg_b"]]), "cc_agx6")

    h = dict(base)
    h.update(wa=hb["m4_wa"], xpT=hb["m4_xpT"], xpTc=hb["m4_xpTc"], cw=hb["m4_cw"], cb=hb["m4_cb"],
             xc_o=hb["xc_s"], sz_o=hb["sgz_s"], db_o=hb["dbc_loc"][0:80, :], c_o=hb["dbc_loc"][80:96, :])
    h["x"] = split_view(hb["xg_b"])
    add(setup_mka(nc, h, "Ga_", mode="ln"), "mka3")
    add(("CC", "AllReduce", [hb["dbc_loc"], hb["dbc_sum"]]), "cc_dbc4")
    h = dict(base_h16)
    outv = hb["op_loc"]
    def out_fn3(t0, t1, v=outv):
        if t1 <= Tc:
            return fm_ap(v[0:D, :], t0, t1)
        return fm_ap(v[D:2 * D, :], t0 - Tc, t1 - Tc)
    h.update(xc=hb["xc_s"], sgz=hb["sgz_s"], db0=hb["dbc_sum"][0:80, :], c0=hb["dbc_sum"][80:96, :],
             dtwT=hb["m4_dtwT"], dtb=hb["m4_dtb"], alog=hb["m4_alog"],
             dcol=hb["m4_dcol"], owT=hb["m4_owT"], op_o=out_fn3,
             ybs=hb["ybs"])
    add(setup_mkb3(nc, h, "Gb_", momentum=True), "mkb3")
    add(("CC", "ReduceScatter", [hb["op_loc"], hb["op_rs3"]]), "cc_rs3")

    h = dict(base_l3f); h.update(x=hb["x_a"], gwT=hb["f3_gwT"], ewT=hb["f3_ewT"],
                             pp0=hb["op_rs3"], xo=out_ap)
    add(setup_l3f(nc, h, "H_", add_parts=1), "l3f")

    with contextlib.ExitStack() as ctx:
        halo_sb = ctx.enter_context(nc.sbuf_tensor("sb_halo", [128, 24], F32))

        def body(s):
            for em, tag in emitters:
                if isinstance(em, tuple) and em[0] == "CC":
                    _, kind, (cin, cout) = em
                    for e in ("q", "g", "v", "s", "t"):
                        s.wait(e)
                    s.op("g", lambda E, kind=kind, cin=cin, cout=cout: E.collective_compute(
                        kind, ALU.add if kind != "AllGather" else ALU.bypass,
                        replica_groups=RG_PAIRS, ins=[cin.opt()], outs=[cout.opt()]), k=1)
                    for e in ("q", "g", "v", "s", "t"):
                        s.wait(e)
                elif isinstance(em, tuple) and em[0] == "HALO":
                    s.op("q", lambda E: E.dma_start(
                        out=halo_sb[:, 0:8],
                        in_=hb["x_b"][:, Tc - 1:Tc].rearrange("(c p) o -> p c o", p=128),
                        allow_slow_non_contiguous=True))
                    s.op("q", lambda E: E.dma_start(
                        out=halo_sb[:, 8:16],
                        in_=hb["op_rs"][:, Tc - 1:Tc].rearrange("(c p) o -> p c o", p=128),
                        allow_slow_non_contiguous=True))
                    s.wait("v", ["ld"])
                    s.op("v", lambda E: E.tensor_tensor(out=halo_sb[:, 16:24], in0=halo_sb[:, 0:8],
                                                        in1=halo_sb[:, 8:16], op=ALU.add))
                    s.wait("q", ["sv"])
                    s.op("q", lambda E: E.dma_start(out=hb["halo_loc"], in_=halo_sb[:, 16:24]))
                else:
                    em(s)
                    for e in ("q", "g", "v", "s", "t"):
                        s.wait(e)

        run_sched(nc, ctx, body)
    return nc


def fused_inmaps(ins):
    consts = make_consts()
    stmU, stmD = l0f_stage_tiles(ins["L0_ffn_up_weights"], ins["L0_ffn_down_weights"])
    perm = _rmaj_perm()
    w1i = np.ascontiguousarray(np.concatenate(
        [np.asarray(ins["L0_ffn_W_exp"]).T, np.asarray(ins["L0_ffn_b_exp"]).reshape(-1, 1)], axis=1).astype(np.float32))
    w2i = np.ascontiguousarray(np.asarray(ins["L0_ffn_W_con"]).astype(np.float32))
    ub_p = np.ascontiguousarray(np.asarray(ins["L0_ffn_up_bias"])[perm].reshape(4, 1024))
    wg = _pad_cols(np.asarray(ins["L1_attn_Wg"]), 2816)
    wu = _pad_cols(np.asarray(ins["L1_attn_Wu"]), 2816)
    wd = _pad_rows(np.asarray(ins["L1_attn_Wd"]), 2816)
    bg = _pad_cols(host_row(ins["L1_attn_bg"]), 2816)
    bu = _pad_cols(host_row(ins["L1_attn_bu"]), 2816)
    x = np.asarray(ins["L0_attn_x"])
    ew = np.asarray(ins["L3_ffn_expert_w"])
    ewT = np.ascontiguousarray(np.transpose(ew, (0, 2, 1)).reshape(8 * D, D))
    maps = []
    for c in range(NCORE):
        b, eh = c // 2, c % 2
        m = dict(consts)
        m["x_in"] = host_fm(x[b, eh * 4096:(eh + 1) * 4096])
        m["summ_in"] = host_fm(np.roll(x[b, ::256, :], -16 * eh, axis=0))
        m["l0_wq"] = np.ascontiguousarray(np.asarray(ins["L0_attn_wq"]).T)
        m["l0_wk"] = np.ascontiguousarray(np.asarray(ins["L0_attn_wk"]).T)
        m["l0_wv"] = np.ascontiguousarray(np.asarray(ins["L0_attn_wv"]).T)
        m["l0_wc"] = np.ascontiguousarray(np.asarray(ins["L0_attn_wc"]).T)
        m["l0_bq"] = host_row(ins["L0_attn_bq"]); m["l0_bk"] = host_row(ins["L0_attn_bk"])
        m["l0_bv"] = host_row(ins["L0_attn_bv"]); m["l0_bc"] = host_row(ins["L0_attn_bc"])
        m["f0_w1i"] = w1i; m["f0_w2i"] = w2i; m["f0_stmU"] = stmU; m["f0_stmD"] = stmD
        m["f0_ubias"] = ub_p
        m["f0_bcon"] = host_row(ins["L0_ffn_b_con"]); m["f0_dbias"] = host_row(ins["L0_ffn_down_bias"])
        m["l1_wqkv"] = np.ascontiguousarray(np.asarray(ins["L1_attn_Wqkv"]).astype(np.float32))
        m["l1_bqkv"] = host_row(ins["L1_attn_bqkv"])
        m["l1_wg"] = wg; m["l1_wu"] = wu; m["l1_wd"] = wd
        m["l1_bg"] = bg; m["l1_bu"] = bu; m["l1_bd"] = host_row(ins["L1_attn_bd"])
        m["l1_g1"] = host_col(ins["L1_attn_g1"]); m["l1_b1"] = host_col(ins["L1_attn_b1"])
        m["l1_g2"] = host_col(ins["L1_attn_g2"]); m["l1_b2"] = host_col(ins["L1_attn_b2"])
        m["f1_wgT"] = np.ascontiguousarray(np.asarray(ins["L1_ffn_w_gate"]).T)
        m["f1_wuT"] = np.ascontiguousarray(np.asarray(ins["L1_ffn_w_up"]).T)
        m["f1_wdT"] = np.ascontiguousarray(np.asarray(ins["L1_ffn_w_down"]).T)
        rows = np.r_[eh * 1024:(eh + 1) * 1024, 2048 + eh * 1024: 2048 + (eh + 1) * 1024]
        for mm in range(5):
            if mm < 4:
                in_w = np.asarray(ins["L2_attn_in_proj_w"][mm]); xp_w = np.asarray(ins["L2_attn_x_proj_w"][mm])
                cwf = np.asarray(ins["L2_attn_conv_w"][mm]); cbf = np.asarray(ins["L2_attn_conv_b"][mm])
                dtw = np.asarray(ins["L2_attn_dt_proj_w"][mm]); dtb = np.asarray(ins["L2_attn_dt_proj_b"][mm])
                alog = np.asarray(ins["L2_attn_A_log"][mm]); dv = np.asarray(ins["L2_attn_D"][mm])
                oww = np.asarray(ins["L2_attn_out_proj_w"][mm])
            else:
                in_w = np.asarray(ins["L3_attn_W_in"]); xp_w = np.asarray(ins["L3_attn_W_x"])
                cwf = np.asarray(ins["L3_attn_conv_w"]); cbf = np.asarray(ins["L3_attn_conv_b"])
                dtw = np.asarray(ins["L3_attn_W_dt"]); dtb = np.asarray(ins["L3_attn_b_dt"])
                alog = np.asarray(ins["L3_attn_A_log"]); dv = np.asarray(ins["L3_attn_D"])
                oww = np.asarray(ins["L3_attn_W_out"])
            sl = slice(eh * 1024, (eh + 1) * 1024)
            m[f"m{mm}_wa"] = np.ascontiguousarray(in_w[rows, :].T)
            xs = xp_w[:, sl]
            m[f"m{mm}_xpT"] = np.ascontiguousarray(xs[0:80, :].T)
            m[f"m{mm}_xpTc"] = np.ascontiguousarray(xs[80:96, :].T)
            cw2 = cwf[sl, :]
            m[f"m{mm}_cw"] = np.ascontiguousarray(cw2.reshape(8, 128, 4).transpose(1, 0, 2).reshape(128, 32))
            m[f"m{mm}_cb"] = host_col(cbf[sl])
            m[f"m{mm}_dtwT"] = np.ascontiguousarray(dtw[sl, :].T)
            m[f"m{mm}_dtb"] = host_row(dtb[sl])
            m[f"m{mm}_alog"] = np.ascontiguousarray(alog[sl, :].reshape(8, 128, 16).transpose(1, 0, 2).reshape(128, 128))
            m[f"m{mm}_dcol"] = host_col(dv[sl])
            m[f"m{mm}_owT"] = np.ascontiguousarray(oww[:, sl].T)
        m["f2_wrT"] = np.ascontiguousarray(np.asarray(ins["L2_ffn_W_r"]).T)
        m["f2_wkT"] = np.ascontiguousarray(np.asarray(ins["L2_ffn_W_k"]).T)
        m["f2_wvT"] = np.ascontiguousarray(np.asarray(ins["L2_ffn_W_v"]).T)
        m["f2_mur"] = host_col(ins["L2_ffn_mu_r"]); m["f2_muk"] = host_col(ins["L2_ffn_mu_k"])
        m["f2_hflag"] = np.full((128, 8), float(eh), dtype=np.float32)
        m["f3_gwT"] = np.ascontiguousarray(np.asarray(ins["L3_ffn_gate_w"]).T)
        m["f3_ewT"] = ewT
        maps.append(m)
    return maps


def kernel_fused(**inputs):
    ins = {k: np.asarray(v) for k, v in inputs.items()}
    nc = fused_build()
    maps = fused_inmaps(ins)
    res = run_bass_kernel_spmd(nc, maps, core_ids=list(range(NCORE)))
    return gather_fm(res.results, key="out")


kernel = kernel_fused
